# version 05
# baseline (speedup 1.0000x reference)
import numpy as np
import ml_dtypes
import concourse.bass as bass
import concourse.mybir as mybir
from concourse.bass_utils import run_bass_kernel_spmd

F32 = mybir.dt.float32
BF16 = mybir.dt.bfloat16
AF = mybir.ActivationFunctionType
ALU = mybir.AluOpType
AX = mybir.AxisListType

NCORES = 8
D = 1024
NT = 4096
SEQ = 2048
NCH = NT // 128
TG = 512
NTG = NT // TG
CPG = TG // 128
NH = 16
EH = 64

ENGINES = ("sync", "tensor", "vector", "scalar", "gpsimd")

P3_NE = 32
P5_NE = 9
P1B_DBG = False

def dbg_ufm_hook(ctx, gidx):
    pass
P5_NOACC = False

FUSED = True
PHASE_GROUPS = [["p1a"], ["p1bc"], ["p2"], ["p3"], ["p4"], ["p5"]]


class S:
    def __init__(self, h):
        self.h = h
        self.n = 0

    @property
    def mark(self):
        return (self, self.n)


class Prog:
    def __init__(self, nc):
        self.nc = nc
        self.entries = {e: [] for e in ENGINES}

    def do(self, eng, fn, wait=(), inc=(), drain=False):
        w = []
        for x in wait:
            if isinstance(x, S):
                w.append((x.h, x.n))
            else:
                s, v = x
                assert v <= s.n, f"wait value {v} > recorded count {s.n}"
                if v > 0:
                    w.append((s.h, v))
        ii = []
        lst = inc if isinstance(inc, (list, tuple)) and not (
            len(inc) == 2 and isinstance(inc[0], S) and isinstance(inc[1], int)) else [inc]
        for x in lst:
            if x is None:
                continue
            if isinstance(x, S):
                x.n += 1
                ii.append((x.h, 1))
            else:
                s, n = x
                s.n += n
                ii.append((s.h, n))
        self.entries[eng].append((w, fn, ii, drain))

    def dma(self, eng, out, in_, sem, wait=(), **kw):
        self.do(eng, lambda e: e.dma_start(out, in_, **kw), wait=wait, inc=[(sem, 16)])

    def emit(self):
        with self.nc.Block() as block:
            for eng in ENGINES:
                entries = self.entries[eng]
                if not entries:
                    continue

                def body(e, entries=entries):
                    for w, fn, ii, drain in entries:
                        for h, v in w:
                            e.wait_ge(h, v)
                        inst = fn(e)
                        for h, n in ii:
                            inst.then_inc(h, n)
                        if drain:
                            e.drain()

                getattr(block, eng)(body)


class Ctx:
    def salloc(self, name, shape, dt):
        g = self.nc.sbuf_tensor(name, shape, dt)
        h = g.__enter__()
        self._phase_allocs.append(g)
        return h.ap()

    def palloc(self, name, shape, dt=F32):
        g = self.nc.psum_tensor(name, shape, dt)
        h = g.__enter__()
        self._phase_allocs.append(g)
        return h.ap()

    def free_phase_allocs(self):
        for g in reversed(self._phase_allocs):
            g.__exit__(None, None, None)
        self._phase_allocs = []

    def push_scope(self):
        self._scope_marks = getattr(self, "_scope_marks", [])
        self._scope_marks.append(len(self._phase_allocs))

    def pop_scope(self):
        k = self._scope_marks.pop()
        for g in reversed(self._phase_allocs[k:]):
            g.__exit__(None, None, None)
        del self._phase_allocs[k:]

    def phase_end(self):
        P = self.P
        P.do("sync", lambda e: e.nop(),
             wait=[self.dL, self.dG, self.dS])
        for e in ENGINES:
            P.do(e, lambda en: en.nop(), inc=self.bar)
        tgt = self.bar.n
        for e in ENGINES:
            P.do(e, lambda en: en.nop(), wait=[(self.bar, tgt)])



def load_cols(ctx, vec_ap, cols_dst, n128):
    P = ctx.P
    off = 0
    while off < n128:
        k = min(4, n128 - off)
        P.dma("sync", ctx.rowbuf[0:1, 0:k * 128],
              vec_ap[off * 128:(off + k) * 128].rearrange("(o d) -> o d", o=1), ctx.dL,
              wait=[ctx.tns.mark])
        for c in range(k):
            P.do("tensor",
                 lambda e, c=c, off=off: e.matmul(
                     ctx.ps_misc[:, off + c: off + c + 1],
                     ctx.rowbuf[0:1, c * 128:(c + 1) * 128],
                     ctx.ones_col_f32[0:1, 0:1], start=True, stop=True),
                 wait=[ctx.dL.mark, ctx.consts_ready, ctx.vec.mark], inc=ctx.tns)
        off += k
    P.do("vector", lambda e: e.tensor_copy(cols_dst, ctx.ps_misc[:, 0:n128]),
         wait=[ctx.tns], inc=ctx.vec, drain=True)


def emit_ln(ctx, src, rstd_out, src_wait=(), dump=None):
    P = ctx.P
    N = src.shape[-1]
    P.do("scalar", lambda e: e.activation(dump[:, 0:N], src, AF.Square,
                                          accum_out=ctx.sqsum),
         wait=list(src_wait) + [ctx.vec.mark], inc=ctx.scl, drain=True)
    P.do("vector", lambda e: e.tensor_reduce(ctx.sumcol, src, AX.X, ALU.add),
         wait=list(src_wait) + [ctx.scl.mark], drain=True)
    P.do("vector", lambda e: e.tensor_scalar(ctx.meancol, ctx.sumcol, 1.0 / N, None,
                                             ALU.mult), drain=True)
    P.do("vector", lambda e: e.tensor_tensor(ctx.msq, ctx.meancol, ctx.meancol, ALU.mult),
         drain=True)
    P.do("vector", lambda e: e.scalar_tensor_tensor(ctx.varcol, ctx.sqsum, 1.0 / N,
                                                    ctx.msq, ALU.mult, ALU.subtract),
         wait=[ctx.scl], inc=ctx.vec, drain=True)
    P.do("scalar", lambda e: e.activation(ctx.lnv, ctx.varcol, AF.Ln, bias=ctx.epscol),
         wait=[ctx.vec, ctx.consts_ready], drain=True)
    P.do("scalar", lambda e: e.activation(rstd_out, ctx.lnv, AF.Exp, scale=-0.5),
         inc=ctx.scl, drain=True)



def emit_p2(ctx):
    nc, P = ctx.nc, ctx.P
    dL, dG, dS, tns, vec, scl = ctx.dL, ctx.dG, ctx.dS, ctx.tns, ctx.vec, ctx.scl

    w1sb = ctx.salloc("p2_w1", [128, 8 * 4096], BF16)
    w2sb = ctx.salloc("p2_w2", [128, 32 * 1024], BF16)
    b2row = ctx.salloc("p2_b2", [1, 1024], BF16)
    x1t = ctx.salloc("p2_x1t", [128, CPG * 1024], F32)
    z1 = ctx.salloc("p2_z1", [128, 1024], F32)
    xnfm = ctx.salloc("p2_xnfm", [128, 8 * TG], BF16)
    hsb = ctx.salloc("p2_h", [128, 32 * TG], BF16)
    outb = ctx.salloc("p2_out", [128, 1024], F32)
    gmcols = ctx.salloc("p2_gmc", [128, 8], F32)
    bmcols = ctx.salloc("p2_bmc", [128, 8], F32)
    b1cols = ctx.salloc("p2_b1c", [128, 32], F32)

    ps_t = [ctx.palloc("p2_pst0", [128, 128]), ctx.palloc("p2_pst1", [128, 128])]
    ps_h = [ctx.palloc("p2_psh0", [128, 512]), ctx.palloc("p2_psh1", [128, 512])]
    ps_o = ctx.palloc("p2_pso", [128, 1024])

    P.dma("sync", w1sb.rearrange("p (c g) -> p c g", c=8),
          ctx.wsc["L0_attn_W1"].rearrange("(c p) g -> p c g", p=128), dL,
          wait=[ctx.castdone["L0_attn_W1"]])
    P.dma("sync", w2sb.rearrange("p (j d) -> p j d", j=32),
          ctx.wsc["L0_attn_W2"].rearrange("(j p) d -> p j d", p=128), dL,
          wait=[ctx.castdone["L0_attn_W2"]])
    P.dma("gpsimd", b2row, ctx.I["L0_attn_b2m"].rearrange("(o d) -> o d", o=1), dG)
    load_cols(ctx, ctx.I["L0_attn_gm"], gmcols, 8)
    load_cols(ctx, ctx.I["L0_attn_bm"], bmcols, 8)
    load_cols(ctx, ctx.I["L0_attn_b1m"], b1cols, 32)
    cols_ready = ctx.vec.mark
    wload = dL.mark

    x1v = ctx.streams["x1"].rearrange("(g u p) d -> g p u d", p=128, u=CPG)
    xAv = ctx.streams["xA"].rearrange("(g u p) d -> g p u d", p=128, u=CPG)
    xinv = ctx.streams["xin"].rearrange("(g u p) d -> g p u d", p=128, u=CPG)
    xst = ctx.salloc("p2_xst", [128, 1024], F32)

    for tg in range(NTG):
        P.dma("sync", x1t.rearrange("p (u d) -> p u d", u=CPG), x1v[tg], dL,
              wait=[vec.mark])
        ld = dL.mark
        for u in range(CPG):
            x1c = x1t[:, u * 1024:(u + 1) * 1024]
            emit_ln(ctx, x1c, ctx.rstd, src_wait=[ld], dump=z1)
            P.do("vector", lambda e, x1c=x1c: e.tensor_scalar(
                    z1, x1c, ctx.meancol, ctx.rstd, ALU.subtract, ALU.mult),
                 wait=[scl], inc=vec, drain=True)
            for c in range(8):
                par = c % 2
                P.do("tensor", lambda e, c=c, par=par: e.transpose(
                        ps_t[par], z1[:, c * 128:(c + 1) * 128], ctx.id_f32),
                     wait=[vec.mark, scl.mark], inc=tns)
                P.do("vector", lambda e, c=c, u=u, par=par: e.tensor_scalar(
                        xnfm[:, c * TG + u * 128: c * TG + (u + 1) * 128],
                        ps_t[par], gmcols[:, c:c + 1], bmcols[:, c:c + 1],
                        ALU.mult, ALU.add),
                     wait=[tns.mark], inc=vec, drain=False)
        for j in range(32):
            pb = ps_h[j % 2]
            for c in range(8):
                P.do("tensor", lambda e, j=j, c=c, pb=pb: e.matmul(
                        pb, w1sb[:, c * 4096 + j * 128: c * 4096 + (j + 1) * 128],
                        xnfm[:, c * TG:(c + 1) * TG],
                        start=(c == 0), stop=(c == 7)),
                     wait=([vec.mark, scl.mark, wload] if c == 0 else []),
                     inc=(tns if c == 7 else None))
            P.do("scalar", lambda e, j=j, pb=pb: e.activation(
                    hsb[:, j * TG:(j + 1) * TG], pb, AF.Gelu,
                    bias=b1cols[:, j:j + 1]),
                 wait=[tns.mark, cols_ready], inc=scl, drain=False)
        for u in range(CPG):
            xs = xst
            P.dma("sync", xs, xinv[tg][:, u], dL, wait=[ctx.pol.mark])
            xld = dL.mark
            for n in range(2):
                for j in range(32):
                    P.do("tensor", lambda e, u=u, n=n, j=j: e.matmul(
                            ps_o[:, n * 512:(n + 1) * 512],
                            hsb[:, j * TG + u * 128: j * TG + (u + 1) * 128],
                            w2sb[:, j * 1024 + n * 512: j * 1024 + n * 512 + 512],
                            start=(j == 0), stop=False),
                         wait=([scl.mark, vec.mark, wload] if (n == 0 and j == 0) else []))
                P.do("tensor", lambda e, n=n: e.matmul(
                        ps_o[:, n * 512:(n + 1) * 512], ctx.ones_row_bf[0:1, 0:128],
                        b2row[0:1, n * 512:(n + 1) * 512], start=False, stop=True),
                     inc=(tns if n == 1 else None))
            P.do("vector", lambda e, u=u: e.scalar_tensor_tensor(
                    outb, ps_o, 1.0, x1t[:, u * 1024:(u + 1) * 1024], ALU.mult, ALU.add),
                 wait=[tns, dS.mark], inc=vec, drain=False)
            P.do("gpsimd", lambda e: e.tensor_tensor(outb, outb, xs, ALU.add),
                 wait=[vec.mark, xld], inc=ctx.pol, drain=True)
            P.dma("sync", xAv[tg][:, u], outb, dS, wait=[ctx.pol.mark])



def emit_p1a(ctx):
    nc, P = ctx.nc, ctx.P
    dL, dG, dS, tns, vec, scl, pol = (ctx.dL, ctx.dG, ctx.dS, ctx.tns, ctx.vec,
                                      ctx.scl, ctx.pol)

    wq = ctx.salloc("p1a_wq", [128, 8 * 3072], BF16)
    bqrow = ctx.salloc("p1a_bq", [1, 3072], BF16)
    g1cols = ctx.salloc("p1a_g1c", [128, 8], F32)
    b1cols = ctx.salloc("p1a_b1c", [128, 8], F32)
    kfm = ctx.salloc("p1a_kfm", [128, 8 * SEQ], BF16)
    vtm = ctx.salloc("p1a_vtm", [128, 16 * 1024], BF16)
    qfm = ctx.salloc("p1a_qfm", [128, 8 * TG], BF16)
    xnfm = ctx.salloc("p1a_xnfm", [128, 8 * TG], BF16)
    sout = ctx.salloc("p1a_sout", [128, 8 * TG], BF16)
    xt = ctx.salloc("p1a_xt", [128, CPG * 1024], F32)
    zb = ctx.salloc("p1a_zb", [128, 1024], F32)
    z2 = ctx.salloc("p1a_z2", [128, 1024], F32)
    pf = ctx.salloc("p1a_pf", [128, 2 * 256], F32)
    pm = ctx.salloc("p1a_pm", [128, 2 * 256], F32)
    pbf = ctx.salloc("p1a_pbf", [128, 2 * 256], BF16)
    ptsb = ctx.salloc("p1a_ptsb", [128, 2 * 256], BF16)
    rmax = ctx.salloc("p1a_rmax", [128, 2], F32)
    rmneg = ctx.salloc("p1a_rmneg", [128, 2], F32)
    rs = ctx.salloc("p1a_rs", [128, 2], F32)
    rcp = ctx.salloc("p1a_rcp", [128, 2], F32)
    mgen = ctx.salloc("p1a_mgen", [128, 256], F32)
    mb0 = ctx.salloc("p1a_mb0", [128, 128], F32)

    ps_q = [ctx.palloc("p1a_psq0", [128, 512])]
    ps_q.append(ps_q[0])
    ps_v = ctx.palloc("p1a_psv", [128, 1024])
    ps_sm = [ctx.palloc("p1a_pssm0", [128, 256]), ctx.palloc("p1a_pssm1", [128, 256])]
    ps_pt = ctx.palloc("p1a_pspt", [128, 256], BF16)
    ps_pv = ctx.palloc("p1a_pspv", [64, 128])

    P.do("gpsimd", lambda e: e.memset(mgen, 1.0), drain=True)
    P.do("gpsimd", lambda e: e.affine_select(out=mgen, in_=mgen,
            compare_op=ALU.is_ge, fill=0.0, base=128, pattern=[[-1, 256]],
            channel_multiplier=1), drain=True)
    P.do("gpsimd", lambda e: e.affine_select(out=mgen, in_=mgen,
            compare_op=ALU.is_ge, fill=0.0, base=-1, pattern=[[1, 256]],
            channel_multiplier=-1), drain=True)
    P.do("gpsimd", lambda e: e.memset(mb0, 1.0), drain=True)
    P.do("gpsimd", lambda e: e.affine_select(out=mb0, in_=mb0,
            compare_op=ALU.is_ge, fill=0.0, base=0, pattern=[[-1, 128]],
            channel_multiplier=1), inc=pol, drain=True)
    masks_ready = pol.mark

    P.dma("sync", wq.rearrange("p (c f) -> p c f", c=8),
          ctx.wsc["L0_attn_Wqkv_a"].rearrange("(c p) f -> p c f", p=128), dL,
          wait=[ctx.castdone["L0_attn_Wqkv_a"]])
    P.dma("gpsimd", bqrow, ctx.I["L0_attn_bqkv_a"].rearrange("(o f) -> o f", o=1), dG)
    load_cols(ctx, ctx.I["L0_attn_g1"], g1cols, 8)
    load_cols(ctx, ctx.I["L0_attn_b1"], b1cols, 8)
    wld = (dL.mark, dG.mark, vec.mark)

    xinv = ctx.streams["xin"].rearrange("(g u p) d -> g p u d", p=128, u=CPG)
    swav = ctx.streams["swa"]
    xnfv = ctx.streams["xnf"]

    for s in range(2):
        for g4 in range(4):
            gidx = s * 4 + g4
            P.dma("sync", xt.rearrange("p (u d) -> p u d", u=CPG), xinv[gidx], dL,
                  wait=[scl.mark, vec.mark])
            ld = dL.mark
            for u in range(CPG):
                xc = xt[:, u * 1024:(u + 1) * 1024]
                emit_ln(ctx, xc, ctx.rstd, src_wait=[ld], dump=zb)
                P.do("vector", lambda e, xc=xc: e.tensor_scalar(
                        zb, xc, ctx.meancol, ctx.rstd, ALU.subtract, ALU.mult),
                     wait=[scl], inc=vec, drain=True)
                emit_ln(ctx, zb, ctx.rstd, src_wait=[], dump=z2)
                P.do("vector", lambda e: e.tensor_scalar(
                        z2, zb, ctx.meancol, ctx.rstd, ALU.subtract, ALU.mult),
                     wait=[scl], inc=vec, drain=True)
                for c in range(8):
                    P.do("tensor", lambda e, c=c: e.transpose(
                            ps_sm[c % 2][:, 0:128],
                            z2[:, c * 128:(c + 1) * 128], ctx.id_f32),
                         wait=[vec.mark, scl.mark, ctx.consts_ready], inc=tns)
                    P.do("vector", lambda e, c=c, u=u: e.tensor_scalar(
                            xnfm[:, c * TG + u * 128: c * TG + (u + 1) * 128],
                            ps_sm[c % 2][:, 0:128],
                            g1cols[:, c:c + 1], b1cols[:, c:c + 1],
                            ALU.mult, ALU.add),
                         wait=[tns.mark, wld[2], dS.mark], inc=vec, drain=True)
            P.dma("sync", xnfv[gidx], xnfm.rearrange("p (c t) -> p c t", c=8), dS,
                  wait=[vec.mark])
            for fc in range(8):
                pq = ps_q[fc % 2]
                for c in range(8):
                    P.do("tensor", lambda e, fc=fc, c=c, pq=pq: e.matmul(
                            pq, wq[:, c * 3072 + fc * 128: c * 3072 + (fc + 1) * 128],
                            xnfm[:, c * TG:(c + 1) * TG],
                            start=(c == 0), stop=False),
                         wait=([vec.mark, scl.mark, wld[0], wld[1]] if c == 0 else []))
                P.do("tensor", lambda e, fc=fc, pq=pq: e.matmul(
                        pq, bqrow[0:1, fc * 128:(fc + 1) * 128],
                        ctx.ones_row_bf[0:1, 0:TG], start=False, stop=True), inc=tns)
                P.do("scalar", lambda e, fc=fc, pq=pq: e.activation(
                        qfm[:, fc * TG:(fc + 1) * TG], pq, AF.Copy),
                     wait=[tns.mark], inc=scl, drain=False)
            for fc in range(8):
                pq = ps_q[fc % 2]
                for c in range(8):
                    P.do("tensor", lambda e, fc=fc, c=c, pq=pq: e.matmul(
                            pq, wq[:, c * 3072 + 1024 + fc * 128:
                                   c * 3072 + 1024 + (fc + 1) * 128],
                            xnfm[:, c * TG:(c + 1) * TG],
                            start=(c == 0), stop=False),
                         wait=([scl.mark] if c == 0 else []))
                P.do("tensor", lambda e, fc=fc, pq=pq: e.matmul(
                        pq, bqrow[0:1, 1024 + fc * 128: 1024 + (fc + 1) * 128],
                        ctx.ones_row_bf[0:1, 0:TG], start=False, stop=True), inc=tns)
                P.do("scalar", lambda e, fc=fc, g4=g4, pq=pq: e.activation(
                        kfm[:, fc * SEQ + g4 * TG: fc * SEQ + (g4 + 1) * TG],
                        pq, AF.Copy),
                     wait=[tns.mark], inc=scl, drain=False)
            for u in range(CPG):
                cs = g4 * CPG + u
                for n in range(2):
                    for c in range(8):
                        P.do("tensor", lambda e, u=u, n=n, c=c: e.matmul(
                                ps_v[:, n * 512:(n + 1) * 512],
                                xnfm[:, c * TG + u * 128: c * TG + (u + 1) * 128],
                                wq[:, c * 3072 + 2048 + n * 512:
                                   c * 3072 + 2048 + (n + 1) * 512],
                                start=(c == 0), stop=False),
                             wait=([vec.mark, scl.mark] if (n == 0 and c == 0) else []))
                    P.do("tensor", lambda e, n=n: e.matmul(
                            ps_v[:, n * 512:(n + 1) * 512],
                            ctx.ones_row_bf[0:1, 0:128],
                            bqrow[0:1, 2048 + n * 512: 2048 + (n + 1) * 512],
                            start=False, stop=True),
                         inc=(tns if n == 1 else None))
                P.do("vector", lambda e, cs=cs: e.tensor_copy(
                        vtm[:, cs * 1024:(cs + 1) * 1024], ps_v),
                     wait=[tns], inc=vec, drain=True)
            smark = [scl.mark, scl.mark]
            state = {"ptmark": scl.mark, "pvmark": scl.mark, "tail": None}

            def emit_tail(t):
                (u, h, par, W2, klo, pbf_mark) = t
                nw = W2 // 128
                hp = slice((h % 2) * 64, (h % 2) * 64 + 64)
                po = slice(par * 256, par * 256 + W2)
                for w in range(nw):
                    P.do("tensor", lambda e, w=w, par=par: e.transpose(
                            ps_pt[:, w * 128:(w + 1) * 128],
                            pbf[:, par * 256 + w * 128: par * 256 + (w + 1) * 128],
                            ctx.id_bf),
                         wait=([pbf_mark, state["ptmark"]] if w == 0 else []),
                         inc=(tns if w == nw - 1 else None))
                P.do("scalar", lambda e, W2=W2, po=po: e.copy(
                        ptsb[:, po], ps_pt[:, 0:W2]),
                     wait=[tns.mark, state["pvmark"]], inc=scl, drain=True)
                state["ptmark"] = scl.mark
                for w in range(nw):
                    kv = klo // 128 + w
                    P.do("tensor", lambda e, kv=kv, h=h, w=w, par=par: e.matmul(
                            ps_pv, vtm[:, kv * 1024 + h * 64: kv * 1024 + h * 64 + 64],
                            ptsb[:, par * 256 + w * 128: par * 256 + (w + 1) * 128],
                            start=(w == 0), stop=(w == nw - 1)),
                         wait=([scl.mark, vec.mark] if w == 0 else []),
                         inc=(tns if w == nw - 1 else None))
                P.do("scalar", lambda e, h=h, hp=hp, u=u: e.copy(
                        sout[hp, (h // 2) * TG + u * 128:
                             (h // 2) * TG + (u + 1) * 128], ps_pv),
                     wait=[tns.mark, dS.mark], inc=scl, drain=True)
                state["pvmark"] = scl.mark

            for u in range(CPG):
                cs = g4 * CPG + u
                b0 = (cs == 0)
                W2 = 128 if b0 else 256
                for h in range(NH):
                    par = h % 2
                    sm = ps_sm[par]
                    po = slice(par * 256, par * 256 + W2)
                    pc = slice(par, par + 1)
                    hp = slice((h % 2) * 64, (h % 2) * 64 + 64)
                    fc = h // 2
                    klo = (cs - (0 if b0 else 1)) * 128
                    P.do("tensor", lambda e, u=u, h=h, hp=hp, fc=fc, klo=klo, W2=W2,
                         sm=sm: e.matmul(sm[:, 0:W2],
                                  qfm[hp, fc * TG + u * 128: fc * TG + (u + 1) * 128],
                                  kfm[hp, fc * SEQ + klo: fc * SEQ + klo + W2],
                                  start=True, stop=True),
                         wait=[smark[par]], inc=tns)
                    P.do("vector", lambda e, W2=W2, sm=sm, pc=pc: e.tensor_reduce(
                            rmax[:, pc], sm[:, 0:W2], AX.X, ALU.max),
                         wait=[tns.mark], inc=vec, drain=True)
                    P.do("scalar", lambda e, pc=pc: e.mul(rmneg[:, pc], rmax[:, pc],
                                                          -0.125),
                         wait=[vec.mark], drain=True)
                    P.do("scalar", lambda e, W2=W2, sm=sm, po=po, pc=pc: e.activation(
                            pf[:, po], sm[:, 0:W2], AF.Exp,
                            bias=rmneg[:, pc], scale=0.125),
                         inc=scl, drain=True)
                    smark[par] = scl.mark
                    P.do("vector", lambda e, W2=W2, b0=b0, po=po: e.tensor_tensor(
                            pm[:, po], pf[:, po],
                            (mb0 if b0 else mgen)[:, 0:W2], ALU.mult),
                         wait=[scl.mark, masks_ready], drain=True)
                    P.do("vector", lambda e, W2=W2, po=po, pc=pc: e.tensor_reduce(
                            rs[:, pc], pm[:, po], AX.X, ALU.add), drain=True)
                    P.do("vector", lambda e, pc=pc: e.reciprocal(rcp[:, pc], rs[:, pc]),
                         drain=True)
                    P.do("vector", lambda e, W2=W2, po=po, pc=pc: e.tensor_scalar(
                            pbf[:, po], pm[:, po], rcp[:, pc], None, ALU.mult),
                         inc=vec, drain=True)
                    newtail = (u, h, par, W2, klo, vec.mark)
                    if state["tail"] is not None:
                        emit_tail(state["tail"])
                    state["tail"] = newtail
            if state["tail"] is not None:
                emit_tail(state["tail"])
                state["tail"] = None
            P.dma("sync", swav[gidx], sout.rearrange("p (c t) -> p c t", c=8), dS,
                  wait=[scl.mark, vec.mark])



def emit_p1bc(ctx):
    nc, P = ctx.nc, ctx.P
    dL, dG, dS, tns, vec, scl, pol = (ctx.dL, ctx.dG, ctx.dS, ctx.tns, ctx.vec,
                                      ctx.scl, ctx.pol)

    wqs = ctx.salloc("p1b_wqs", [128, 8 * 3072], BF16)
    wp = ctx.salloc("p1b_wp", [128, 8 * 1024], BF16)
    wos = ctx.salloc("p1b_wos", [128, 8 * 1024], BF16)
    wo = ctx.salloc("p1b_wo", [128, 8 * 1024], BF16)
    bqrow = ctx.salloc("p1b_bq", [1, 3072], BF16)
    bpbo = ctx.salloc("p1b_bpbo", [1, 1024], BF16)
    borow = ctx.salloc("p1b_bo", [1, 1024], BF16)
    brow2 = ctx.salloc("p1b_br2", [1, 1024], BF16)
    triu = ctx.salloc("p1b_triu", [128, 128], F32)
    xt = ctx.salloc("p1b_xt", [128, CPG * 1024], F32)
    xnin = ctx.salloc("p1b_xnin", [128, 8 * TG], BF16)
    swin = ctx.salloc("p1b_swin", [128, 8 * TG], BF16)
    qsf = ctx.salloc("p1b_qsf", [128, 8 * TG], BF16)
    ksf = ctx.salloc("p1b_ksf", [128, 8 * TG], BF16)
    vst = ctx.salloc("p1b_vst", [128, CPG * 1024], BF16)
    kst = ctx.salloc("p1b_kst", [128, CPG * 1024], BF16)
    lout = ctx.salloc("p1b_lout", [128, 8 * TG], BF16)
    ufm = ctx.salloc("p1b_ufm", [128, 8 * TG], BF16)
    Sf = ctx.salloc("p1b_Sf", [128, 8 * 64], F32)
    Sb = ctx.salloc("p1b_Sb", [128, 8 * 64], BF16)
    zsf = ctx.salloc("p1b_zsf", [128, 8], F32)
    zsb = ctx.salloc("p1b_zsb", [128, 8], BF16)
    t0b = ctx.salloc("p1b_t0", [128, 512], F32)
    t1b = t0b
    t2b = ctx.salloc("p1b_t2", [128, 512], F32)
    atb = ctx.salloc("p1b_atb", [128, 256], BF16)
    obf = ctx.salloc("p1b_obf", [128, 128], BF16)
    dcol = ctx.salloc("p1b_dcol", [128, 2], F32)
    rcpd = ctx.salloc("p1b_rcpd", [128, 2], F32)
    zb = ctx.salloc("p1b_zb", [128, 1024], F32)
    x1b = zb

    if P1B_DBG:
        ctx.dbg = nc.dram_tensor("s_dbg", [NTG, 128, 8, TG], BF16,
                                 kind="ExternalOutput").ap()
        ctx.dbg2 = nc.dram_tensor("s_dbg2", [NTG, 128, 8, TG], BF16,
                                  kind="ExternalOutput").ap()
    ps_q = ctx.palloc("p1b_psq", [128, 512])
    ps_v = ctx.palloc("p1b_psv", [128, 1024])
    ps_at = ctx.palloc("p1b_psat", [128, 128])
    ps_nd = ctx.palloc("p1b_psnd", [128, 128])
    ps_Sz = ctx.palloc("p1b_psSz", [64, 128])
    ps_ot = ctx.palloc("p1b_psot", [128, 128], BF16)

    P.do("gpsimd", lambda e: e.memset(triu, 1.0), drain=True)
    P.do("gpsimd", lambda e: e.affine_select(out=triu, in_=triu,
            compare_op=ALU.is_ge, fill=0.0, base=0, pattern=[[1, 128]],
            channel_multiplier=-1), inc=pol, drain=True)
    masks_ready = pol.mark

    P.dma("sync", wqs.rearrange("p (c f) -> p c f", c=8),
          ctx.wsc["L0_attn_Wqkv_s"].rearrange("(c p) f -> p c f", p=128), dL,
          wait=[ctx.castdone["L0_attn_Wqkv_s"]])
    P.dma("sync", wp.rearrange("p (c f) -> p c f", c=8),
          ctx.wsc["L0_attn_Wp_a"].rearrange("(c p) f -> p c f", p=128), dL,
          wait=[ctx.castdone["L0_attn_Wp_a"]])
    P.dma("sync", wos.rearrange("p (c f) -> p c f", c=8),
          ctx.wsc["L0_attn_Wo_s"].rearrange("(c p) f -> p c f", p=128), dL,
          wait=[ctx.castdone["L0_attn_Wo_s"]])
    P.dma("sync", wo.rearrange("p (c f) -> p c f", c=8),
          ctx.wsc["L0_attn_Wo"].rearrange("(c p) f -> p c f", p=128), dL,
          wait=[ctx.castdone["L0_attn_Wo"]])
    P.dma("gpsimd", bqrow, ctx.I["L0_attn_bqkv_s"].rearrange("(o f) -> o f", o=1), dG)
    P.dma("gpsimd", bpbo, ctx.I["L0_attn_bp_a"].rearrange("(o f) -> o f", o=1), dG)
    P.dma("gpsimd", brow2, ctx.I["L0_attn_bo_s"].rearrange("(o f) -> o f", o=1), dG)
    P.dma("gpsimd", borow, ctx.I["L0_attn_bo"].rearrange("(o f) -> o f", o=1), dG)
    P.do("vector", lambda e: e.tensor_tensor(bpbo, bpbo, brow2, ALU.add),
         wait=[dG], inc=vec, drain=True)
    wld = (dL.mark, dG.mark, vec.mark)

    xinv = ctx.streams["xin"].rearrange("(g u p) d -> g p u d", p=128, u=CPG)
    x1v = ctx.streams["x1"].rearrange("(g u p) d -> g p u d", p=128, u=CPG)
    swav = ctx.streams["swa"]
    xnfv = ctx.streams["xnf"]

    for s in range(2):
        for g4 in range(4):
            gidx = s * 4 + g4
            P.dma("sync", xt.rearrange("p (u d) -> p u d", u=CPG), xinv[gidx], dL,
                  wait=[scl.mark, vec.mark])
            P.dma("sync", xnin.rearrange("p (c t) -> p c t", c=8), xnfv[gidx], dL)
            P.dma("sync", swin.rearrange("p (c t) -> p c t", c=8), swav[gidx], dL)
            ld = dL.mark
            for fc in range(8):
                for c in range(8):
                    P.do("tensor", lambda e, fc=fc, c=c: e.matmul(
                            ps_q, wqs[:, c * 3072 + fc * 128: c * 3072 + (fc + 1) * 128],
                            xnin[:, c * TG:(c + 1) * TG],
                            start=(c == 0), stop=False),
                         wait=([vec.mark, scl.mark, ld, wld[0], wld[1]]
                               if c == 0 else []))
                P.do("tensor", lambda e, fc=fc: e.matmul(
                        ps_q, bqrow[0:1, fc * 128:(fc + 1) * 128],
                        ctx.ones_row_bf[0:1, 0:TG], start=False, stop=True), inc=tns)
                P.do("vector", lambda e: e.tensor_scalar(
                        t0b, ps_q, 0.125, 0.0, ALU.mult, ALU.min),
                     wait=[tns.mark], drain=True)
                P.do("vector", lambda e: e.tensor_scalar(
                        t2b, ps_q, 0.125, 0.0, ALU.mult, ALU.max),
                     inc=vec, drain=True)
                P.do("scalar", lambda e: e.activation(t1b, t0b, AF.Exp),
                     wait=[vec], inc=scl, drain=True)
                P.do("vector", lambda e, fc=fc: e.scalar_tensor_tensor(
                        qsf[:, fc * TG:(fc + 1) * TG], t1b, 1e-4, t2b,
                        ALU.add, ALU.add),
                     wait=[scl], inc=vec, drain=True)
            for fc in range(8):
                for c in range(8):
                    P.do("tensor", lambda e, fc=fc, c=c: e.matmul(
                            ps_q, wqs[:, c * 3072 + 1024 + fc * 128:
                                      c * 3072 + 1024 + (fc + 1) * 128],
                            xnin[:, c * TG:(c + 1) * TG],
                            start=(c == 0), stop=False),
                         wait=([vec.mark] if c == 0 else []))
                P.do("tensor", lambda e, fc=fc: e.matmul(
                        ps_q, bqrow[0:1, 1024 + fc * 128: 1024 + (fc + 1) * 128],
                        ctx.ones_row_bf[0:1, 0:TG], start=False, stop=True), inc=tns)
                P.do("vector", lambda e: e.tensor_single_scalar(
                        t0b, ps_q, 0.0, ALU.min),
                     wait=[tns.mark], drain=True)
                P.do("vector", lambda e: e.tensor_single_scalar(
                        t2b, ps_q, 0.0, ALU.max),
                     inc=vec, drain=True)
                P.do("scalar", lambda e: e.activation(t1b, t0b, AF.Exp),
                     wait=[vec], inc=scl, drain=True)
                P.do("vector", lambda e, fc=fc: e.scalar_tensor_tensor(
                        ksf[:, fc * TG:(fc + 1) * TG], t1b, 1e-4, t2b,
                        ALU.add, ALU.add),
                     wait=[scl], inc=vec, drain=True)
            for u in range(CPG):
                for n in range(2):
                    for c in range(8):
                        P.do("tensor", lambda e, u=u, n=n, c=c: e.matmul(
                                ps_v[:, n * 512:(n + 1) * 512],
                                xnin[:, c * TG + u * 128: c * TG + (u + 1) * 128],
                                wqs[:, c * 3072 + 2048 + n * 512:
                                    c * 3072 + 2048 + (n + 1) * 512],
                                start=(c == 0), stop=False),
                             wait=([vec.mark, scl.mark] if (n == 0 and c == 0) else []))
                    P.do("tensor", lambda e, n=n: e.matmul(
                            ps_v[:, n * 512:(n + 1) * 512],
                            ctx.ones_row_bf[0:1, 0:128],
                            bqrow[0:1, 2048 + n * 512: 2048 + (n + 1) * 512],
                            start=False, stop=True),
                         inc=(tns if n == 1 else None))
                P.do("vector", lambda e, u=u: e.tensor_copy(
                        vst[:, u * 1024:(u + 1) * 1024], ps_v),
                     wait=[tns], inc=vec, drain=True)
            for u in range(CPG):
                for fc in range(8):
                    P.do("tensor", lambda e, u=u, fc=fc: e.transpose(
                            ps_ot, ksf[:, fc * TG + u * 128: fc * TG + (u + 1) * 128],
                            ctx.id_bf),
                         wait=[vec.mark], inc=tns)
                    P.do("vector", lambda e, u=u, fc=fc: e.tensor_copy(
                            kst[:, u * 1024 + fc * 128: u * 1024 + (fc + 1) * 128],
                            ps_ot),
                         wait=[tns.mark], inc=vec, drain=True)
            lstate = {"atmark": [vec.mark, vec.mark], "ndmark": [vec.mark, vec.mark],
                      "otmark": scl.mark, "Szmark": vec.mark, "tail": None}
            ps_at2 = [ps_at, ps_v[:, 0:128]]
            ps_nd2 = [ps_nd, ps_q[:, 0:128]]

            def lin_tail(t):
                (u, h, par, obf_mark, atb_mark) = t
                hp = slice((h % 2) * 64, (h % 2) * 64 + 64)
                fc = h // 2
                tok = slice(fc * TG + u * 128, fc * TG + (u + 1) * 128)
                cs_t = g4 * CPG + u
                first = (cs_t == 0)
                P.do("tensor", lambda e, par=par: e.transpose(
                        ps_ot[0:64, :], obf[:, par * 64:(par + 1) * 64], ctx.id_bf),
                     wait=[obf_mark, lstate["otmark"]], inc=tns)
                P.do("scalar", lambda e, hp=hp, tok=tok: e.copy(
                        lout[hp, tok], ps_ot[0:64, :]),
                     wait=[tns.mark], inc=scl, drain=True)
                lstate["otmark"] = scl.mark
                P.do("tensor", lambda e, u=u, fc=fc, h=h: e.matmul(
                        ps_Sz[:, 0:64],
                        kst[:, u * 1024 + fc * 128 + (h % 2) * 64:
                            u * 1024 + fc * 128 + (h % 2) * 64 + 64],
                        vst[:, u * 1024 + h * 64: u * 1024 + h * 64 + 64],
                        start=True, stop=True),
                     wait=[lstate["Szmark"]])
                P.do("tensor", lambda e, u=u, fc=fc, h=h: e.matmul(
                        ps_Sz[:, 64:65],
                        kst[:, u * 1024 + fc * 128 + (h % 2) * 64:
                            u * 1024 + fc * 128 + (h % 2) * 64 + 64],
                        ctx.ones_col_bf, start=True, stop=True), inc=tns)
                Ssl = (hp, slice((h // 2) * 64, (h // 2 + 1) * 64))
                zsl = (hp, slice(h // 2, h // 2 + 1))
                if first:
                    P.do("vector", lambda e, Ssl=Ssl: e.tensor_copy(
                            Sf[Ssl], ps_Sz[:, 0:64]),
                         wait=[tns.mark], drain=True)
                    P.do("vector", lambda e, zsl=zsl: e.tensor_copy(
                            zsf[zsl], ps_Sz[:, 64:65]),
                         inc=vec, drain=True)
                else:
                    P.do("vector", lambda e, Ssl=Ssl: e.tensor_tensor(
                            Sf[Ssl], Sf[Ssl], ps_Sz[:, 0:64], ALU.add),
                         wait=[tns.mark], drain=True)
                    P.do("vector", lambda e, zsl=zsl: e.tensor_tensor(
                            zsf[zsl], zsf[zsl], ps_Sz[:, 64:65], ALU.add),
                         inc=vec, drain=True)
                lstate["Szmark"] = vec.mark

            for u in range(CPG):
                cs = g4 * CPG + u
                first = (cs == 0)
                for h in range(NH):
                    par = h % 2
                    pat = ps_at2[par]
                    pnd = ps_nd2[par]
                    hp = slice((h % 2) * 64, (h % 2) * 64 + 64)
                    fc = h // 2
                    tok = slice(fc * TG + u * 128, fc * TG + (u + 1) * 128)
                    atsl = atb[:, par * 128:(par + 1) * 128]
                    P.do("tensor", lambda e, hp=hp, tok=tok, pat=pat: e.matmul(
                            pat, ksf[hp, tok], qsf[hp, tok],
                            start=True, stop=True),
                         wait=[lstate["atmark"][par]], inc=tns)
                    P.do("vector", lambda e, pat=pat, atsl=atsl: e.scalar_tensor_tensor(
                            atsl, pat, 1.0, triu, ALU.mult, ALU.mult),
                         wait=[tns.mark, masks_ready], inc=vec, drain=True)
                    lstate["atmark"][par] = vec.mark
                    if first:
                        P.do("tensor", lambda e, u=u, h=h, pnd=pnd, atsl=atsl: e.matmul(
                                pnd[:, 0:64], atsl,
                                vst[:, u * 1024 + h * 64: u * 1024 + h * 64 + 64],
                                start=True, stop=True),
                             wait=[vec.mark, lstate["ndmark"][par]])
                        P.do("tensor", lambda e, pnd=pnd, atsl=atsl: e.matmul(
                                pnd[:, 64:65], atsl, ctx.ones_col_bf,
                                start=True, stop=True), inc=tns)
                    else:
                        P.do("tensor", lambda e, hp=hp, tok=tok, h=h, pnd=pnd: e.matmul(
                                pnd[:, 0:64], qsf[hp, tok],
                                Sb[hp, (h // 2) * 64:(h // 2 + 1) * 64],
                                start=True, stop=False),
                             wait=[vec.mark, lstate["ndmark"][par]])
                        P.do("tensor", lambda e, u=u, h=h, pnd=pnd, atsl=atsl: e.matmul(
                                pnd[:, 0:64], atsl,
                                vst[:, u * 1024 + h * 64: u * 1024 + h * 64 + 64],
                                start=False, stop=True))
                        P.do("tensor", lambda e, hp=hp, tok=tok, h=h, pnd=pnd: e.matmul(
                                pnd[:, 64:65], qsf[hp, tok],
                                zsb[hp, h // 2: h // 2 + 1],
                                start=True, stop=False))
                        P.do("tensor", lambda e, pnd=pnd, atsl=atsl: e.matmul(
                                pnd[:, 64:65], atsl, ctx.ones_col_bf,
                                start=False, stop=True), inc=tns)
                    P.do("vector", lambda e, pnd=pnd, par=par: e.tensor_single_scalar(
                            dcol[:, par:par + 1], pnd[:, 64:65], 1e-6, ALU.add),
                         wait=[tns.mark], drain=True)
                    P.do("vector", lambda e, par=par: e.reciprocal(
                            rcpd[:, par:par + 1], dcol[:, par:par + 1]), drain=True)
                    P.do("vector", lambda e, pnd=pnd, par=par: e.tensor_scalar(
                            obf[:, par * 64:(par + 1) * 64], pnd[:, 0:64],
                            rcpd[:, par:par + 1], None, ALU.mult),
                         inc=vec, drain=True)
                    lstate["ndmark"][par] = vec.mark
                    newt = (u, h, par, vec.mark, vec.mark)
                    if lstate["tail"] is not None:
                        lin_tail(lstate["tail"])
                    lstate["tail"] = newt
                if lstate["tail"] is not None:
                    lin_tail(lstate["tail"])
                    lstate["tail"] = None
                P.do("vector", lambda e: e.tensor_copy(Sb, Sf), drain=True)
                P.do("vector", lambda e: e.tensor_copy(zsb, zsf), inc=vec, drain=True)
            for fc in range(8):
                for c in range(8):
                    P.do("tensor", lambda e, fc=fc, c=c: e.matmul(
                            ps_q, wp[:, c * 1024 + fc * 128: c * 1024 + (fc + 1) * 128],
                            swin[:, c * TG:(c + 1) * TG],
                            start=(c == 0), stop=False),
                         wait=([vec.mark, scl.mark, ld] if c == 0 else []))
                for c in range(8):
                    P.do("tensor", lambda e, fc=fc, c=c: e.matmul(
                            ps_q, wos[:, c * 1024 + fc * 128: c * 1024 + (fc + 1) * 128],
                            lout[:, c * TG:(c + 1) * TG],
                            start=False, stop=False))
                P.do("tensor", lambda e, fc=fc: e.matmul(
                        ps_q, bpbo[0:1, fc * 128:(fc + 1) * 128],
                        ctx.ones_row_bf[0:1, 0:TG], start=False, stop=True),
                     wait=[wld[2]], inc=tns)
                P.do("scalar", lambda e, fc=fc: e.activation(
                        ufm[:, fc * TG:(fc + 1) * TG], ps_q, AF.Copy),
                     wait=[tns.mark], inc=scl, drain=False)
            if P1B_DBG:
                P.dma("sync", ctx.dbg2[gidx], ufm.rearrange("p (c t) -> p c t", c=8),
                      dS, wait=[scl.mark])
            for u in range(CPG):
                xc = xt[:, u * 1024:(u + 1) * 1024]
                emit_ln(ctx, xc, ctx.rstd, src_wait=[ld, dS.mark], dump=zb)
                P.do("vector", lambda e, xc=xc: e.tensor_scalar(
                        zb, xc, ctx.meancol, ctx.rstd, ALU.subtract, ALU.mult),
                     wait=[scl, dS.mark], inc=vec, drain=True)
                for n in range(2):
                    for c in range(8):
                        P.do("tensor", lambda e, u=u, n=n, c=c: e.matmul(
                                ps_v[:, n * 512:(n + 1) * 512],
                                ufm[:, c * TG + u * 128: c * TG + (u + 1) * 128],
                                wo[:, c * 1024 + n * 512: c * 1024 + (n + 1) * 512],
                                start=(c == 0), stop=False),
                             wait=([scl.mark, vec.mark] if (n == 0 and c == 0) else []))
                    P.do("tensor", lambda e, n=n: e.matmul(
                            ps_v[:, n * 512:(n + 1) * 512],
                            ctx.ones_row_bf[0:1, 0:128],
                            borow[0:1, n * 512:(n + 1) * 512],
                            start=False, stop=True),
                         inc=(tns if n == 1 else None))
                P.do("vector", lambda e: e.tensor_tensor(
                        zb, ps_v, zb, ALU.add),
                     wait=[tns, dS.mark], inc=vec, drain=False)
                P.dma("sync", x1v[gidx][:, u], zb, dS, wait=[vec.mark])



def emit_p3(ctx):
    nc, P = ctx.nc, ctx.P
    dL, dG, dS, tns, vec, scl, pol = (ctx.dL, ctx.dG, ctx.dS, ctx.tns, ctx.vec,
                                      ctx.scl, ctx.pol)

    xnfm3 = ctx.salloc("p3_xnfm", [128, 8 * NT], BF16)
    accy = ctx.salloc("p3_acc", [128, NCH * 1024], BF16)
    w1e = ctx.salloc("p3_w1e", [128, 2 * 8 * 256], BF16)
    w2e = ctx.salloc("p3_w2e", [128, 2 * 2 * 1024], BF16)
    b1e = ctx.salloc("p3_b1e", [1, 2 * 256], BF16)
    b2e = ctx.salloc("p3_b2e", [1, 2 * 1024], BF16)
    lgall = ctx.salloc("p3_lg", [128, NCH * 32], F32)
    rmaxall = ctx.salloc("p3_rmax", [128, NCH], F32)
    xt = ctx.salloc("p3_xt", [128, 2 * 1024], F32)
    z3 = ctx.salloc("p3_z3", [128, 1024], F32)
    zfm = ctx.salloc("p3_zfm", [128, 8 * 128], F32)
    hbf = ctx.salloc("p3_h", [128, 2 * 512], BF16)
    t2b = ctx.salloc("p3_t2", [128, 2 * 1024], BF16)
    mcol = ctx.salloc("p3_mcol", [128, 2], F32)
    outb = ctx.salloc("p3_out", [128, 1024], F32)

    xAv = ctx.streams["xA"].rearrange("(n p) d -> n p d", p=128)
    xBv = ctx.streams["xB"].rearrange("(n p) d -> n p d", p=128)

    ctx.push_scope()
    rwsb = ctx.salloc("p3_rw", [32, 1024], F32)
    rwT = ctx.salloc("p3_rwT", [128, 8 * 32], F32)
    rbrow = ctx.salloc("p3_rb", [1, 32], F32)
    ps_t = [ctx.palloc("p3_pst0", [128, 128]), ctx.palloc("p3_pst1", [128, 128])]
    ps_lg = ctx.palloc("p3_pslg", [128, 32])

    P.dma("sync", rwsb, ctx.I["L0_ffn_router_w"], dL)
    P.dma("sync", rbrow, ctx.I["L0_ffn_router_b"].rearrange("(o d) -> o d", o=1), dL)
    rwl = dL.mark
    for c in range(8):
        P.do("tensor", lambda e, c=c: e.transpose(
                ctx.ps_misc[:, 0:32], rwsb[:, c * 128:(c + 1) * 128],
                ctx.id_f32[0:32, 0:32]),
             wait=[rwl, ctx.consts_ready, vec.mark], inc=tns)
        P.do("vector", lambda e, c=c: e.tensor_copy(rwT[:, c * 32:(c + 1) * 32],
                                                    ctx.ps_misc[:, 0:32]),
             wait=[tns], inc=vec, drain=True)

    for ci in range(NCH):
        xs = xt[:, (ci % 2) * 1024:(ci % 2 + 1) * 1024]
        P.dma("sync", xs, xAv[ci], dL, wait=[scl.mark, vec.mark])
        ld = dL.mark
        emit_ln(ctx, xs, ctx.rstd, src_wait=[ld], dump=z3)
        P.do("vector", lambda e, xs=xs: e.tensor_scalar(
                z3, xs, ctx.meancol, ctx.rstd, ALU.subtract, ALU.mult),
             wait=[scl], inc=vec, drain=True)
        for c in range(8):
            par = c % 2
            P.do("tensor", lambda e, c=c, par=par: e.transpose(
                    ps_t[par], z3[:, c * 128:(c + 1) * 128], ctx.id_f32),
                 wait=[vec.mark, scl.mark], inc=tns)
            P.do("vector", lambda e, c=c, par=par: e.tensor_copy(
                    zfm[:, c * 128:(c + 1) * 128], ps_t[par]),
                 wait=[tns.mark], inc=vec, drain=False)
            P.do("vector", lambda e, c=c, ci=ci, par=par: e.tensor_copy(
                    xnfm3[:, c * NT + ci * 128: c * NT + (ci + 1) * 128],
                    ps_t[par]),
                 wait=[tns.mark], inc=vec, drain=False)
        for c in range(8):
            P.do("tensor", lambda e, c=c: e.matmul(
                    ps_lg, zfm[:, c * 128:(c + 1) * 128],
                    rwT[:, c * 32:(c + 1) * 32], start=(c == 0), stop=False),
                 wait=([vec.mark] if c == 0 else []))
        P.do("tensor", lambda e: e.matmul(
                ps_lg, ctx.ones_f32row[0:1, 0:128], rbrow,
                start=False, stop=True), wait=[rwl], inc=tns)
        P.do("vector", lambda e, ci=ci: e.tensor_copy(lgall[:, ci * 32:(ci + 1) * 32],
                                                      ps_lg),
             wait=[tns], inc=vec, drain=True)
        P.do("vector", lambda e, ci=ci: e.tensor_reduce(
                rmaxall[:, ci:ci + 1], lgall[:, ci * 32:(ci + 1) * 32],
                AX.X, ALU.max), inc=vec, drain=True)
    ctx.phase_end()
    ctx.pop_scope()

    ps_h = [ctx.palloc("p3_psh0", [128, 512]), ctx.palloc("p3_psh1", [128, 512])]
    ps_o = [ctx.palloc("p3_pso0", [128, 1024]), ctx.palloc("p3_pso1", [128, 1024])]
    w1v = ctx.wsc["L0_ffn_weights1"].rearrange("e (c p) h -> e p c h", p=128)
    w2v = ctx.wsc["L0_ffn_weights2"].rearrange("e (g p) d -> e p g d", p=128)

    def emit_eloads(e_i):
        sl = e_i % 2
        wt = [ctx.castdone["L0_ffn_weights1"], ctx.castdone["L0_ffn_weights2"],
              tns.mark]
        P.dma("sync", w1e.rearrange("p (s c h) -> p s c h", s=2, c=8)[:, sl],
              w1v[e_i], dL, wait=wt)
        P.dma("sync", w2e.rearrange("p (s g d) -> p s g d", s=2, g=2)[:, sl],
              w2v[e_i], dL, wait=wt)
        P.dma("gpsimd", b1e[0:1, sl * 256:(sl + 1) * 256],
              ctx.I["L0_ffn_bias1"][e_i].rearrange("(o h) -> o h", o=1), dG,
              wait=[tns.mark])
        P.dma("gpsimd", b2e[0:1, sl * 1024:(sl + 1) * 1024],
              ctx.I["L0_ffn_bias2"][e_i].rearrange("(o h) -> o h", o=1), dG,
              wait=[tns.mark])
        return (dL.mark, dG.mark)

    emarks = {0: emit_eloads(0)}
    omark = [vec.mark, vec.mark]
    t2mark = [pol.mark, pol.mark]
    for e_i in range(P3_NE):
        sl = e_i % 2
        wld = emarks[e_i]
        hmark = [vec.mark, vec.mark]
        for tgg in range(NTG):
            if tgg == NTG // 2 and e_i + 1 < P3_NE:
                emarks[e_i + 1] = emit_eloads(e_i + 1)
            for g in range(2):
                pb = ps_h[g]
                for c in range(8):
                    P.do("tensor", lambda e, c=c, g=g, sl=sl, tgg=tgg, pb=pb: e.matmul(
                            pb, w1e[:, sl * 2048 + c * 256 + g * 128:
                                    sl * 2048 + c * 256 + (g + 1) * 128],
                            xnfm3[:, c * NT + tgg * TG:(c * NT + (tgg + 1) * TG)],
                            start=(c == 0), stop=False),
                         wait=([wld[0], wld[1], hmark[g]] if c == 0 else []))
                P.do("tensor", lambda e, g=g, sl=sl, pb=pb: e.matmul(
                        pb, b1e[0:1, sl * 256 + g * 128: sl * 256 + (g + 1) * 128],
                        ctx.ones_row_bf[0:1, 0:TG], start=False, stop=True),
                     inc=tns)
                P.do("vector", lambda e, g=g, pb=pb: e.tensor_scalar(
                        hbf[:, g * 512:(g + 1) * 512], pb, 0.0, None, ALU.max),
                     wait=[tns.mark], inc=vec, drain=False)
                hmark[g] = vec.mark
            hready = vec.mark
            for u in range(CPG):
                ci = tgg * CPG + u
                po = ps_o[u % 2]
                for n in range(2):
                    for g in range(2):
                        P.do("tensor", lambda e, u=u, n=n, g=g, sl=sl, po=po: e.matmul(
                                po[:, n * 512:(n + 1) * 512],
                                hbf[:, g * 512 + u * 128: g * 512 + (u + 1) * 128],
                                w2e[:, sl * 2048 + g * 1024 + n * 512:
                                    sl * 2048 + g * 1024 + (n + 1) * 512],
                                start=(g == 0), stop=False),
                             wait=([hready, omark[u % 2]] if (n == 0 and g == 0)
                                   else []))
                    P.do("tensor", lambda e, n=n, sl=sl, po=po: e.matmul(
                            po[:, n * 512:(n + 1) * 512],
                            ctx.ones_row_bf[0:1, 0:128],
                            b2e[0:1, sl * 1024 + n * 512: sl * 1024 + (n + 1) * 512],
                            start=False, stop=True),
                         inc=(tns if n == 1 else None))
                mc = mcol[:, (u % 2):(u % 2) + 1]
                P.do("vector", lambda e, ci=ci, e_i=e_i, mc=mc: e.tensor_scalar(
                        mc, lgall[:, ci * 32 + e_i: ci * 32 + e_i + 1],
                        rmaxall[:, ci:ci + 1], None, ALU.is_ge),
                     wait=[scl.mark], inc=vec, drain=True)
                t2s = t2b[:, (u % 2) * 1024:(u % 2 + 1) * 1024]
                P.do("scalar", lambda e, po=po, t2s=t2s, mc=mc: e.activation(
                        t2s, po, AF.Relu, scale=mc),
                     wait=[tns.mark, vec.mark, t2mark[u % 2]], inc=scl, drain=False)
                omark[u % 2] = scl.mark
                asl = accy[:, ci * 1024:(ci + 1) * 1024]
                if e_i == 0:
                    P.do("gpsimd", lambda e, asl=asl, t2s=t2s: e.tensor_copy(asl, t2s),
                         wait=[scl.mark], inc=pol, drain=True)
                else:
                    P.do("gpsimd", lambda e, asl=asl, t2s=t2s: e.tensor_tensor(
                            asl, asl, t2s, ALU.add),
                         wait=[scl.mark], inc=pol, drain=True)
                t2mark[u % 2] = pol.mark

    for ci in range(NCH):
        xs = xt[:, (ci % 2) * 1024:(ci % 2 + 1) * 1024]
        P.dma("sync", xs, xAv[ci], dL, wait=[vec.mark])
        P.do("vector", lambda e, ci=ci, xs=xs: e.tensor_tensor(
                outb, xs, accy[:, ci * 1024:(ci + 1) * 1024], ALU.add),
             wait=[dL.mark, pol.mark, dS.mark], inc=vec, drain=False)
        P.dma("sync", xBv[ci], outb, dS, wait=[vec.mark])



def emit_p4(ctx):
    nc, P = ctx.nc, ctx.P
    dL, dG, dS, tns, vec, scl, pol = (ctx.dL, ctx.dG, ctx.dS, ctx.tns, ctx.vec,
                                      ctx.scl, ctx.pol)

    waT = {W: ctx.salloc(f"p4_waT_{W}", [128, 8 * 1024], BF16) for W in "qkv"}
    wsqm = {W: ctx.salloc(f"p4_wsqm_{W}", [1, 1024], BF16) for W in "qkv"}
    qfm4 = ctx.salloc("p4_qfm", [128, 8 * SEQ], BF16)
    kfm4 = ctx.salloc("p4_kfm", [128, 8 * SEQ], BF16)
    vtm4 = ctx.salloc("p4_vtm", [128, 16 * 1040], BF16)
    ksqb = ctx.salloc("p4_ksqb", [128, 16 * 16], F32)
    cmcol = ctx.salloc("p4_cm", [64, 1], BF16)
    P.do("gpsimd", lambda e: e.memset(cmcol, -0.5), drain=True)
    P.do("gpsimd", lambda e: e.memset(vtm4, 1.0), inc=pol, drain=True)
    cm_ready = pol.mark

    wa_dram = {W: nc.dram_tensor(f"p4_wa_{W}", [D, D], BF16, kind="Internal").ap()
               for W in "qkv"}

    ctx.push_scope()
    stage = ctx.salloc("p4_stage", [128, 4 * 1024], F32)
    wabf = ctx.salloc("p4_wabf", [128, 8 * 1024], BF16)
    ps_row = ctx.palloc("p4_psrow0", [1, 512])
    Wmap = {"q": "L1_attn_Wq", "k": "L1_attn_Wk", "v": "L1_attn_Wv"}
    Gmap = {"q": "L1_attn_gq", "k": "L1_attn_gk", "v": "L1_attn_gv"}
    for W in "qkv":
        for half in range(2):
            P.dma("sync", stage.rearrange("p (c f) -> p c f", c=4),
                  ctx.I[Wmap[W]].rearrange("(c p) f -> p c f", p=128)[
                      :, half * 4:(half + 1) * 4], dL,
                  wait=[scl.mark])
            P.do("scalar", lambda e, half=half: e.activation(
                    wabf[:, half * 4096:(half + 1) * 4096], stage, AF.Tanh),
                 wait=[dL], inc=scl, drain=True)
        P.dma("sync", wa_dram[W].rearrange("(c p) f -> p c f", p=128),
              wabf.rearrange("p (c f) -> p c f", c=8), dS, wait=[scl.mark])
        for c in range(8):
            P.do("sync", lambda e, W=W, c=c: e.dma_start_transpose(
                    waT[W][:, c * 1024:(c + 1) * 1024],
                    wa_dram[W][:, c * 128:(c + 1) * 128]),
                 wait=([dS.mark] if c == 0 else []), inc=[(dL, 16)])
        P.do("scalar", lambda e, W=W: e.activation(wabf, waT[W], AF.Square),
             wait=[dL], inc=scl, drain=True)
        for n in range(2):
            for c in range(8):
                P.do("tensor", lambda e, n=n, c=c: e.matmul(
                        ps_row, ctx.ones_col_bf,
                        wabf[:, c * 1024 + n * 512: c * 1024 + (n + 1) * 512],
                        start=(c == 0), stop=(c == 7)),
                     wait=([scl.mark, vec.mark, ctx.consts_ready] if c == 0 else []),
                     inc=(tns if c == 7 else None))
            P.do("vector", lambda e, W=W, n=n: e.tensor_scalar(
                    wsqm[W][0:1, n * 512:(n + 1) * 512], ps_row, -0.5, None,
                    ALU.mult),
                 wait=[tns], inc=vec, drain=True)
    prol_done = (dL.mark, vec.mark, scl.mark)
    ctx.phase_end()
    ctx.pop_scope()

    xBv = ctx.streams["xB"].rearrange("(n p) d -> n p d", p=128)
    xCv = ctx.streams["xC"].rearrange("(n p) d -> n p d", p=128)

    for s in range(2):
        ctx.push_scope()
        xt = ctx.salloc(f"p4_xt{s}", [128, 1024], F32)
        zb = ctx.salloc(f"p4_zb{s}", [128, 1024], F32)
        xaf = ctx.salloc(f"p4_xaf{s}", [128, 1024], F32)
        xabf = ctx.salloc(f"p4_xabf{s}", [128, 1024], BF16)
        xafm = ctx.salloc(f"p4_xafm{s}", [128, 2 * 1024], BF16)
        Ub = ctx.salloc(f"p4_U{s}", [128, 1024], F32)
        xsqc = ctx.salloc(f"p4_xsq{s}", [128, 2], F32)
        qtmb = ctx.salloc(f"p4_qtm{s}", [128, 1024], BF16)
        ktmb = ctx.salloc(f"p4_ktm{s}", [128, 1024], BF16)
        ksqraw = ctx.salloc(f"p4_ksqr{s}", [128, 16], F32)
        G1 = {W: ctx.salloc(f"p4s_G1_{W}{s}", [128, 1024], BF16) for W in "qkv"}
        G2 = {W: ctx.salloc(f"p4s_G2_{W}{s}", [128, 1024], BF16) for W in "qkv"}
        ps_tb = ctx.palloc(f"p4_pstb{s}", [128, 256], BF16)
        ps_d = ctx.palloc(f"p4_psd{s}", [128, 1024])
        for W in "qkv":
            gflat = ctx.I[Gmap[W]].rearrange("a b d -> (a b d)")
            for half in range(2):
                P.dma("sync", ctx.rowbuf,
                      gflat[half * 512:(half + 1) * 512].rearrange("(o d) -> o d", o=1),
                      dL, wait=[tns.mark])
                P.do("tensor", lambda e: e.matmul(
                        ctx.ps_misc, ctx.ones_f32row[0:1, 0:128], ctx.rowbuf,
                        start=True, stop=True),
                     wait=[dL.mark, ctx.consts_ready, vec.mark], inc=tns)
                P.do("vector", lambda e, W=W, half=half: e.tensor_scalar(
                        G1[W][:, half * 512:(half + 1) * 512], ctx.ps_misc,
                        1.0 / 1024, None, ALU.mult),
                     wait=[tns], drain=True)
                P.do("vector", lambda e, W=W, half=half: e.tensor_scalar(
                        G2[W][:, half * 512:(half + 1) * 512], ctx.ps_misc,
                        0.5, None, ALU.mult),
                     inc=vec, drain=True)
        k2s = ctx.salloc(f"p4_k2s{s}", [128, 1024], F32)
        tstate = {"dmark": scl.mark, "vmark": vec.mark, "tail": None}

        def pdl_tail(t):
            ci, par, hm = t
            xafm_p = xafm[:, par * 1024:(par + 1) * 1024]
            xsq_p = xsqc[:, par:par + 1]
            for W in "qkv":
                for n in range(2):
                    for c in range(8):
                        P.do("tensor", lambda e, W=W, n=n, c=c, xafm_p=xafm_p: e.matmul(
                                ps_d[:, n * 512:(n + 1) * 512],
                                xafm_p[:, c * 128:(c + 1) * 128],
                                waT[W][:, c * 1024 + n * 512: c * 1024 + (n + 1) * 512],
                                start=(c == 0), stop=False),
                             wait=([hm[0], hm[1], tstate["dmark"]] + list(prol_done)
                                   if (n == 0 and c == 0) else []))
                    P.do("tensor", lambda e, W=W, n=n: e.matmul(
                            ps_d[:, n * 512:(n + 1) * 512],
                            ctx.ones_row_bf[0:1, 0:128],
                            wsqm[W][0:1, n * 512:(n + 1) * 512],
                            start=False, stop=True),
                         inc=(tns if n == 1 else None))
                P.do("scalar", lambda e, xsq_p=xsq_p: e.activation(
                        Ub, ps_d, AF.Relu, bias=xsq_p, scale=-2.0),
                     wait=[tns, tstate["vmark"]], inc=scl, drain=True)
                tstate["dmark"] = scl.mark
                P.do("vector", lambda e, W=W: e.tensor_tensor(
                        Ub, Ub, G1[W], ALU.mult),
                     wait=[scl], drain=True)
                P.do("vector", lambda e, W=W: e.tensor_tensor(
                        Ub, G2[W], Ub, ALU.subtract),
                     inc=vec, drain=True)
                tstate["vmark"] = vec.mark
                if W == "v":
                    P.do("vector", lambda e, ci=ci: e.tensor_copy(
                            vtm4[:, ci * 1040:ci * 1040 + 1040].rearrange(
                                "p (h y) -> p h y", h=16)[:, :, 0:64],
                            Ub.rearrange("p (h y) -> p h y", h=16)),
                         wait=[cm_ready], inc=vec, drain=True)
                    tstate["vmark"] = vec.mark
                else:
                    dst = qtmb if W == "q" else ktmb
                    P.do("scalar", lambda e, dst=dst: e.activation(dst, Ub, AF.Tanh),
                         wait=[vec], inc=scl, drain=True)
            P.do("scalar", lambda e: e.activation(k2s, ktmb, AF.Square),
                 wait=[scl.mark], inc=scl, drain=True)
            P.do("vector", lambda e: e.tensor_reduce(
                    ksqraw, k2s.rearrange("p (h x) -> p h x", h=16), AX.X, ALU.add),
                 wait=[scl], drain=True)
            P.do("vector", lambda e, ci=ci: e.tensor_scalar(
                    ksqb[:, ci * 16:(ci + 1) * 16], ksqraw, -1.0 / 80.0, None,
                    ALU.mult),
                 inc=vec, drain=True)
            for src_b, dstb in ((qtmb, qfm4), (ktmb, kfm4)):
                for c in range(8):
                    P.do("tensor", lambda e, src_b=src_b, c=c: e.transpose(
                            ps_tb[:, (c % 2) * 128:(c % 2 + 1) * 128],
                            src_b[:, c * 128:(c + 1) * 128], ctx.id_bf),
                         wait=[vec.mark, scl.mark], inc=tns)
                    P.do("vector", lambda e, dstb=dstb, c=c, ci=ci: e.tensor_copy(
                            dstb[:, c * SEQ + ci * 128: c * SEQ + (ci + 1) * 128],
                            ps_tb[:, (c % 2) * 128:(c % 2 + 1) * 128]),
                         wait=[tns.mark], inc=vec, drain=True)

        for ci in range(16):
            gi = s * 16 + ci
            par = ci % 2
            P.dma("sync", xt, xBv[gi], dL, wait=[scl.mark, vec.mark])
            ld = dL.mark
            emit_ln(ctx, xt, ctx.rstd, src_wait=[ld], dump=zb)
            P.do("vector", lambda e: e.tensor_scalar(
                    zb, xt, ctx.meancol, ctx.rstd, ALU.subtract, ALU.mult),
                 wait=[scl], inc=vec, drain=True)
            P.do("scalar", lambda e: e.activation(xaf, zb, AF.Tanh),
                 wait=[vec], inc=scl, drain=True)
            P.do("scalar", lambda e, par=par: e.activation(zb, xaf, AF.Square,
                                                  accum_out=xsqc[:, par:par + 1]),
                 inc=scl, drain=True)
            P.do("vector", lambda e: e.tensor_copy(xabf, xaf),
                 wait=[scl.mark], inc=vec, drain=True)
            for c in range(8):
                P.do("tensor", lambda e, c=c: e.transpose(
                        ps_tb[:, (c % 2) * 128:(c % 2 + 1) * 128],
                        xabf[:, c * 128:(c + 1) * 128], ctx.id_bf),
                     wait=[vec.mark, scl.mark], inc=tns)
                P.do("vector", lambda e, c=c, par=par: e.tensor_copy(
                        xafm[:, par * 1024 + c * 128: par * 1024 + (c + 1) * 128],
                        ps_tb[:, (c % 2) * 128:(c % 2 + 1) * 128]),
                     wait=[tns.mark], inc=vec, drain=True)
            hm = (vec.mark, scl.mark)
            if tstate["tail"] is not None:
                pdl_tail(tstate["tail"])
            tstate["tail"] = (ci, par, hm)
        if tstate["tail"] is not None:
            pdl_tail(tstate["tail"])
            tstate["tail"] = None
        ctx.phase_end()
        ctx.pop_scope()

        ctx.push_scope()
        woT = ctx.salloc(f"p4_woT{s}", [128, 8 * 1024], BF16)
        pbuf = ctx.salloc(f"p4_pbuf{s}", [128, 16 * 512], BF16)
        ofm = ctx.salloc(f"p4_ofm{s}", [128, 8 * 512], BF16)
        q2b = ctx.salloc(f"p4_q2b{s}", [64, 512], BF16)
        qsqrow = ctx.salloc(f"p4_qsqr{s}", [1, 512], BF16)
        denrow = ctx.salloc(f"p4_den{s}", [1, 512], F32)
        rcpden = denrow
        rbsb = ctx.salloc(f"p4_rb{s}", [64, 512], F32)
        xt2 = ctx.salloc(f"p4_xt2{s}", [128, 1024], F32)
        ps_s = [ctx.palloc(f"p4_pss0{s}", [128, 512]), ctx.palloc(f"p4_pss1{s}", [128, 512])]
        ps_dn = ctx.palloc(f"p4_psdn{s}", [1, 512])
        ps_rb = ctx.palloc(f"p4_psrb{s}", [64, 512])
        ps_pv = ctx.palloc(f"p4_pspv{s}", [65, 512])
        ps_y = ctx.palloc(f"p4_psy{s}", [128, 1024])
        for c in range(8):
            P.do("sync", lambda e, c=c: e.dma_start_transpose(
                    woT[:, c * 1024:(c + 1) * 1024],
                    ctx.wsc["L1_attn_Wo"][:, c * 128:(c + 1) * 128]),
                 wait=([ctx.castdone["L1_attn_Wo"], tns.mark] if c == 0 else []),
                 inc=[(dL, 16)])
        wol = dL.mark
        for qtg in range(4):
            for h in range(NH):
                hp = slice((h % 2) * 64, (h % 2) * 64 + 64)
                fc = h // 2
                qsl = slice(fc * SEQ + qtg * 512, fc * SEQ + (qtg + 1) * 512)
                P.do("scalar", lambda e, hp=hp, qsl=qsl: e.activation(
                        q2b, qfm4[hp, qsl], AF.Square),
                     wait=[vec.mark, tns.mark], inc=scl, drain=True)
                P.do("tensor", lambda e: e.matmul(
                        ps_dn, cmcol, q2b, start=True, stop=True),
                     wait=[scl.mark, vec.mark, cm_ready], inc=tns)
                P.do("scalar", lambda e: e.activation(qsqrow, ps_dn, AF.Copy),
                     wait=[tns.mark], inc=scl, drain=True)
                smark = [scl.mark, scl.mark]
                for si in range(16):
                    pp = ps_s[si % 2]
                    P.do("tensor", lambda e, hp=hp, qsl=qsl, si=si, pp=pp, fc=fc:
                         e.matmul(pp, kfm4[hp, fc * SEQ + si * 128:
                                           fc * SEQ + (si + 1) * 128],
                                  qfm4[hp, qsl], start=True, stop=False),
                         wait=[smark[si % 2]])
                    P.do("tensor", lambda e, si=si, pp=pp: e.matmul(
                            pp, ctx.ones_row_bf[0:1, 0:128], qsqrow,
                            start=False, stop=True), inc=tns)
                    P.do("scalar", lambda e, si=si, h=h, pp=pp: e.activation(
                            pbuf[:, si * 512:(si + 1) * 512], pp, AF.Exp,
                            scale=0.025, bias=ksqb[:, si * 16 + h: si * 16 + h + 1]),
                         wait=[tns.mark], inc=scl, drain=False)
                    smark[si % 2] = scl.mark
                for si in range(16):
                    P.do("tensor", lambda e, si=si, h=h: e.matmul(
                            ps_pv, vtm4[:, si * 1040 + h * 65: si * 1040 + h * 65 + 65],
                            pbuf[:, si * 512:(si + 1) * 512],
                            start=(si == 0), stop=(si == 15)),
                         wait=([vec.mark, scl.mark] if si == 0 else []),
                         inc=(tns if si == 15 else None))
                P.do("vector", lambda e: e.tensor_copy(denrow, ps_pv[64:65, :]),
                     wait=[tns], inc=vec, drain=True)
                P.do("scalar", lambda e: e.activation(denrow, denrow, AF.Ln),
                     wait=[vec], drain=True)
                P.do("scalar", lambda e: e.activation(denrow, denrow, AF.Exp,
                                                      scale=-1.0),
                     inc=scl, drain=True)
                P.do("tensor", lambda e: e.matmul(
                        ps_rb, ctx.ones_f32row[0:1, 0:64], rcpden,
                        start=True, stop=True),
                     wait=[scl.mark], inc=tns)
                P.do("vector", lambda e: e.tensor_copy(rbsb, ps_rb),
                     wait=[tns], inc=vec, drain=True)
                P.do("vector", lambda e, hp=hp, fc=fc: e.tensor_tensor(
                        ofm[hp, fc * 512:(fc + 1) * 512], ps_pv[0:64, :], rbsb,
                        ALU.mult),
                     wait=[vec.mark], inc=vec, drain=True)
            for u in range(CPG):
                gi = s * 16 + qtg * CPG + u
                P.dma("sync", xt2, xBv[gi], dL, wait=[vec.mark])
                xld = dL.mark
                for n in range(2):
                    for c in range(8):
                        P.do("tensor", lambda e, u=u, n=n, c=c: e.matmul(
                                ps_y[:, n * 512:(n + 1) * 512],
                                ofm[:, c * 512 + u * 128: c * 512 + (u + 1) * 128],
                                woT[:, c * 1024 + n * 512: c * 1024 + (n + 1) * 512],
                                start=(c == 0), stop=(c == 7)),
                             wait=([vec.mark, wol] if (n == 0 and c == 0) else []),
                             inc=(tns if (n == 1 and c == 7) else None))
                P.do("vector", lambda e: e.tensor_tensor(xt2, ps_y, xt2, ALU.add),
                     wait=[tns, xld, dS.mark], inc=vec, drain=False)
                P.dma("sync", xCv[gi], xt2, dS, wait=[vec.mark])
        ctx.phase_end()
        ctx.pop_scope()



QT = 1024
NQ = NT // QT
CPQ = QT // 128


def emit_p5(ctx):
    nc, P = ctx.nc, ctx.P
    dL, dG, dS, dT, tns, vec, scl, pol = (ctx.dL, ctx.dG, ctx.dS, ctx.dT,
                                          ctx.tns, ctx.vec, ctx.scl, ctx.pol)

    wts = ctx.salloc("p5_wts", [128, NCH * 8], F32)
    xn5sc = nc.dram_tensor("p5_xnsc", [NCH, 128, 8, 128], BF16, kind="Internal").ap()

    ctx.push_scope()
    wgsb = ctx.salloc("p5_wg", [8, 1024], F32)
    wgT = ctx.salloc("p5_wgT", [128, 8 * 8], F32)
    bgrow = ctx.salloc("p5_bg", [1, 8], F32)
    scb = ctx.salloc("p5_sc", [128, 8], F32)
    m1 = ctx.salloc("p5_m1", [128, 1], F32)
    m2 = ctx.salloc("p5_m2", [128, 1], F32)
    ltm = ctx.salloc("p5_ltm", [128, 8], F32)
    s2b = ctx.salloc("p5_s2", [128, 8], F32)
    xt = ctx.salloc("p5_xt", [128, 1024], F32)
    z5 = ctx.salloc("p5_z5", [128, 1024], F32)
    zfm = ctx.salloc("p5_zfm", [128, 1024], F32)
    xnb = ctx.salloc("p5_xnb", [128, 8 * 128], BF16)
    ps_t = [ctx.palloc("p5_pst0", [128, 128]), ctx.palloc("p5_pst1", [128, 128])]
    ps_g = ctx.palloc("p5_psg", [128, 8])

    P.dma("sync", wgsb, ctx.I["L1_ffn_Wg"], dL)
    gl = dL.mark
    for c in range(8):
        P.do("tensor", lambda e, c=c: e.transpose(
                ctx.ps_misc[:, 0:8], wgsb[:, c * 128:(c + 1) * 128],
                ctx.id_f32[0:8, 0:8]),
             wait=[gl, ctx.consts_ready, vec.mark], inc=tns)
        P.do("vector", lambda e, c=c: e.tensor_copy(wgT[:, c * 8:(c + 1) * 8],
                                                    ctx.ps_misc[:, 0:8]),
             wait=[tns], inc=vec, drain=True)
    P.dma("sync", ctx.rowbuf[0:1, 0:8],
          ctx.I["L1_ffn_bg"].rearrange("(o d) -> o d", o=1), dL, wait=[tns.mark])
    P.dma("sync", ctx.rowbuf[0:1, 8:16],
          ctx.I["L1_ffn_bias"].rearrange("(o d) -> o d", o=1), dL)
    P.do("vector", lambda e: e.tensor_tensor(bgrow, ctx.rowbuf[0:1, 0:8],
                                             ctx.rowbuf[0:1, 8:16], ALU.add),
         wait=[dL], inc=vec, drain=True)
    gate_ready = (vec.mark, tns.mark)

    xCv = ctx.streams["xC"].rearrange("(n p) d -> n p d", p=128)
    outv = ctx.streams["out"].rearrange("(n p) d -> n p d", p=128)

    for ci in range(NCH):
        P.dma("sync", xt, xCv[ci], dL, wait=[scl.mark, vec.mark])
        ld = dL.mark
        emit_ln(ctx, xt, ctx.rstd, src_wait=[ld], dump=z5)
        P.do("vector", lambda e: e.tensor_scalar(
                z5, xt, ctx.meancol, ctx.rstd, ALU.subtract, ALU.mult),
             wait=[scl], inc=vec, drain=True)
        for c in range(8):
            par = c % 2
            P.do("tensor", lambda e, c=c, par=par: e.transpose(
                    ps_t[par], z5[:, c * 128:(c + 1) * 128], ctx.id_f32),
                 wait=[vec.mark, dS.mark], inc=tns)
            P.do("vector", lambda e, c=c, par=par: e.tensor_copy(
                    zfm[:, c * 128:(c + 1) * 128], ps_t[par]),
                 wait=[tns.mark], inc=vec, drain=False)
            P.do("vector", lambda e, c=c, par=par: e.tensor_copy(
                    xnb[:, c * 128:(c + 1) * 128], ps_t[par]),
                 wait=[tns.mark, dS.mark], inc=vec, drain=True)
        P.dma("sync", xn5sc[ci], xnb.rearrange("p (c t) -> p c t", c=8), dS,
              wait=[vec.mark])
        for c in range(8):
            P.do("tensor", lambda e, c=c: e.matmul(
                    ps_g, zfm[:, c * 128:(c + 1) * 128],
                    wgT[:, c * 8:(c + 1) * 8], start=(c == 0), stop=False),
                 wait=([vec.mark, gate_ready[0], gate_ready[1]] if c == 0 else []))
        P.do("tensor", lambda e: e.matmul(
                ps_g, ctx.ones_f32row[0:1, 0:128], bgrow,
                start=False, stop=True), inc=tns)
        P.do("scalar", lambda e: e.activation(scb, ps_g, AF.Sigmoid),
             wait=[tns.mark, vec.mark], inc=scl, drain=True)
        P.do("vector", lambda e: e.tensor_reduce(m1, scb, AX.X, ALU.max),
             wait=[scl], drain=True)
        P.do("vector", lambda e: e.tensor_scalar(ltm, scb, m1, None, ALU.is_lt),
             drain=True)
        P.do("vector", lambda e: e.tensor_tensor(s2b, scb, ltm, ALU.mult),
             drain=True)
        P.do("vector", lambda e: e.tensor_reduce(m2, s2b, AX.X, ALU.max),
             drain=True)
        P.do("vector", lambda e: e.tensor_scalar(ltm, scb, m2, None, ALU.is_ge),
             drain=True)
        P.do("vector", lambda e, ci=ci: e.tensor_tensor(
                wts[:, ci * 8:(ci + 1) * 8], scb, ltm, ALU.mult),
             inc=vec, drain=True)
    ctx.phase_end()
    ctx.pop_scope()

    xnfm5 = ctx.salloc("p5_xnfm", [128, 8 * QT], BF16)
    w1T = ctx.salloc("p5_w1T", [128, 8 * 4096], BF16)
    w2T = ctx.salloc("p5_w2T", [128, 32 * 1024], BF16)
    h5 = ctx.salloc("p5_h", [128, 32 * 512], BF16)
    acc5 = ctx.salloc("p5_acc", [128, CPQ * 1024], BF16)
    b1cols = ctx.salloc("p5_b1c", [128, 32], F32)
    b2row = ctx.salloc("p5_b2r", [1, 1024], BF16)
    t2b = ctx.salloc("p5_t2", [128, 2 * 1024], BF16)
    xt2 = ctx.salloc("p5_xt2", [128, 1024], F32)
    outb = xt2
    ps_h = [ctx.palloc("p5_psh0", [128, 512]), ctx.palloc("p5_psh1", [128, 512])]
    ps_o = [ctx.palloc("p5_pso0", [128, 1024]), ctx.palloc("p5_pso1", [128, 1024])]

    w1names = ["L1_ffn_W1"] * 8 + ["L1_ffn_Ws1"]
    w2names = ["L1_ffn_W2"] * 8 + ["L1_ffn_Ws2"]

    def w_src(names, e_i):
        shared = (e_i == 8)
        return ctx.wsc[names[e_i]] if shared else ctx.wsc[names[e_i]][e_i]

    def emit_w1_loads(e_i):
        s_ap = w_src(w1names, e_i)
        for c in range(8):
            P.do("sync", lambda e, c=c, s_ap=s_ap: e.dma_start_transpose(
                    w1T[:, c * 4096:(c + 1) * 4096],
                    s_ap[:, c * 128:(c + 1) * 128]),
                 wait=([ctx.castdone[w1names[e_i]], tns.mark] if c == 0 else []),
                 inc=[(dL, 16)])
        return dL.mark

    def emit_w2_loads(e_i):
        s_ap = w_src(w2names, e_i)
        for gc in range(32):
            P.do("sync", lambda e, gc=gc, s_ap=s_ap: e.dma_start_transpose(
                    w2T[:, gc * 1024:(gc + 1) * 1024],
                    s_ap[:, gc * 128:(gc + 1) * 128]),
                 wait=([ctx.castdone[w2names[e_i]], tns.mark] if gc == 0 else []),
                 inc=[(dL, 16)])
        return dL.mark

    def emit_bias_loads(e_i):
        shared = (e_i == 8)
        b1vec = (ctx.I["L1_ffn_bs1"] if shared else ctx.I["L1_ffn_b1"][e_i])
        b2vec = (ctx.I["L1_ffn_bs2"] if shared else ctx.I["L1_ffn_b2"][e_i])
        load_cols(ctx, b1vec, b1cols, 32)
        P.dma("gpsimd", b2row, b2vec.rearrange("(o d) -> o d", o=1), dG,
              wait=[tns.mark])
        return (vec.mark, dG.mark)

    insts = [(q, e_i) for q in range(NQ) for e_i in range(P5_NE)]
    nmarks = {}
    if insts:
        nmarks[insts[0]] = (emit_w1_loads(insts[0][1]), emit_w2_loads(insts[0][1]),
                            emit_bias_loads(insts[0][1]))
    omark = [vec.mark, vec.mark]
    lastq = [-1]
    for idx, (q, e_i) in enumerate(insts):
        if q != lastq[0]:
            lastq[0] = q
            P.dma("sync",
                  xnfm5.rearrange("p (c cq t) -> p c cq t", c=8, cq=CPQ),
                  xn5sc[q * CPQ:(q + 1) * CPQ].rearrange("cq p c t -> p c cq t"),
                  dL, wait=[tns.mark])
        xnld = dL.mark
        w1m, w2m, bm = nmarks[(q, e_i)]
        nxt = insts[idx + 1] if idx + 1 < len(insts) else None
        hmark = [scl.mark, scl.mark]
        n1 = n2 = None
        for tg2 in range(2):
            for j in range(32):
                pb = ps_h[j % 2]
                for c in range(8):
                    P.do("tensor", lambda e, j=j, c=c, tg2=tg2, pb=pb: e.matmul(
                            pb, w1T[:, c * 4096 + j * 128: c * 4096 + (j + 1) * 128],
                            xnfm5[:, c * QT + tg2 * 512: c * QT + (tg2 + 1) * 512],
                            start=(c == 0), stop=(c == 7)),
                         wait=([w1m, bm[0], bm[1], xnld, hmark[j % 2]]
                               if c == 0 else []),
                         inc=(tns if c == 7 else None))
                P.do("scalar", lambda e, j=j, pb=pb: e.activation(
                        h5[:, j * 512:(j + 1) * 512], pb, AF.Gelu,
                        bias=b1cols[:, j:j + 1]),
                     wait=[tns.mark], inc=scl, drain=False)
                hmark[j % 2] = scl.mark
            if tg2 == 1 and nxt is not None:
                n1 = emit_w1_loads(nxt[1])
            for u in range(CPG):
                cq = tg2 * CPG + u
                ci = q * CPQ + cq
                po = ps_o[u % 2]
                for n in range(2):
                    for j in range(32):
                        P.do("tensor", lambda e, u=u, n=n, j=j, po=po: e.matmul(
                                po[:, n * 512:(n + 1) * 512],
                                h5[:, j * 512 + u * 128: j * 512 + (u + 1) * 128],
                                w2T[:, j * 1024 + n * 512: j * 1024 + n * 512 + 512],
                                start=(j == 0), stop=False),
                             wait=([scl.mark, w2m, omark[u % 2]]
                                   if (n == 0 and j == 0) else []))
                    P.do("tensor", lambda e, n=n, po=po: e.matmul(
                            po[:, n * 512:(n + 1) * 512],
                            ctx.ones_row_bf[0:1, 0:128],
                            b2row[0:1, n * 512:(n + 1) * 512],
                            start=False, stop=True),
                         inc=(tns if n == 1 else None))
                t2s = t2b[:, (u % 2) * 1024:(u % 2 + 1) * 1024]
                if e_i == 8:
                    P.do("vector", lambda e, po=po, t2s=t2s: e.tensor_copy(t2s, po),
                         wait=[tns, pol.mark], inc=vec, drain=True)
                else:
                    P.do("vector", lambda e, po=po, t2s=t2s, ci=ci, e_i=e_i:
                         e.tensor_scalar(
                            t2s, po, wts[:, ci * 8 + e_i: ci * 8 + e_i + 1],
                            None, ALU.mult),
                         wait=[tns, pol.mark], inc=vec, drain=True)
                omark[u % 2] = vec.mark
                asl = acc5[:, cq * 1024:(cq + 1) * 1024]
                if e_i == 0:
                    P.do("gpsimd", lambda e, asl=asl, t2s=t2s: e.tensor_copy(asl, t2s),
                         wait=[vec.mark], inc=pol, drain=True)
                else:
                    P.do("gpsimd", lambda e, asl=asl, t2s=t2s: e.tensor_tensor(
                            asl, asl, t2s, ALU.add),
                         wait=[vec.mark], inc=pol, drain=True)
        if nxt is not None:
            n2 = emit_w2_loads(nxt[1])
            nb = emit_bias_loads(nxt[1])
            nmarks[nxt] = (n1, n2, nb)
        if e_i == P5_NE - 1:
            for cq in range(CPQ):
                ci = q * CPQ + cq
                P.dma("sync", xt2, xCv[ci], dL, wait=[vec.mark])
                P.do("vector", lambda e, cq=cq: e.tensor_tensor(
                        outb, xt2, acc5[:, cq * 1024:(cq + 1) * 1024], ALU.add),
                     wait=[dL, dS.mark, pol.mark], inc=vec, drain=False)
                P.dma("sync", outv[ci], outb, dS, wait=[vec.mark])



INPUT_SPECS = {
    "L0_attn_g1": (D,), "L0_attn_b1": (D,),
    "L0_attn_Wqkv_a": (D, 3 * D), "L0_attn_bqkv_a": (3 * D,),
    "L0_attn_Wp_a": (D, D), "L0_attn_bp_a": (D,),
    "L0_attn_Wqkv_s": (D, 3 * D), "L0_attn_bqkv_s": (3 * D,),
    "L0_attn_Wo_s": (D, D), "L0_attn_bo_s": (D,),
    "L0_attn_Wo": (D, D), "L0_attn_bo": (D,),
    "L0_attn_gm": (D,), "L0_attn_bm": (D,),
    "L0_attn_W1": (D, 4 * D), "L0_attn_b1m": (4 * D,),
    "L0_attn_W2": (4 * D, D), "L0_attn_b2m": (D,),
    "L0_ffn_router_w": (32, D), "L0_ffn_router_b": (32,),
    "L0_ffn_weights1": (32, D, 256), "L0_ffn_bias1": (32, 256),
    "L0_ffn_weights2": (32, 256, D), "L0_ffn_bias2": (32, D),
    "L1_attn_Wq": (D, D), "L1_attn_gq": (1, 1, D),
    "L1_attn_Wk": (D, D), "L1_attn_gk": (1, 1, D),
    "L1_attn_Wv": (D, D), "L1_attn_gv": (1, 1, D),
    "L1_attn_Wo": (D, D),
    "L1_ffn_Wg": (8, D), "L1_ffn_bg": (8,), "L1_ffn_bias": (8,),
    "L1_ffn_W1": (8, 4 * D, D), "L1_ffn_b1": (8, 4 * D),
    "L1_ffn_W2": (8, D, 4 * D), "L1_ffn_b2": (8, D),
    "L1_ffn_Ws1": (4 * D, D), "L1_ffn_bs1": (4 * D,),
    "L1_ffn_Ws2": (D, 4 * D), "L1_ffn_bs2": (D,),
}

STREAM_DEFS = {
    "xin": ((NT, D), F32),
    "swa": ((NTG, 128, 8, TG), BF16),
    "xnf": ((NTG, 128, 8, TG), BF16),
    "x1": ((NT, D), F32),
    "xA": ((NT, D), F32),
    "xB": ((NT, D), F32),
    "xC": ((NT, D), F32),
    "out": ((NT, D), F32),
}

PHASE_IO = {
    "p1a": (["xin"], ["swa", "xnf"]),
    "p1bc": (["xin", "swa", "xnf"], ["x1"]),
    "p2": (["x1", "xin"], ["xA"]),
    "p3": (["xA"], ["xB"]),
    "p4": (["xB"], ["xC"]),
    "p5": (["xC"], ["out"]),
}

PHASE_WEIGHTS = {
    "p1a": ["L0_attn_g1", "L0_attn_b1", "L0_attn_Wqkv_a", "L0_attn_bqkv_a"],
    "p1bc": ["L0_attn_g1", "L0_attn_b1", "L0_attn_Wqkv_s", "L0_attn_bqkv_s",
             "L0_attn_Wp_a", "L0_attn_bp_a", "L0_attn_Wo_s", "L0_attn_bo_s",
             "L0_attn_Wo", "L0_attn_bo"],
    "p2": ["L0_attn_gm", "L0_attn_bm", "L0_attn_W1", "L0_attn_b1m",
           "L0_attn_W2", "L0_attn_b2m"],
    "p3": ["L0_ffn_router_w", "L0_ffn_router_b", "L0_ffn_weights1",
           "L0_ffn_bias1", "L0_ffn_weights2", "L0_ffn_bias2"],
    "p4": ["L1_attn_Wq", "L1_attn_gq", "L1_attn_Wk", "L1_attn_gk",
           "L1_attn_Wv", "L1_attn_gv", "L1_attn_Wo"],
    "p5": ["L1_ffn_Wg", "L1_ffn_bg", "L1_ffn_bias", "L1_ffn_W1", "L1_ffn_b1",
           "L1_ffn_W2", "L1_ffn_b2", "L1_ffn_Ws1", "L1_ffn_bs1",
           "L1_ffn_Ws2", "L1_ffn_bs2"],
}

PHASE_CASTS = {
    "p1a": ["L0_attn_Wqkv_a"],
    "p1bc": ["L0_attn_Wqkv_s", "L0_attn_Wp_a", "L0_attn_Wo_s", "L0_attn_Wo"],
    "p2": ["L0_attn_W1", "L0_attn_W2"],
    "p3": ["L0_ffn_weights1", "L0_ffn_weights2"],
    "p4": ["L1_attn_Wo"],
    "p5": ["L1_ffn_W1", "L1_ffn_W2", "L1_ffn_Ws1", "L1_ffn_Ws2"],
}

EMITTERS = {}


def build(phases, debug_streams=False):
    nc = bass.Bass("TRN2", target_bir_lowering=False, debug=False)
    ctx = Ctx()
    ctx.nc = nc
    ctx._phase_allocs = []
    P = Prog(nc)
    ctx.P = P

    wanted = set()
    for ph in phases:
        wanted.update(PHASE_WEIGHTS[ph])
    ctx.I = {n: nc.dram_tensor(n, list(INPUT_SPECS[n]), F32, kind="ExternalInput").ap()
             for n in sorted(wanted)}

    reads, writes = set(), set()
    for ph in phases:
        r, w = PHASE_IO[ph]
        reads.update(r)
        writes.update(w)
    ctx.streams = {}
    ctx.external_streams_in = []
    ctx.external_streams_out = []
    for sname in sorted(reads | writes):
        shape, dt = STREAM_DEFS[sname]
        if sname in writes:
            if sname == "out" or debug_streams or sname not in reads:
                kind = "ExternalOutput"
                ctx.external_streams_out.append(sname)
            else:
                kind = "Internal"
        else:
            kind = "ExternalInput"
            ctx.external_streams_in.append(sname)
        ctx.streams[sname] = nc.dram_tensor("s_" + sname, list(shape), dt, kind=kind).ap()

    sem_names = ["dL", "dG", "dS", "tns", "vec", "scl", "pol", "bar", "dT"]
    ctx._sem_guards = []
    def fresh_sems(tag, names):
        for snm in names:
            g = nc.semaphore(f"{snm}_{tag}")
            h = g.__enter__()
            ctx._sem_guards.append(g)
            setattr(ctx, snm, S(h))
    fresh_sems("g", sem_names)

    perm = []
    def psalloc(name, shape, dt):
        g = nc.sbuf_tensor(name, shape, dt)
        h = g.__enter__()
        perm.append(g)
        return h.ap()
    ctx.id_f32 = psalloc("id_f32", [128, 128], F32)
    ctx.id_bf = psalloc("id_bf", [128, 128], BF16)
    ctx.ones_row_bf = psalloc("ones_row_bf", [1, 512], BF16)
    ctx.ones_col_bf = psalloc("ones_col_bf", [128, 1], BF16)
    ctx.ones_col_f32 = psalloc("ones_col_f32", [1, 1], F32)
    ctx.sqsum = psalloc("sqsum", [128, 1], F32)
    ctx.sumcol = psalloc("sumcol", [128, 1], F32)
    ctx.meancol = psalloc("meancol", [128, 1], F32)
    ctx.msq = psalloc("msq", [128, 1], F32)
    ctx.varcol = psalloc("varcol", [128, 1], F32)
    ctx.lnv = psalloc("lnv", [128, 1], F32)
    ctx.rstd = psalloc("rstd", [128, 1], F32)
    ctx.rowbuf = psalloc("rowbuf", [1, 512], F32)
    ctx.epscol = psalloc("epscol", [128, 1], F32)
    ctx.ones_f32row = psalloc("ones_f32row", [1, 128], F32)
    gp = nc.psum_tensor("ps_misc", [128, 512], F32)
    ctx.ps_misc = gp.__enter__().ap()
    perm.append(gp)
    ctx._perm = perm

    P.do("gpsimd", lambda e: e.memset(ctx.id_f32, 0.0), drain=True)
    P.do("gpsimd", lambda e: e.affine_select(out=ctx.id_f32, in_=ctx.id_f32,
            compare_op=ALU.not_equal, fill=1.0, base=0, pattern=[[-1, 128]],
            channel_multiplier=1), drain=True)
    P.do("gpsimd", lambda e: e.memset(ctx.id_bf, 0.0), drain=True)
    P.do("gpsimd", lambda e: e.affine_select(out=ctx.id_bf, in_=ctx.id_bf,
            compare_op=ALU.not_equal, fill=1.0, base=0, pattern=[[-1, 128]],
            channel_multiplier=1), drain=True)
    P.do("gpsimd", lambda e: e.memset(ctx.ones_row_bf, 1.0), drain=True)
    P.do("gpsimd", lambda e: e.memset(ctx.ones_col_bf, 1.0), drain=True)
    P.do("gpsimd", lambda e: e.memset(ctx.ones_col_f32, 1.0), drain=True)
    P.do("gpsimd", lambda e: e.memset(ctx.epscol, 1e-5), drain=True)
    P.do("gpsimd", lambda e: e.memset(ctx.ones_f32row, 1.0), inc=ctx.pol, drain=True)
    ctx.consts_ready = ctx.pol.mark

    ctx.wsc = {}
    ctx.castdone = {}
    fresh_sems("c", ["dC"])

    def emit_casts(ph):
        for wn in PHASE_CASTS.get(ph, []):
            if wn in ctx.wsc:
                continue
            shape = list(INPUT_SPECS[wn])
            ctx.wsc[wn] = nc.dram_tensor("wsc_" + wn, shape, BF16, kind="Internal").ap()
            s_ap = ctx.I[wn]
            d_ap = ctx.wsc[wn]
            if len(shape) == 3:
                s_ap = s_ap.rearrange("a b c -> (a b) c")
                d_ap = d_ap.rearrange("a b c -> (a b) c")
            P.dma("gpsimd", d_ap, s_ap, ctx.dC)
            ctx.castdone[wn] = ctx.dC.mark

    emit_casts(phases[0])
    for i, ph in enumerate(phases):
        fresh_sems(ph, ["dL", "dG", "dS", "tns", "vec", "scl", "pol", "dT"])
        EMITTERS[ph](ctx)
        if i == 0:
            for ph2 in phases[1:]:
                emit_casts(ph2)
        ctx.phase_end()
        ctx.free_phase_allocs()

    P.emit()
    return nc, ctx



def run_phases(phases, inputs, extra_streams=None, core_ids=None,
               debug_streams=False):
    core_ids = core_ids if core_ids is not None else list(range(NCORES))
    nc, ctx = build(phases, debug_streams=debug_streams)
    x = np.ascontiguousarray(inputs["L0_attn_x"], dtype=np.float32)
    in_maps = []
    for i, c in enumerate(core_ids):
        m = {}
        for n in ctx.I:
            m[n] = np.ascontiguousarray(inputs[n], dtype=np.float32)
        for sn in ctx.external_streams_in:
            if sn == "xin":
                m["s_xin"] = x[2 * c:2 * c + 2].reshape(NT, D)
            else:
                m["s_" + sn] = extra_streams[sn][i]
        if "xin" in ctx.streams and "s_xin" not in m:
            m["s_xin"] = x[2 * c:2 * c + 2].reshape(NT, D)
        in_maps.append(m)
    res = run_bass_kernel_spmd(nc, in_maps, core_ids=core_ids)
    return res, ctx


def kernel(**inputs):
    if FUSED:
        phases = ["p1a", "p1bc", "p2", "p3", "p4", "p5"]
        res, ctx = run_phases(phases, inputs)
        outs = [r["s_out"] for r in res.results]
    else:
        carry = {}
        ctx = None
        for g in PHASE_GROUPS:
            res, ctx = run_phases(g, inputs, extra_streams=carry,
                                  debug_streams=True)
            for sn in ctx.external_streams_out:
                carry[sn] = [r["s_" + sn] for r in res.results]
        outs = carry["out"]
    full = np.concatenate([o.reshape(2, SEQ, D) for o in outs], axis=0)
    return full.astype(np.float32)


EMITTERS["p2"] = emit_p2
EMITTERS["p3"] = emit_p3
EMITTERS["p5"] = emit_p5
EMITTERS["p1a"] = emit_p1a
EMITTERS["p1bc"] = emit_p1bc
EMITTERS["p4"] = emit_p4


# version 06
# speedup vs baseline: 1.2283x; 1.2283x over previous
import numpy as np
import ml_dtypes
import concourse.bass as bass
import concourse.mybir as mybir
from concourse.bass_utils import run_bass_kernel_spmd

F32 = mybir.dt.float32
BF16 = mybir.dt.bfloat16
AF = mybir.ActivationFunctionType
ALU = mybir.AluOpType
AX = mybir.AxisListType

NCORES = 8
D = 1024
NT = 4096
SEQ = 2048
NCH = NT // 128
TG = 512
NTG = NT // TG
CPG = TG // 128
NH = 16
EH = 64

ENGINES = ("sync", "tensor", "vector", "scalar", "gpsimd")

P3_NE = 32
P5_NE = 9
P1B_DBG = False

def dbg_ufm_hook(ctx, gidx):
    pass
P5_NOACC = False

FUSED = True
PHASE_GROUPS = [["p1a"], ["p1bc"], ["p2"], ["p3"], ["p4"], ["p5"]]


class S:
    def __init__(self, h):
        self.h = h
        self.n = 0

    @property
    def mark(self):
        return (self, self.n)


class Prog:
    def __init__(self, nc):
        self.nc = nc
        self.entries = {e: [] for e in ENGINES}

    def do(self, eng, fn, wait=(), inc=(), drain=False):
        w = []
        for x in wait:
            if isinstance(x, S):
                w.append((x.h, x.n))
            else:
                s, v = x
                assert v <= s.n, f"wait value {v} > recorded count {s.n}"
                if v > 0:
                    w.append((s.h, v))
        ii = []
        lst = inc if isinstance(inc, (list, tuple)) and not (
            len(inc) == 2 and isinstance(inc[0], S) and isinstance(inc[1], int)) else [inc]
        for x in lst:
            if x is None:
                continue
            if isinstance(x, S):
                x.n += 1
                ii.append((x.h, 1))
            else:
                s, n = x
                s.n += n
                ii.append((s.h, n))
        self.entries[eng].append((w, fn, ii, drain))

    def dma(self, eng, out, in_, sem, wait=(), **kw):
        self.do(eng, lambda e: e.dma_start(out, in_, **kw), wait=wait, inc=[(sem, 16)])

    def emit(self):
        with self.nc.Block() as block:
            for eng in ENGINES:
                entries = self.entries[eng]
                if not entries:
                    continue

                def body(e, entries=entries):
                    for w, fn, ii, drain in entries:
                        for h, v in w:
                            e.wait_ge(h, v)
                        inst = fn(e)
                        for h, n in ii:
                            inst.then_inc(h, n)
                        if drain:
                            e.drain()

                getattr(block, eng)(body)


class Ctx:
    def salloc(self, name, shape, dt):
        g = self.nc.sbuf_tensor(name, shape, dt)
        h = g.__enter__()
        self._phase_allocs.append(g)
        return h.ap()

    def palloc(self, name, shape, dt=F32):
        g = self.nc.psum_tensor(name, shape, dt)
        h = g.__enter__()
        self._phase_allocs.append(g)
        return h.ap()

    def free_phase_allocs(self):
        for g in reversed(self._phase_allocs):
            g.__exit__(None, None, None)
        self._phase_allocs = []

    def push_scope(self):
        self._scope_marks = getattr(self, "_scope_marks", [])
        self._scope_marks.append(len(self._phase_allocs))

    def pop_scope(self):
        k = self._scope_marks.pop()
        for g in reversed(self._phase_allocs[k:]):
            g.__exit__(None, None, None)
        del self._phase_allocs[k:]

    def phase_end(self):
        P = self.P
        P.do("sync", lambda e: e.nop(),
             wait=[self.dL, self.dG, self.dS])
        for e in ENGINES:
            P.do(e, lambda en: en.nop(), inc=self.bar)
        tgt = self.bar.n
        for e in ENGINES:
            P.do(e, lambda en: en.nop(), wait=[(self.bar, tgt)])



def load_cols(ctx, vec_ap, cols_dst, n128):
    P = ctx.P
    off = 0
    while off < n128:
        k = min(4, n128 - off)
        P.dma("sync", ctx.rowbuf[0:1, 0:k * 128],
              vec_ap[off * 128:(off + k) * 128].rearrange("(o d) -> o d", o=1), ctx.dL,
              wait=[ctx.tns.mark])
        for c in range(k):
            P.do("tensor",
                 lambda e, c=c, off=off: e.matmul(
                     ctx.ps_misc[:, off + c: off + c + 1],
                     ctx.rowbuf[0:1, c * 128:(c + 1) * 128],
                     ctx.ones_col_f32[0:1, 0:1], start=True, stop=True),
                 wait=[ctx.dL.mark, ctx.consts_ready, ctx.vec.mark], inc=ctx.tns)
        off += k
    P.do("vector", lambda e: e.tensor_copy(cols_dst, ctx.ps_misc[:, 0:n128]),
         wait=[ctx.tns], inc=ctx.vec, drain=True)


def emit_ln(ctx, src, rstd_out, src_wait=(), dump=None):
    P = ctx.P
    N = src.shape[-1]
    P.do("scalar", lambda e: e.activation(dump[:, 0:N], src, AF.Square,
                                          accum_out=ctx.sqsum),
         wait=list(src_wait) + [ctx.vec.mark], inc=ctx.scl, drain=True)
    P.do("vector", lambda e: e.tensor_reduce(ctx.sumcol, src, AX.X, ALU.add),
         wait=list(src_wait) + [ctx.scl.mark], drain=True)
    P.do("vector", lambda e: e.tensor_scalar(ctx.meancol, ctx.sumcol, 1.0 / N, None,
                                             ALU.mult), drain=True)
    P.do("vector", lambda e: e.tensor_tensor(ctx.msq, ctx.meancol, ctx.meancol, ALU.mult),
         drain=True)
    P.do("vector", lambda e: e.scalar_tensor_tensor(ctx.varcol, ctx.sqsum, 1.0 / N,
                                                    ctx.msq, ALU.mult, ALU.subtract),
         wait=[ctx.scl], inc=ctx.vec, drain=True)
    P.do("scalar", lambda e: e.activation(ctx.lnv, ctx.varcol, AF.Ln, bias=ctx.epscol),
         wait=[ctx.vec, ctx.consts_ready], drain=True)
    P.do("scalar", lambda e: e.activation(rstd_out, ctx.lnv, AF.Exp, scale=-0.5),
         inc=ctx.scl, drain=True)



def emit_p2(ctx):
    nc, P = ctx.nc, ctx.P
    dL, dG, dS, tns, vec, scl = ctx.dL, ctx.dG, ctx.dS, ctx.tns, ctx.vec, ctx.scl

    w1sb = ctx.salloc("p2_w1", [128, 8 * 4096], BF16)
    w2sb = ctx.salloc("p2_w2", [128, 32 * 1024], BF16)
    b2row = ctx.salloc("p2_b2", [1, 1024], BF16)
    x1t = ctx.salloc("p2_x1t", [128, CPG * 1024], F32)
    z1 = ctx.salloc("p2_z1", [128, 1024], F32)
    xnfm = ctx.salloc("p2_xnfm", [128, 8 * TG], BF16)
    hsb = ctx.salloc("p2_h", [128, 32 * TG], BF16)
    outb = ctx.salloc("p2_out", [128, 1024], F32)
    gmcols = ctx.salloc("p2_gmc", [128, 8], F32)
    bmcols = ctx.salloc("p2_bmc", [128, 8], F32)
    b1cols = ctx.salloc("p2_b1c", [128, 32], F32)

    ps_t0 = ctx.palloc("p2_pst0", [128, 128])
    ps_t = [ps_t0, ps_t0]
    ps_h = [ctx.palloc("p2_psh0", [128, 512]), ctx.palloc("p2_psh1", [128, 512])]
    ps_o = [ctx.palloc("p2_pso0", [128, 1024]), ctx.palloc("p2_pso1", [128, 1024])]

    P.dma("sync", w1sb.rearrange("p (c g) -> p c g", c=8),
          ctx.wsc["L0_attn_W1"].rearrange("(c p) g -> p c g", p=128), dL,
          wait=[ctx.castdone["L0_attn_W1"]])
    P.dma("sync", w2sb.rearrange("p (j d) -> p j d", j=32),
          ctx.wsc["L0_attn_W2"].rearrange("(j p) d -> p j d", p=128), dL,
          wait=[ctx.castdone["L0_attn_W2"]])
    P.dma("gpsimd", b2row, ctx.I["L0_attn_b2m"].rearrange("(o d) -> o d", o=1), dG)
    load_cols(ctx, ctx.I["L0_attn_gm"], gmcols, 8)
    load_cols(ctx, ctx.I["L0_attn_bm"], bmcols, 8)
    load_cols(ctx, ctx.I["L0_attn_b1m"], b1cols, 32)
    cols_ready = ctx.vec.mark
    wload = dL.mark

    x1v = ctx.streams["x1"].rearrange("(g u p) d -> g p u d", p=128, u=CPG)
    xAv = ctx.streams["xA"].rearrange("(g u p) d -> g p u d", p=128, u=CPG)
    xinv = ctx.streams["xin"].rearrange("(g u p) d -> g p u d", p=128, u=CPG)
    xst = ctx.salloc("p2_xst", [128, 1024], F32)

    for tg in range(NTG):
        P.dma("sync", x1t.rearrange("p (u d) -> p u d", u=CPG), x1v[tg], dL,
              wait=[vec.mark])
        ld = dL.mark
        for u in range(CPG):
            x1c = x1t[:, u * 1024:(u + 1) * 1024]
            emit_ln(ctx, x1c, ctx.rstd, src_wait=[ld], dump=z1)
            P.do("vector", lambda e, x1c=x1c: e.tensor_scalar(
                    z1, x1c, ctx.meancol, ctx.rstd, ALU.subtract, ALU.mult),
                 wait=[scl], inc=vec, drain=True)
            for c in range(8):
                par = c % 2
                P.do("tensor", lambda e, c=c, par=par: e.transpose(
                        ps_t[par], z1[:, c * 128:(c + 1) * 128], ctx.id_f32),
                     wait=[vec.mark, scl.mark], inc=tns)
                P.do("vector", lambda e, c=c, u=u, par=par: e.tensor_scalar(
                        xnfm[:, c * TG + u * 128: c * TG + (u + 1) * 128],
                        ps_t[par], gmcols[:, c:c + 1], bmcols[:, c:c + 1],
                        ALU.mult, ALU.add),
                     wait=[tns.mark], inc=vec, drain=False)
        hmark = [scl.mark, scl.mark]
        xn_ready = (vec.mark, scl.mark)
        for j in range(32):
            pb = ps_h[j % 2]
            for c in range(8):
                P.do("tensor", lambda e, j=j, c=c, pb=pb: e.matmul(
                        pb, w1sb[:, c * 4096 + j * 128: c * 4096 + (j + 1) * 128],
                        xnfm[:, c * TG:(c + 1) * TG],
                        start=(c == 0), stop=(c == 7)),
                     wait=([xn_ready[0], xn_ready[1], wload, hmark[j % 2]]
                           if c == 0 else []),
                     inc=(tns if c == 7 else None))
            P.do("scalar", lambda e, j=j, pb=pb: e.activation(
                    hsb[:, j * TG:(j + 1) * TG], pb, AF.Gelu,
                    bias=b1cols[:, j:j + 1]),
                 wait=[tns.mark, cols_ready], inc=scl, drain=False)
            hmark[j % 2] = scl.mark
        omark = [vec.mark, vec.mark]
        for u in range(CPG):
            po = ps_o[u % 2]
            xs = xst
            P.dma("sync", xs, xinv[tg][:, u], dL, wait=[ctx.pol.mark])
            xld = dL.mark
            for n in range(2):
                for j in range(32):
                    P.do("tensor", lambda e, u=u, n=n, j=j, po=po: e.matmul(
                            po[:, n * 512:(n + 1) * 512],
                            hsb[:, j * TG + u * 128: j * TG + (u + 1) * 128],
                            w2sb[:, j * 1024 + n * 512: j * 1024 + n * 512 + 512],
                            start=(j == 0), stop=False),
                         wait=([scl.mark, omark[u % 2], wload]
                               if (n == 0 and j == 0) else []))
                P.do("tensor", lambda e, n=n, po=po: e.matmul(
                        po[:, n * 512:(n + 1) * 512], ctx.ones_row_bf[0:1, 0:128],
                        b2row[0:1, n * 512:(n + 1) * 512], start=False, stop=True),
                     inc=(tns if n == 1 else None))
            P.do("vector", lambda e, u=u, po=po: e.scalar_tensor_tensor(
                    outb, po, 1.0, x1t[:, u * 1024:(u + 1) * 1024], ALU.mult, ALU.add),
                 wait=[tns, dS.mark], inc=vec, drain=False)
            omark[u % 2] = vec.mark
            P.do("gpsimd", lambda e: e.tensor_tensor(outb, outb, xs, ALU.add),
                 wait=[vec.mark, xld], inc=ctx.pol, drain=True)
            P.dma("sync", xAv[tg][:, u], outb, dS, wait=[ctx.pol.mark])



def emit_p1a(ctx):
    nc, P = ctx.nc, ctx.P
    dL, dG, dS, tns, vec, scl, pol = (ctx.dL, ctx.dG, ctx.dS, ctx.tns, ctx.vec,
                                      ctx.scl, ctx.pol)

    wq = ctx.salloc("p1a_wq", [128, 8 * 3072], BF16)
    bqrow = ctx.salloc("p1a_bq", [1, 3072], BF16)
    g1cols = ctx.salloc("p1a_g1c", [128, 8], F32)
    b1cols = ctx.salloc("p1a_b1c", [128, 8], F32)
    kfm = ctx.salloc("p1a_kfm", [128, 8 * SEQ], BF16)
    vtm = ctx.salloc("p1a_vtm", [128, 16 * 1024], BF16)
    qfm = ctx.salloc("p1a_qfm", [128, 8 * TG], BF16)
    xnfm = ctx.salloc("p1a_xnfm", [128, 8 * TG], BF16)
    sout = ctx.salloc("p1a_sout", [128, 8 * TG], BF16)
    xt = ctx.salloc("p1a_xt", [128, CPG * 1024], F32)
    zb = ctx.salloc("p1a_zb", [128, 1024], F32)
    z2 = ctx.salloc("p1a_z2", [128, 1024], F32)
    pf = ctx.salloc("p1a_pf", [128, 2 * 256], F32)
    pm = ctx.salloc("p1a_pm", [128, 2 * 256], F32)
    pbf = ctx.salloc("p1a_pbf", [128, 2 * 256], BF16)
    ptsb = ctx.salloc("p1a_ptsb", [128, 2 * 256], BF16)
    rmax = ctx.salloc("p1a_rmax", [128, 2], F32)
    rmneg = ctx.salloc("p1a_rmneg", [128, 2], F32)
    rs = ctx.salloc("p1a_rs", [128, 2], F32)
    rcp = ctx.salloc("p1a_rcp", [128, 2], F32)
    mgen = ctx.salloc("p1a_mgen", [128, 256], F32)
    mb0 = ctx.salloc("p1a_mb0", [128, 128], F32)

    ps_q = [ctx.palloc("p1a_psq0", [128, 512])]
    ps_q.append(ps_q[0])
    ps_v = ctx.palloc("p1a_psv", [128, 1024])
    ps_sm = [ctx.palloc("p1a_pssm0", [128, 256]), ctx.palloc("p1a_pssm1", [128, 256])]
    ps_pt = ctx.palloc("p1a_pspt", [128, 256], BF16)
    ps_pv = ctx.palloc("p1a_pspv", [64, 128])

    P.do("gpsimd", lambda e: e.memset(mgen, 1.0), drain=True)
    P.do("gpsimd", lambda e: e.affine_select(out=mgen, in_=mgen,
            compare_op=ALU.is_ge, fill=0.0, base=128, pattern=[[-1, 256]],
            channel_multiplier=1), drain=True)
    P.do("gpsimd", lambda e: e.affine_select(out=mgen, in_=mgen,
            compare_op=ALU.is_ge, fill=0.0, base=-1, pattern=[[1, 256]],
            channel_multiplier=-1), drain=True)
    P.do("gpsimd", lambda e: e.memset(mb0, 1.0), drain=True)
    P.do("gpsimd", lambda e: e.affine_select(out=mb0, in_=mb0,
            compare_op=ALU.is_ge, fill=0.0, base=0, pattern=[[-1, 128]],
            channel_multiplier=1), inc=pol, drain=True)
    masks_ready = pol.mark

    P.dma("sync", wq.rearrange("p (c f) -> p c f", c=8),
          ctx.wsc["L0_attn_Wqkv_a"].rearrange("(c p) f -> p c f", p=128), dL,
          wait=[ctx.castdone["L0_attn_Wqkv_a"]])
    P.dma("gpsimd", bqrow, ctx.I["L0_attn_bqkv_a"].rearrange("(o f) -> o f", o=1), dG)
    load_cols(ctx, ctx.I["L0_attn_g1"], g1cols, 8)
    load_cols(ctx, ctx.I["L0_attn_b1"], b1cols, 8)
    wld = (dL.mark, dG.mark, vec.mark)

    xinv = ctx.streams["xin"].rearrange("(g u p) d -> g p u d", p=128, u=CPG)
    swav = ctx.streams["swa"]
    xnfv = ctx.streams["xnf"]

    for s in range(2):
        for g4 in range(4):
            gidx = s * 4 + g4
            P.dma("sync", xt.rearrange("p (u d) -> p u d", u=CPG), xinv[gidx], dL,
                  wait=[scl.mark, vec.mark])
            ld = dL.mark
            for u in range(CPG):
                xc = xt[:, u * 1024:(u + 1) * 1024]
                emit_ln(ctx, xc, ctx.rstd, src_wait=[ld], dump=zb)
                P.do("vector", lambda e, xc=xc: e.tensor_scalar(
                        zb, xc, ctx.meancol, ctx.rstd, ALU.subtract, ALU.mult),
                     wait=[scl], inc=vec, drain=True)
                emit_ln(ctx, zb, ctx.rstd, src_wait=[], dump=z2)
                P.do("vector", lambda e: e.tensor_scalar(
                        z2, zb, ctx.meancol, ctx.rstd, ALU.subtract, ALU.mult),
                     wait=[scl], inc=vec, drain=True)
                for c in range(8):
                    P.do("tensor", lambda e, c=c: e.transpose(
                            ps_sm[c % 2][:, 0:128],
                            z2[:, c * 128:(c + 1) * 128], ctx.id_f32),
                         wait=[vec.mark, scl.mark, ctx.consts_ready], inc=tns)
                    P.do("vector", lambda e, c=c, u=u: e.tensor_scalar(
                            xnfm[:, c * TG + u * 128: c * TG + (u + 1) * 128],
                            ps_sm[c % 2][:, 0:128],
                            g1cols[:, c:c + 1], b1cols[:, c:c + 1],
                            ALU.mult, ALU.add),
                         wait=[tns.mark, wld[2], dS.mark], inc=vec, drain=True)
            P.dma("sync", xnfv[gidx], xnfm.rearrange("p (c t) -> p c t", c=8), dS,
                  wait=[vec.mark])
            for fc in range(8):
                pq = ps_q[fc % 2]
                for c in range(8):
                    P.do("tensor", lambda e, fc=fc, c=c, pq=pq: e.matmul(
                            pq, wq[:, c * 3072 + fc * 128: c * 3072 + (fc + 1) * 128],
                            xnfm[:, c * TG:(c + 1) * TG],
                            start=(c == 0), stop=False),
                         wait=([vec.mark, scl.mark, wld[0], wld[1]] if c == 0 else []))
                P.do("tensor", lambda e, fc=fc, pq=pq: e.matmul(
                        pq, bqrow[0:1, fc * 128:(fc + 1) * 128],
                        ctx.ones_row_bf[0:1, 0:TG], start=False, stop=True), inc=tns)
                P.do("scalar", lambda e, fc=fc, pq=pq: e.activation(
                        qfm[:, fc * TG:(fc + 1) * TG], pq, AF.Copy),
                     wait=[tns.mark], inc=scl, drain=False)
            for fc in range(8):
                pq = ps_q[fc % 2]
                for c in range(8):
                    P.do("tensor", lambda e, fc=fc, c=c, pq=pq: e.matmul(
                            pq, wq[:, c * 3072 + 1024 + fc * 128:
                                   c * 3072 + 1024 + (fc + 1) * 128],
                            xnfm[:, c * TG:(c + 1) * TG],
                            start=(c == 0), stop=False),
                         wait=([scl.mark] if c == 0 else []))
                P.do("tensor", lambda e, fc=fc, pq=pq: e.matmul(
                        pq, bqrow[0:1, 1024 + fc * 128: 1024 + (fc + 1) * 128],
                        ctx.ones_row_bf[0:1, 0:TG], start=False, stop=True), inc=tns)
                P.do("scalar", lambda e, fc=fc, g4=g4, pq=pq: e.activation(
                        kfm[:, fc * SEQ + g4 * TG: fc * SEQ + (g4 + 1) * TG],
                        pq, AF.Copy),
                     wait=[tns.mark], inc=scl, drain=False)
            for u in range(CPG):
                cs = g4 * CPG + u
                for n in range(2):
                    for c in range(8):
                        P.do("tensor", lambda e, u=u, n=n, c=c: e.matmul(
                                ps_v[:, n * 512:(n + 1) * 512],
                                xnfm[:, c * TG + u * 128: c * TG + (u + 1) * 128],
                                wq[:, c * 3072 + 2048 + n * 512:
                                   c * 3072 + 2048 + (n + 1) * 512],
                                start=(c == 0), stop=False),
                             wait=([vec.mark, scl.mark] if (n == 0 and c == 0) else []))
                    P.do("tensor", lambda e, n=n: e.matmul(
                            ps_v[:, n * 512:(n + 1) * 512],
                            ctx.ones_row_bf[0:1, 0:128],
                            bqrow[0:1, 2048 + n * 512: 2048 + (n + 1) * 512],
                            start=False, stop=True),
                         inc=(tns if n == 1 else None))
                P.do("vector", lambda e, cs=cs: e.tensor_copy(
                        vtm[:, cs * 1024:(cs + 1) * 1024], ps_v),
                     wait=[tns], inc=vec, drain=True)
            smark = [scl.mark, scl.mark]
            state = {"ptmark": scl.mark, "pvmark": scl.mark, "tail": None}

            def emit_tail(t):
                (u, h, par, W2, klo, pbf_mark) = t
                nw = W2 // 128
                hp = slice((h % 2) * 64, (h % 2) * 64 + 64)
                po = slice(par * 256, par * 256 + W2)
                for w in range(nw):
                    P.do("tensor", lambda e, w=w, par=par: e.transpose(
                            ps_pt[:, w * 128:(w + 1) * 128],
                            pbf[:, par * 256 + w * 128: par * 256 + (w + 1) * 128],
                            ctx.id_bf),
                         wait=([pbf_mark, state["ptmark"]] if w == 0 else []),
                         inc=(tns if w == nw - 1 else None))
                P.do("scalar", lambda e, W2=W2, po=po: e.copy(
                        ptsb[:, po], ps_pt[:, 0:W2]),
                     wait=[tns.mark, state["pvmark"]], inc=scl, drain=True)
                state["ptmark"] = scl.mark
                for w in range(nw):
                    kv = klo // 128 + w
                    P.do("tensor", lambda e, kv=kv, h=h, w=w, par=par: e.matmul(
                            ps_pv, vtm[:, kv * 1024 + h * 64: kv * 1024 + h * 64 + 64],
                            ptsb[:, par * 256 + w * 128: par * 256 + (w + 1) * 128],
                            start=(w == 0), stop=(w == nw - 1)),
                         wait=([scl.mark, vec.mark] if w == 0 else []),
                         inc=(tns if w == nw - 1 else None))
                P.do("scalar", lambda e, h=h, hp=hp, u=u: e.copy(
                        sout[hp, (h // 2) * TG + u * 128:
                             (h // 2) * TG + (u + 1) * 128], ps_pv),
                     wait=[tns.mark, dS.mark], inc=scl, drain=True)
                state["pvmark"] = scl.mark

            for u in range(CPG):
                cs = g4 * CPG + u
                b0 = (cs == 0)
                W2 = 128 if b0 else 256
                for h in range(NH):
                    par = h % 2
                    sm = ps_sm[par]
                    po = slice(par * 256, par * 256 + W2)
                    pc = slice(par, par + 1)
                    hp = slice((h % 2) * 64, (h % 2) * 64 + 64)
                    fc = h // 2
                    klo = (cs - (0 if b0 else 1)) * 128
                    P.do("tensor", lambda e, u=u, h=h, hp=hp, fc=fc, klo=klo, W2=W2,
                         sm=sm: e.matmul(sm[:, 0:W2],
                                  qfm[hp, fc * TG + u * 128: fc * TG + (u + 1) * 128],
                                  kfm[hp, fc * SEQ + klo: fc * SEQ + klo + W2],
                                  start=True, stop=True),
                         wait=[smark[par]], inc=tns)
                    P.do("vector", lambda e, W2=W2, sm=sm, pc=pc: e.tensor_reduce(
                            rmax[:, pc], sm[:, 0:W2], AX.X, ALU.max),
                         wait=[tns.mark], inc=vec, drain=True)
                    P.do("scalar", lambda e, pc=pc: e.mul(rmneg[:, pc], rmax[:, pc],
                                                          -0.125),
                         wait=[vec.mark], drain=True)
                    P.do("scalar", lambda e, W2=W2, sm=sm, po=po, pc=pc: e.activation(
                            pf[:, po], sm[:, 0:W2], AF.Exp,
                            bias=rmneg[:, pc], scale=0.125),
                         inc=scl, drain=True)
                    smark[par] = scl.mark
                    P.do("vector", lambda e, W2=W2, b0=b0, po=po: e.tensor_tensor(
                            pm[:, po], pf[:, po],
                            (mb0 if b0 else mgen)[:, 0:W2], ALU.mult),
                         wait=[scl.mark, masks_ready], drain=True)
                    P.do("vector", lambda e, W2=W2, po=po, pc=pc: e.tensor_reduce(
                            rs[:, pc], pm[:, po], AX.X, ALU.add), drain=True)
                    P.do("vector", lambda e, pc=pc: e.reciprocal(rcp[:, pc], rs[:, pc]),
                         drain=True)
                    P.do("vector", lambda e, W2=W2, po=po, pc=pc: e.tensor_scalar(
                            pbf[:, po], pm[:, po], rcp[:, pc], None, ALU.mult),
                         inc=vec, drain=True)
                    newtail = (u, h, par, W2, klo, vec.mark)
                    if state["tail"] is not None:
                        emit_tail(state["tail"])
                    state["tail"] = newtail
            if state["tail"] is not None:
                emit_tail(state["tail"])
                state["tail"] = None
            P.dma("sync", swav[gidx], sout.rearrange("p (c t) -> p c t", c=8), dS,
                  wait=[scl.mark, vec.mark])



def emit_p1bc(ctx):
    nc, P = ctx.nc, ctx.P
    dL, dG, dS, tns, vec, scl, pol = (ctx.dL, ctx.dG, ctx.dS, ctx.tns, ctx.vec,
                                      ctx.scl, ctx.pol)

    wqs = ctx.salloc("p1b_wqs", [128, 8 * 3072], BF16)
    wp = ctx.salloc("p1b_wp", [128, 8 * 1024], BF16)
    wos = ctx.salloc("p1b_wos", [128, 8 * 1024], BF16)
    wo = ctx.salloc("p1b_wo", [128, 8 * 1024], BF16)
    bqrow = ctx.salloc("p1b_bq", [1, 3072], BF16)
    bpbo = ctx.salloc("p1b_bpbo", [1, 1024], BF16)
    borow = ctx.salloc("p1b_bo", [1, 1024], BF16)
    brow2 = ctx.salloc("p1b_br2", [1, 1024], BF16)
    triu = ctx.salloc("p1b_triu", [128, 128], F32)
    xt = ctx.salloc("p1b_xt", [128, 2 * 1024], F32)
    xnin = ctx.salloc("p1b_xnin", [128, 8 * TG], BF16)
    swin = ctx.salloc("p1b_swin", [128, 8 * TG], BF16)
    qsf = ctx.salloc("p1b_qsf", [128, 8 * TG], BF16)
    ksf = ctx.salloc("p1b_ksf", [128, 8 * TG], BF16)
    vst = ctx.salloc("p1b_vst", [128, CPG * 1024], BF16)
    kst = ctx.salloc("p1b_kst", [128, CPG * 1024], BF16)
    lout = ctx.salloc("p1b_lout", [128, 8 * TG], BF16)
    ufm = ctx.salloc("p1b_ufm", [128, 8 * TG], BF16)
    Sf = ctx.salloc("p1b_Sf", [128, 8 * 64], F32)
    Sb = ctx.salloc("p1b_Sb", [128, 8 * 64], BF16)
    zsf = ctx.salloc("p1b_zsf", [128, 8], F32)
    zsb = ctx.salloc("p1b_zsb", [128, 8], BF16)
    t0b = ctx.salloc("p1b_t0", [128, 512], F32)
    t1b = t0b
    t2b = ctx.salloc("p1b_t2", [128, 512], F32)
    atb = ctx.salloc("p1b_atb", [128, 256], BF16)
    obf = ctx.salloc("p1b_obf", [128, 128], BF16)
    dcol = ctx.salloc("p1b_dcol", [128, 2], F32)
    rcpd = ctx.salloc("p1b_rcpd", [128, 2], F32)
    zb = ctx.salloc("p1b_zb", [128, 2 * 1024], F32)

    if P1B_DBG:
        ctx.dbg = nc.dram_tensor("s_dbg", [NTG, 128, 8, TG], BF16,
                                 kind="ExternalOutput").ap()
        ctx.dbg2 = nc.dram_tensor("s_dbg2", [NTG, 128, 8, TG], BF16,
                                  kind="ExternalOutput").ap()
    ps_q = ctx.palloc("p1b_psq", [128, 512])
    ps_v = ctx.palloc("p1b_psv", [128, 1024])
    ps_at = ctx.palloc("p1b_psat", [128, 128])
    ps_nd = ctx.palloc("p1b_psnd", [128, 128])
    ps_Sz = ctx.palloc("p1b_psSz", [64, 128])
    ps_ot = ctx.palloc("p1b_psot", [128, 128], BF16)

    P.do("gpsimd", lambda e: e.memset(triu, 1.0), drain=True)
    P.do("gpsimd", lambda e: e.affine_select(out=triu, in_=triu,
            compare_op=ALU.is_ge, fill=0.0, base=0, pattern=[[1, 128]],
            channel_multiplier=-1), inc=pol, drain=True)
    masks_ready = pol.mark

    P.dma("sync", wqs.rearrange("p (c f) -> p c f", c=8),
          ctx.wsc["L0_attn_Wqkv_s"].rearrange("(c p) f -> p c f", p=128), dL,
          wait=[ctx.castdone["L0_attn_Wqkv_s"]])
    P.dma("sync", wp.rearrange("p (c f) -> p c f", c=8),
          ctx.wsc["L0_attn_Wp_a"].rearrange("(c p) f -> p c f", p=128), dL,
          wait=[ctx.castdone["L0_attn_Wp_a"]])
    P.dma("sync", wos.rearrange("p (c f) -> p c f", c=8),
          ctx.wsc["L0_attn_Wo_s"].rearrange("(c p) f -> p c f", p=128), dL,
          wait=[ctx.castdone["L0_attn_Wo_s"]])
    P.dma("sync", wo.rearrange("p (c f) -> p c f", c=8),
          ctx.wsc["L0_attn_Wo"].rearrange("(c p) f -> p c f", p=128), dL,
          wait=[ctx.castdone["L0_attn_Wo"]])
    P.dma("gpsimd", bqrow, ctx.I["L0_attn_bqkv_s"].rearrange("(o f) -> o f", o=1), dG)
    P.dma("gpsimd", bpbo, ctx.I["L0_attn_bp_a"].rearrange("(o f) -> o f", o=1), dG)
    P.dma("gpsimd", brow2, ctx.I["L0_attn_bo_s"].rearrange("(o f) -> o f", o=1), dG)
    P.dma("gpsimd", borow, ctx.I["L0_attn_bo"].rearrange("(o f) -> o f", o=1), dG)
    P.do("vector", lambda e: e.tensor_tensor(bpbo, bpbo, brow2, ALU.add),
         wait=[dG], inc=vec, drain=True)
    wld = (dL.mark, dG.mark, vec.mark)

    xinv = ctx.streams["xin"].rearrange("(g u p) d -> g p u d", p=128, u=CPG)
    x1v = ctx.streams["x1"].rearrange("(g u p) d -> g p u d", p=128, u=CPG)
    swav = ctx.streams["swa"]
    xnfv = ctx.streams["xnf"]

    for s in range(2):
        for g4 in range(4):
            gidx = s * 4 + g4
            P.dma("sync", xnin.rearrange("p (c t) -> p c t", c=8), xnfv[gidx], dL,
                  wait=[scl.mark, vec.mark])
            P.dma("sync", swin.rearrange("p (c t) -> p c t", c=8), swav[gidx], dL)
            ld = dL.mark
            for fc in range(8):
                for c in range(8):
                    P.do("tensor", lambda e, fc=fc, c=c: e.matmul(
                            ps_q, wqs[:, c * 3072 + fc * 128: c * 3072 + (fc + 1) * 128],
                            xnin[:, c * TG:(c + 1) * TG],
                            start=(c == 0), stop=False),
                         wait=([vec.mark, scl.mark, ld, wld[0], wld[1]]
                               if c == 0 else []))
                P.do("tensor", lambda e, fc=fc: e.matmul(
                        ps_q, bqrow[0:1, fc * 128:(fc + 1) * 128],
                        ctx.ones_row_bf[0:1, 0:TG], start=False, stop=True), inc=tns)
                P.do("vector", lambda e: e.tensor_scalar(
                        t0b, ps_q, 0.125, 0.0, ALU.mult, ALU.min),
                     wait=[tns.mark], drain=True)
                P.do("vector", lambda e: e.tensor_scalar(
                        t2b, ps_q, 0.125, 0.0, ALU.mult, ALU.max),
                     inc=vec, drain=True)
                P.do("scalar", lambda e: e.activation(t1b, t0b, AF.Exp),
                     wait=[vec], inc=scl, drain=True)
                P.do("vector", lambda e, fc=fc: e.scalar_tensor_tensor(
                        qsf[:, fc * TG:(fc + 1) * TG], t1b, 1e-4, t2b,
                        ALU.add, ALU.add),
                     wait=[scl], inc=vec, drain=True)
            for fc in range(8):
                for c in range(8):
                    P.do("tensor", lambda e, fc=fc, c=c: e.matmul(
                            ps_q, wqs[:, c * 3072 + 1024 + fc * 128:
                                      c * 3072 + 1024 + (fc + 1) * 128],
                            xnin[:, c * TG:(c + 1) * TG],
                            start=(c == 0), stop=False),
                         wait=([vec.mark] if c == 0 else []))
                P.do("tensor", lambda e, fc=fc: e.matmul(
                        ps_q, bqrow[0:1, 1024 + fc * 128: 1024 + (fc + 1) * 128],
                        ctx.ones_row_bf[0:1, 0:TG], start=False, stop=True), inc=tns)
                P.do("vector", lambda e: e.tensor_single_scalar(
                        t0b, ps_q, 0.0, ALU.min),
                     wait=[tns.mark], drain=True)
                P.do("vector", lambda e: e.tensor_single_scalar(
                        t2b, ps_q, 0.0, ALU.max),
                     inc=vec, drain=True)
                P.do("scalar", lambda e: e.activation(t1b, t0b, AF.Exp),
                     wait=[vec], inc=scl, drain=True)
                P.do("vector", lambda e, fc=fc: e.scalar_tensor_tensor(
                        ksf[:, fc * TG:(fc + 1) * TG], t1b, 1e-4, t2b,
                        ALU.add, ALU.add),
                     wait=[scl], inc=vec, drain=True)
            for u in range(CPG):
                for n in range(2):
                    for c in range(8):
                        P.do("tensor", lambda e, u=u, n=n, c=c: e.matmul(
                                ps_v[:, n * 512:(n + 1) * 512],
                                xnin[:, c * TG + u * 128: c * TG + (u + 1) * 128],
                                wqs[:, c * 3072 + 2048 + n * 512:
                                    c * 3072 + 2048 + (n + 1) * 512],
                                start=(c == 0), stop=False),
                             wait=([vec.mark, scl.mark] if (n == 0 and c == 0) else []))
                    P.do("tensor", lambda e, n=n: e.matmul(
                            ps_v[:, n * 512:(n + 1) * 512],
                            ctx.ones_row_bf[0:1, 0:128],
                            bqrow[0:1, 2048 + n * 512: 2048 + (n + 1) * 512],
                            start=False, stop=True),
                         inc=(tns if n == 1 else None))
                P.do("vector", lambda e, u=u: e.tensor_copy(
                        vst[:, u * 1024:(u + 1) * 1024], ps_v),
                     wait=[tns], inc=vec, drain=True)
            for u in range(CPG):
                for fc in range(8):
                    P.do("tensor", lambda e, u=u, fc=fc: e.transpose(
                            ps_ot, ksf[:, fc * TG + u * 128: fc * TG + (u + 1) * 128],
                            ctx.id_bf),
                         wait=[vec.mark], inc=tns)
                    P.do("vector", lambda e, u=u, fc=fc: e.tensor_copy(
                            kst[:, u * 1024 + fc * 128: u * 1024 + (fc + 1) * 128],
                            ps_ot),
                         wait=[tns.mark], inc=vec, drain=True)
            lstate = {"atmark": [vec.mark, vec.mark], "ndmark": [vec.mark, vec.mark],
                      "otmark": scl.mark, "Szmark": vec.mark, "tail": None}
            ps_at2 = [ps_at, ps_v[:, 0:128]]
            ps_nd2 = [ps_nd, ps_q[:, 0:128]]

            def lin_tail(t):
                (u, h, par, obf_mark, atb_mark) = t
                hp = slice((h % 2) * 64, (h % 2) * 64 + 64)
                fc = h // 2
                tok = slice(fc * TG + u * 128, fc * TG + (u + 1) * 128)
                cs_t = g4 * CPG + u
                first = (cs_t == 0)
                P.do("tensor", lambda e, par=par: e.transpose(
                        ps_ot[0:64, :], obf[:, par * 64:(par + 1) * 64], ctx.id_bf),
                     wait=[obf_mark, lstate["otmark"]], inc=tns)
                P.do("scalar", lambda e, hp=hp, tok=tok: e.copy(
                        lout[hp, tok], ps_ot[0:64, :]),
                     wait=[tns.mark], inc=scl, drain=True)
                lstate["otmark"] = scl.mark
                P.do("tensor", lambda e, u=u, fc=fc, h=h: e.matmul(
                        ps_Sz[:, 0:64],
                        kst[:, u * 1024 + fc * 128 + (h % 2) * 64:
                            u * 1024 + fc * 128 + (h % 2) * 64 + 64],
                        vst[:, u * 1024 + h * 64: u * 1024 + h * 64 + 64],
                        start=True, stop=True),
                     wait=[lstate["Szmark"]])
                P.do("tensor", lambda e, u=u, fc=fc, h=h: e.matmul(
                        ps_Sz[:, 64:65],
                        kst[:, u * 1024 + fc * 128 + (h % 2) * 64:
                            u * 1024 + fc * 128 + (h % 2) * 64 + 64],
                        ctx.ones_col_bf, start=True, stop=True), inc=tns)
                Ssl = (hp, slice((h // 2) * 64, (h // 2 + 1) * 64))
                zsl = (hp, slice(h // 2, h // 2 + 1))
                if first:
                    P.do("vector", lambda e, Ssl=Ssl: e.tensor_copy(
                            Sf[Ssl], ps_Sz[:, 0:64]),
                         wait=[tns.mark], drain=True)
                    P.do("vector", lambda e, zsl=zsl: e.tensor_copy(
                            zsf[zsl], ps_Sz[:, 64:65]),
                         inc=vec, drain=True)
                else:
                    P.do("vector", lambda e, Ssl=Ssl: e.tensor_tensor(
                            Sf[Ssl], Sf[Ssl], ps_Sz[:, 0:64], ALU.add),
                         wait=[tns.mark], drain=True)
                    P.do("vector", lambda e, zsl=zsl: e.tensor_tensor(
                            zsf[zsl], zsf[zsl], ps_Sz[:, 64:65], ALU.add),
                         inc=vec, drain=True)
                lstate["Szmark"] = vec.mark

            for u in range(CPG):
                cs = g4 * CPG + u
                first = (cs == 0)
                for h in range(NH):
                    par = h % 2
                    pat = ps_at2[par]
                    pnd = ps_nd2[par]
                    hp = slice((h % 2) * 64, (h % 2) * 64 + 64)
                    fc = h // 2
                    tok = slice(fc * TG + u * 128, fc * TG + (u + 1) * 128)
                    atsl = atb[:, par * 128:(par + 1) * 128]
                    P.do("tensor", lambda e, hp=hp, tok=tok, pat=pat: e.matmul(
                            pat, ksf[hp, tok], qsf[hp, tok],
                            start=True, stop=True),
                         wait=[lstate["atmark"][par]], inc=tns)
                    P.do("vector", lambda e, pat=pat, atsl=atsl: e.scalar_tensor_tensor(
                            atsl, pat, 1.0, triu, ALU.mult, ALU.mult),
                         wait=[tns.mark, masks_ready], inc=vec, drain=True)
                    lstate["atmark"][par] = vec.mark
                    if first:
                        P.do("tensor", lambda e, u=u, h=h, pnd=pnd, atsl=atsl: e.matmul(
                                pnd[:, 0:64], atsl,
                                vst[:, u * 1024 + h * 64: u * 1024 + h * 64 + 64],
                                start=True, stop=True),
                             wait=[vec.mark, lstate["ndmark"][par]])
                        P.do("tensor", lambda e, pnd=pnd, atsl=atsl: e.matmul(
                                pnd[:, 64:65], atsl, ctx.ones_col_bf,
                                start=True, stop=True), inc=tns)
                    else:
                        P.do("tensor", lambda e, hp=hp, tok=tok, h=h, pnd=pnd: e.matmul(
                                pnd[:, 0:64], qsf[hp, tok],
                                Sb[hp, (h // 2) * 64:(h // 2 + 1) * 64],
                                start=True, stop=False),
                             wait=[vec.mark, lstate["ndmark"][par]])
                        P.do("tensor", lambda e, u=u, h=h, pnd=pnd, atsl=atsl: e.matmul(
                                pnd[:, 0:64], atsl,
                                vst[:, u * 1024 + h * 64: u * 1024 + h * 64 + 64],
                                start=False, stop=True))
                        P.do("tensor", lambda e, hp=hp, tok=tok, h=h, pnd=pnd: e.matmul(
                                pnd[:, 64:65], qsf[hp, tok],
                                zsb[hp, h // 2: h // 2 + 1],
                                start=True, stop=False))
                        P.do("tensor", lambda e, pnd=pnd, atsl=atsl: e.matmul(
                                pnd[:, 64:65], atsl, ctx.ones_col_bf,
                                start=False, stop=True), inc=tns)
                    P.do("vector", lambda e, pnd=pnd, par=par: e.tensor_single_scalar(
                            dcol[:, par:par + 1], pnd[:, 64:65], 1e-6, ALU.add),
                         wait=[tns.mark], drain=True)
                    P.do("vector", lambda e, par=par: e.reciprocal(
                            rcpd[:, par:par + 1], dcol[:, par:par + 1]), drain=True)
                    P.do("vector", lambda e, pnd=pnd, par=par: e.tensor_scalar(
                            obf[:, par * 64:(par + 1) * 64], pnd[:, 0:64],
                            rcpd[:, par:par + 1], None, ALU.mult),
                         inc=vec, drain=True)
                    lstate["ndmark"][par] = vec.mark
                    newt = (u, h, par, vec.mark, vec.mark)
                    if lstate["tail"] is not None:
                        lin_tail(lstate["tail"])
                    lstate["tail"] = newt
                if lstate["tail"] is not None:
                    lin_tail(lstate["tail"])
                    lstate["tail"] = None
                P.do("vector", lambda e: e.tensor_copy(Sb, Sf), drain=True)
                P.do("vector", lambda e: e.tensor_copy(zsb, zsf), inc=vec, drain=True)
            for fc in range(8):
                for c in range(8):
                    P.do("tensor", lambda e, fc=fc, c=c: e.matmul(
                            ps_q, wp[:, c * 1024 + fc * 128: c * 1024 + (fc + 1) * 128],
                            swin[:, c * TG:(c + 1) * TG],
                            start=(c == 0), stop=False),
                         wait=([vec.mark, scl.mark, ld] if c == 0 else []))
                for c in range(8):
                    P.do("tensor", lambda e, fc=fc, c=c: e.matmul(
                            ps_q, wos[:, c * 1024 + fc * 128: c * 1024 + (fc + 1) * 128],
                            lout[:, c * TG:(c + 1) * TG],
                            start=False, stop=False))
                P.do("tensor", lambda e, fc=fc: e.matmul(
                        ps_q, bpbo[0:1, fc * 128:(fc + 1) * 128],
                        ctx.ones_row_bf[0:1, 0:TG], start=False, stop=True),
                     wait=[wld[2]], inc=tns)
                P.do("scalar", lambda e, fc=fc: e.activation(
                        ufm[:, fc * TG:(fc + 1) * TG], ps_q, AF.Copy),
                     wait=[tns.mark], inc=scl, drain=False)
            if P1B_DBG:
                P.dma("sync", ctx.dbg2[gidx], ufm.rearrange("p (c t) -> p c t", c=8),
                      dS, wait=[scl.mark])
            ufm_ready = (scl.mark, vec.mark)

            def emit_mix_ln(u):
                par = u % 2
                zs = zb[:, par * 1024:(par + 1) * 1024]
                xc = xt[:, par * 1024:(par + 1) * 1024]
                P.dma("sync", xc, xinv[gidx][:, u], dL,
                      wait=[scl.mark, vec.mark])
                emit_ln(ctx, xc, ctx.rstd, src_wait=[dL.mark, dS.mark], dump=zs)
                P.do("vector", lambda e, xc=xc, zs=zs: e.tensor_scalar(
                        zs, xc, ctx.meancol, ctx.rstd, ALU.subtract, ALU.mult),
                     wait=[scl, dS.mark], inc=vec, drain=True)
                return (vec.mark, scl.mark)

            znext = emit_mix_ln(0)
            evmark = vec.mark
            for u in range(CPG):
                par = u % 2
                zs = zb[:, par * 1024:(par + 1) * 1024]
                zmark = znext
                for n in range(2):
                    for c in range(8):
                        P.do("tensor", lambda e, u=u, n=n, c=c: e.matmul(
                                ps_v[:, n * 512:(n + 1) * 512],
                                ufm[:, c * TG + u * 128: c * TG + (u + 1) * 128],
                                wo[:, c * 1024 + n * 512: c * 1024 + (n + 1) * 512],
                                start=(c == 0), stop=False),
                             wait=([ufm_ready[0], ufm_ready[1], evmark, wld[1]]
                                   if (n == 0 and c == 0) else []))
                    P.do("tensor", lambda e, n=n: e.matmul(
                            ps_v[:, n * 512:(n + 1) * 512],
                            ctx.ones_row_bf[0:1, 0:128],
                            borow[0:1, n * 512:(n + 1) * 512],
                            start=False, stop=True),
                         inc=(tns if n == 1 else None))
                mixmark = tns.mark
                if u + 1 < CPG:
                    znext = emit_mix_ln(u + 1)
                P.do("vector", lambda e, zs=zs: e.tensor_tensor(
                        zs, ps_v, zs, ALU.add),
                     wait=[(zmark[0][0], zmark[0][1]), (zmark[1][0], zmark[1][1]),
                           mixmark, dS.mark], inc=vec, drain=False)
                evmark = vec.mark
                P.dma("sync", x1v[gidx][:, u], zs, dS, wait=[vec.mark])



def emit_p3(ctx):
    nc, P = ctx.nc, ctx.P
    dL, dG, dS, tns, vec, scl, pol = (ctx.dL, ctx.dG, ctx.dS, ctx.tns, ctx.vec,
                                      ctx.scl, ctx.pol)

    xnfm3 = ctx.salloc("p3_xnfm", [128, 8 * NT], BF16)
    accy = ctx.salloc("p3_acc", [128, NCH * 1024], BF16)
    w1e = ctx.salloc("p3_w1e", [128, 2 * 8 * 256], BF16)
    w2e = ctx.salloc("p3_w2e", [128, 2 * 2 * 1024], BF16)
    b1e = ctx.salloc("p3_b1e", [1, 2 * 256], BF16)
    b2e = ctx.salloc("p3_b2e", [1, 2 * 1024], BF16)
    lgall = ctx.salloc("p3_lg", [128, NCH * 32], F32)
    rmaxall = ctx.salloc("p3_rmax", [128, NCH], F32)
    xt = ctx.salloc("p3_xt", [128, 2 * 1024], F32)
    z3 = ctx.salloc("p3_z3", [128, 1024], F32)
    zfm = ctx.salloc("p3_zfm", [128, 8 * 128], F32)
    hbf = ctx.salloc("p3_h", [128, 2 * 512], BF16)
    t2b = ctx.salloc("p3_t2", [128, 2 * 1024], BF16)
    mcol = ctx.salloc("p3_mcol", [128, 2], F32)
    outb = ctx.salloc("p3_out", [128, 1024], F32)

    xAv = ctx.streams["xA"].rearrange("(n p) d -> n p d", p=128)
    xBv = ctx.streams["xB"].rearrange("(n p) d -> n p d", p=128)

    ctx.push_scope()
    rwsb = ctx.salloc("p3_rw", [32, 1024], F32)
    rwT = ctx.salloc("p3_rwT", [128, 8 * 32], F32)
    rbrow = ctx.salloc("p3_rb", [1, 32], F32)
    ps_t = [ctx.palloc("p3_pst0", [128, 128]), ctx.palloc("p3_pst1", [128, 128])]
    ps_lg = ctx.palloc("p3_pslg", [128, 32])

    P.dma("sync", rwsb, ctx.I["L0_ffn_router_w"], dL)
    P.dma("sync", rbrow, ctx.I["L0_ffn_router_b"].rearrange("(o d) -> o d", o=1), dL)
    rwl = dL.mark
    for c in range(8):
        P.do("tensor", lambda e, c=c: e.transpose(
                ctx.ps_misc[:, 0:32], rwsb[:, c * 128:(c + 1) * 128],
                ctx.id_f32[0:32, 0:32]),
             wait=[rwl, ctx.consts_ready, vec.mark], inc=tns)
        P.do("vector", lambda e, c=c: e.tensor_copy(rwT[:, c * 32:(c + 1) * 32],
                                                    ctx.ps_misc[:, 0:32]),
             wait=[tns], inc=vec, drain=True)

    for ci in range(NCH):
        xs = xt[:, (ci % 2) * 1024:(ci % 2 + 1) * 1024]
        P.dma("sync", xs, xAv[ci], dL, wait=[scl.mark, vec.mark])
        ld = dL.mark
        emit_ln(ctx, xs, ctx.rstd, src_wait=[ld], dump=z3)
        P.do("vector", lambda e, xs=xs: e.tensor_scalar(
                z3, xs, ctx.meancol, ctx.rstd, ALU.subtract, ALU.mult),
             wait=[scl], inc=vec, drain=True)
        for c in range(8):
            par = c % 2
            P.do("tensor", lambda e, c=c, par=par: e.transpose(
                    ps_t[par], z3[:, c * 128:(c + 1) * 128], ctx.id_f32),
                 wait=[vec.mark, scl.mark], inc=tns)
            P.do("vector", lambda e, c=c, par=par: e.tensor_copy(
                    zfm[:, c * 128:(c + 1) * 128], ps_t[par]),
                 wait=[tns.mark], inc=vec, drain=False)
            P.do("vector", lambda e, c=c, ci=ci, par=par: e.tensor_copy(
                    xnfm3[:, c * NT + ci * 128: c * NT + (ci + 1) * 128],
                    ps_t[par]),
                 wait=[tns.mark], inc=vec, drain=False)
        for c in range(8):
            P.do("tensor", lambda e, c=c: e.matmul(
                    ps_lg, zfm[:, c * 128:(c + 1) * 128],
                    rwT[:, c * 32:(c + 1) * 32], start=(c == 0), stop=False),
                 wait=([vec.mark] if c == 0 else []))
        P.do("tensor", lambda e: e.matmul(
                ps_lg, ctx.ones_f32row[0:1, 0:128], rbrow,
                start=False, stop=True), wait=[rwl], inc=tns)
        P.do("vector", lambda e, ci=ci: e.tensor_copy(lgall[:, ci * 32:(ci + 1) * 32],
                                                      ps_lg),
             wait=[tns], inc=vec, drain=True)
        P.do("vector", lambda e, ci=ci: e.tensor_reduce(
                rmaxall[:, ci:ci + 1], lgall[:, ci * 32:(ci + 1) * 32],
                AX.X, ALU.max), inc=vec, drain=True)
    ctx.phase_end()
    ctx.pop_scope()

    ps_h = [ctx.palloc("p3_psh0", [128, 512]), ctx.palloc("p3_psh1", [128, 512])]
    ps_o = [ctx.palloc("p3_pso0", [128, 1024]), ctx.palloc("p3_pso1", [128, 1024])]
    w1v = ctx.wsc["L0_ffn_weights1"].rearrange("e (c p) h -> e p c h", p=128)
    w2v = ctx.wsc["L0_ffn_weights2"].rearrange("e (g p) d -> e p g d", p=128)

    def emit_eloads(e_i):
        sl = e_i % 2
        wt = [ctx.castdone["L0_ffn_weights1"], ctx.castdone["L0_ffn_weights2"],
              tns.mark]
        P.dma("sync", w1e.rearrange("p (s c h) -> p s c h", s=2, c=8)[:, sl],
              w1v[e_i], dL, wait=wt)
        P.dma("sync", w2e.rearrange("p (s g d) -> p s g d", s=2, g=2)[:, sl],
              w2v[e_i], dL, wait=wt)
        P.dma("gpsimd", b1e[0:1, sl * 256:(sl + 1) * 256],
              ctx.I["L0_ffn_bias1"][e_i].rearrange("(o h) -> o h", o=1), dG,
              wait=[tns.mark])
        P.dma("gpsimd", b2e[0:1, sl * 1024:(sl + 1) * 1024],
              ctx.I["L0_ffn_bias2"][e_i].rearrange("(o h) -> o h", o=1), dG,
              wait=[tns.mark])
        return (dL.mark, dG.mark)

    emarks = {0: emit_eloads(0)}
    omark = [vec.mark, vec.mark]
    t2mark = [pol.mark, pol.mark]
    for e_i in range(P3_NE):
        sl = e_i % 2
        wld = emarks[e_i]
        hmark = [vec.mark, vec.mark]
        for tgg in range(NTG):
            if tgg == NTG // 2 and e_i + 1 < P3_NE:
                emarks[e_i + 1] = emit_eloads(e_i + 1)
            for g in range(2):
                pb = ps_h[g]
                for c in range(8):
                    P.do("tensor", lambda e, c=c, g=g, sl=sl, tgg=tgg, pb=pb: e.matmul(
                            pb, w1e[:, sl * 2048 + c * 256 + g * 128:
                                    sl * 2048 + c * 256 + (g + 1) * 128],
                            xnfm3[:, c * NT + tgg * TG:(c * NT + (tgg + 1) * TG)],
                            start=(c == 0), stop=False),
                         wait=([wld[0], wld[1], hmark[g]] if c == 0 else []))
                P.do("tensor", lambda e, g=g, sl=sl, pb=pb: e.matmul(
                        pb, b1e[0:1, sl * 256 + g * 128: sl * 256 + (g + 1) * 128],
                        ctx.ones_row_bf[0:1, 0:TG], start=False, stop=True),
                     inc=tns)
                P.do("vector", lambda e, g=g, pb=pb: e.tensor_scalar(
                        hbf[:, g * 512:(g + 1) * 512], pb, 0.0, None, ALU.max),
                     wait=[tns.mark], inc=vec, drain=False)
                hmark[g] = vec.mark
            hready = vec.mark
            for u in range(CPG):
                ci = tgg * CPG + u
                po = ps_o[u % 2]
                for n in range(2):
                    for g in range(2):
                        P.do("tensor", lambda e, u=u, n=n, g=g, sl=sl, po=po: e.matmul(
                                po[:, n * 512:(n + 1) * 512],
                                hbf[:, g * 512 + u * 128: g * 512 + (u + 1) * 128],
                                w2e[:, sl * 2048 + g * 1024 + n * 512:
                                    sl * 2048 + g * 1024 + (n + 1) * 512],
                                start=(g == 0), stop=False),
                             wait=([hready, omark[u % 2]] if (n == 0 and g == 0)
                                   else []))
                    P.do("tensor", lambda e, n=n, sl=sl, po=po: e.matmul(
                            po[:, n * 512:(n + 1) * 512],
                            ctx.ones_row_bf[0:1, 0:128],
                            b2e[0:1, sl * 1024 + n * 512: sl * 1024 + (n + 1) * 512],
                            start=False, stop=True),
                         inc=(tns if n == 1 else None))
                mc = mcol[:, (u % 2):(u % 2) + 1]
                P.do("vector", lambda e, ci=ci, e_i=e_i, mc=mc: e.tensor_scalar(
                        mc, lgall[:, ci * 32 + e_i: ci * 32 + e_i + 1],
                        rmaxall[:, ci:ci + 1], None, ALU.is_ge),
                     wait=[scl.mark], inc=vec, drain=True)
                t2s = t2b[:, (u % 2) * 1024:(u % 2 + 1) * 1024]
                P.do("scalar", lambda e, po=po, t2s=t2s, mc=mc: e.activation(
                        t2s, po, AF.Relu, scale=mc),
                     wait=[tns.mark, vec.mark, t2mark[u % 2]], inc=scl, drain=False)
                omark[u % 2] = scl.mark
                asl = accy[:, ci * 1024:(ci + 1) * 1024]
                if e_i == 0:
                    P.do("gpsimd", lambda e, asl=asl, t2s=t2s: e.tensor_copy(asl, t2s),
                         wait=[scl.mark], inc=pol, drain=True)
                else:
                    P.do("gpsimd", lambda e, asl=asl, t2s=t2s: e.tensor_tensor(
                            asl, asl, t2s, ALU.add),
                         wait=[scl.mark], inc=pol, drain=True)
                t2mark[u % 2] = pol.mark

    for ci in range(NCH):
        xs = xt[:, (ci % 2) * 1024:(ci % 2 + 1) * 1024]
        P.dma("sync", xs, xAv[ci], dL, wait=[vec.mark])
        P.do("vector", lambda e, ci=ci, xs=xs: e.tensor_tensor(
                outb, xs, accy[:, ci * 1024:(ci + 1) * 1024], ALU.add),
             wait=[dL.mark, pol.mark, dS.mark], inc=vec, drain=False)
        P.dma("sync", xBv[ci], outb, dS, wait=[vec.mark])



def emit_p4(ctx):
    nc, P = ctx.nc, ctx.P
    dL, dG, dS, tns, vec, scl, pol = (ctx.dL, ctx.dG, ctx.dS, ctx.tns, ctx.vec,
                                      ctx.scl, ctx.pol)

    waT = {W: ctx.salloc(f"p4_waT_{W}", [128, 8 * 1024], BF16) for W in "qkv"}
    wsqm = {W: ctx.salloc(f"p4_wsqm_{W}", [1, 1024], BF16) for W in "qkv"}
    qfm4 = ctx.salloc("p4_qfm", [128, 8 * SEQ], BF16)
    kfm4 = ctx.salloc("p4_kfm", [128, 8 * SEQ], BF16)
    vtm4 = ctx.salloc("p4_vtm", [128, 16 * 1040], BF16)
    ksqb = ctx.salloc("p4_ksqb", [128, 16 * 16], F32)
    cmcol = ctx.salloc("p4_cm", [64, 1], BF16)
    P.do("gpsimd", lambda e: e.memset(cmcol, -0.5), drain=True)
    P.do("gpsimd", lambda e: e.memset(vtm4, 1.0), inc=pol, drain=True)
    cm_ready = pol.mark

    wa_dram = {W: nc.dram_tensor(f"p4_wa_{W}", [D, D], BF16, kind="Internal").ap()
               for W in "qkv"}

    ctx.push_scope()
    stage = ctx.salloc("p4_stage", [128, 4 * 1024], F32)
    wabf = ctx.salloc("p4_wabf", [128, 8 * 1024], BF16)
    ps_row = ctx.palloc("p4_psrow0", [1, 512])
    Wmap = {"q": "L1_attn_Wq", "k": "L1_attn_Wk", "v": "L1_attn_Wv"}
    Gmap = {"q": "L1_attn_gq", "k": "L1_attn_gk", "v": "L1_attn_gv"}
    for W in "qkv":
        for half in range(2):
            P.dma("sync", stage.rearrange("p (c f) -> p c f", c=4),
                  ctx.I[Wmap[W]].rearrange("(c p) f -> p c f", p=128)[
                      :, half * 4:(half + 1) * 4], dL,
                  wait=[scl.mark])
            P.do("scalar", lambda e, half=half: e.activation(
                    wabf[:, half * 4096:(half + 1) * 4096], stage, AF.Tanh),
                 wait=[dL], inc=scl, drain=True)
        P.dma("sync", wa_dram[W].rearrange("(c p) f -> p c f", p=128),
              wabf.rearrange("p (c f) -> p c f", c=8), dS, wait=[scl.mark])
        for c in range(8):
            P.do("sync", lambda e, W=W, c=c: e.dma_start_transpose(
                    waT[W][:, c * 1024:(c + 1) * 1024],
                    wa_dram[W][:, c * 128:(c + 1) * 128]),
                 wait=([dS.mark] if c == 0 else []), inc=[(dL, 16)])
        P.do("scalar", lambda e, W=W: e.activation(wabf, waT[W], AF.Square),
             wait=[dL], inc=scl, drain=True)
        for n in range(2):
            for c in range(8):
                P.do("tensor", lambda e, n=n, c=c: e.matmul(
                        ps_row, ctx.ones_col_bf,
                        wabf[:, c * 1024 + n * 512: c * 1024 + (n + 1) * 512],
                        start=(c == 0), stop=(c == 7)),
                     wait=([scl.mark, vec.mark, ctx.consts_ready] if c == 0 else []),
                     inc=(tns if c == 7 else None))
            P.do("vector", lambda e, W=W, n=n: e.tensor_scalar(
                    wsqm[W][0:1, n * 512:(n + 1) * 512], ps_row, -0.5, None,
                    ALU.mult),
                 wait=[tns], inc=vec, drain=True)
    prol_done = (dL.mark, vec.mark, scl.mark)
    ctx.phase_end()
    ctx.pop_scope()

    xBv = ctx.streams["xB"].rearrange("(n p) d -> n p d", p=128)
    xCv = ctx.streams["xC"].rearrange("(n p) d -> n p d", p=128)

    for s in range(2):
        ctx.push_scope()
        xt = ctx.salloc(f"p4_xt{s}", [128, 1024], F32)
        zb = ctx.salloc(f"p4_zb{s}", [128, 1024], F32)
        xaf = ctx.salloc(f"p4_xaf{s}", [128, 1024], F32)
        xabf = ctx.salloc(f"p4_xabf{s}", [128, 1024], BF16)
        xafm = ctx.salloc(f"p4_xafm{s}", [128, 2 * 1024], BF16)
        Ub = ctx.salloc(f"p4_U{s}", [128, 1024], F32)
        xsqc = ctx.salloc(f"p4_xsq{s}", [128, 2], F32)
        qtmb = ctx.salloc(f"p4_qtm{s}", [128, 1024], BF16)
        ktmb = ctx.salloc(f"p4_ktm{s}", [128, 1024], BF16)
        ksqraw = ctx.salloc(f"p4_ksqr{s}", [128, 16], F32)
        G1 = {W: ctx.salloc(f"p4s_G1_{W}{s}", [128, 1024], BF16) for W in "qkv"}
        G2 = {W: ctx.salloc(f"p4s_G2_{W}{s}", [128, 1024], BF16) for W in "qkv"}
        ps_tb = ctx.palloc(f"p4_pstb{s}", [128, 256], BF16)
        ps_d = ctx.palloc(f"p4_psd{s}", [128, 1024])
        for W in "qkv":
            gflat = ctx.I[Gmap[W]].rearrange("a b d -> (a b d)")
            for half in range(2):
                P.dma("sync", ctx.rowbuf,
                      gflat[half * 512:(half + 1) * 512].rearrange("(o d) -> o d", o=1),
                      dL, wait=[tns.mark])
                P.do("tensor", lambda e: e.matmul(
                        ctx.ps_misc, ctx.ones_f32row[0:1, 0:128], ctx.rowbuf,
                        start=True, stop=True),
                     wait=[dL.mark, ctx.consts_ready, vec.mark], inc=tns)
                P.do("vector", lambda e, W=W, half=half: e.tensor_scalar(
                        G1[W][:, half * 512:(half + 1) * 512], ctx.ps_misc,
                        1.0 / 1024, None, ALU.mult),
                     wait=[tns], drain=True)
                P.do("vector", lambda e, W=W, half=half: e.tensor_scalar(
                        G2[W][:, half * 512:(half + 1) * 512], ctx.ps_misc,
                        0.5, None, ALU.mult),
                     inc=vec, drain=True)
        k2s = ctx.salloc(f"p4_k2s{s}", [128, 1024], F32)
        tstate = {"dmark": scl.mark, "vmark": vec.mark, "tail": None}

        def pdl_tail(t):
            ci, par, hm = t
            xafm_p = xafm[:, par * 1024:(par + 1) * 1024]
            xsq_p = xsqc[:, par:par + 1]
            for W in "qkv":
                for n in range(2):
                    for c in range(8):
                        P.do("tensor", lambda e, W=W, n=n, c=c, xafm_p=xafm_p: e.matmul(
                                ps_d[:, n * 512:(n + 1) * 512],
                                xafm_p[:, c * 128:(c + 1) * 128],
                                waT[W][:, c * 1024 + n * 512: c * 1024 + (n + 1) * 512],
                                start=(c == 0), stop=False),
                             wait=([hm[0], hm[1], tstate["dmark"]] + list(prol_done)
                                   if (n == 0 and c == 0) else []))
                    P.do("tensor", lambda e, W=W, n=n: e.matmul(
                            ps_d[:, n * 512:(n + 1) * 512],
                            ctx.ones_row_bf[0:1, 0:128],
                            wsqm[W][0:1, n * 512:(n + 1) * 512],
                            start=False, stop=True),
                         inc=(tns if n == 1 else None))
                P.do("scalar", lambda e, xsq_p=xsq_p: e.activation(
                        Ub, ps_d, AF.Relu, bias=xsq_p, scale=-2.0),
                     wait=[tns, tstate["vmark"]], inc=scl, drain=True)
                tstate["dmark"] = scl.mark
                P.do("vector", lambda e, W=W: e.tensor_tensor(
                        Ub, Ub, G1[W], ALU.mult),
                     wait=[scl], drain=True)
                P.do("vector", lambda e, W=W: e.tensor_tensor(
                        Ub, G2[W], Ub, ALU.subtract),
                     inc=vec, drain=True)
                tstate["vmark"] = vec.mark
                if W == "v":
                    P.do("vector", lambda e, ci=ci: e.tensor_copy(
                            vtm4[:, ci * 1040:ci * 1040 + 1040].rearrange(
                                "p (h y) -> p h y", h=16)[:, :, 0:64],
                            Ub.rearrange("p (h y) -> p h y", h=16)),
                         wait=[cm_ready], inc=vec, drain=True)
                    tstate["vmark"] = vec.mark
                else:
                    dst = qtmb if W == "q" else ktmb
                    P.do("scalar", lambda e, dst=dst: e.activation(dst, Ub, AF.Tanh),
                         wait=[vec], inc=scl, drain=True)
            P.do("scalar", lambda e: e.activation(k2s, ktmb, AF.Square),
                 wait=[scl.mark], inc=scl, drain=True)
            P.do("vector", lambda e: e.tensor_reduce(
                    ksqraw, k2s.rearrange("p (h x) -> p h x", h=16), AX.X, ALU.add),
                 wait=[scl], drain=True)
            P.do("vector", lambda e, ci=ci: e.tensor_scalar(
                    ksqb[:, ci * 16:(ci + 1) * 16], ksqraw, -1.0 / 80.0, None,
                    ALU.mult),
                 inc=vec, drain=True)
            for src_b, dstb in ((qtmb, qfm4), (ktmb, kfm4)):
                for c in range(8):
                    P.do("tensor", lambda e, src_b=src_b, c=c: e.transpose(
                            ps_tb[:, (c % 2) * 128:(c % 2 + 1) * 128],
                            src_b[:, c * 128:(c + 1) * 128], ctx.id_bf),
                         wait=[vec.mark, scl.mark], inc=tns)
                    P.do("vector", lambda e, dstb=dstb, c=c, ci=ci: e.tensor_copy(
                            dstb[:, c * SEQ + ci * 128: c * SEQ + (ci + 1) * 128],
                            ps_tb[:, (c % 2) * 128:(c % 2 + 1) * 128]),
                         wait=[tns.mark], inc=vec, drain=True)

        for ci in range(16):
            gi = s * 16 + ci
            par = ci % 2
            P.dma("sync", xt, xBv[gi], dL, wait=[scl.mark, vec.mark])
            ld = dL.mark
            emit_ln(ctx, xt, ctx.rstd, src_wait=[ld], dump=zb)
            P.do("vector", lambda e: e.tensor_scalar(
                    zb, xt, ctx.meancol, ctx.rstd, ALU.subtract, ALU.mult),
                 wait=[scl], inc=vec, drain=True)
            P.do("scalar", lambda e: e.activation(xaf, zb, AF.Tanh),
                 wait=[vec], inc=scl, drain=True)
            P.do("scalar", lambda e, par=par: e.activation(zb, xaf, AF.Square,
                                                  accum_out=xsqc[:, par:par + 1]),
                 inc=scl, drain=True)
            P.do("vector", lambda e: e.tensor_copy(xabf, xaf),
                 wait=[scl.mark], inc=vec, drain=True)
            for c in range(8):
                P.do("tensor", lambda e, c=c: e.transpose(
                        ps_tb[:, (c % 2) * 128:(c % 2 + 1) * 128],
                        xabf[:, c * 128:(c + 1) * 128], ctx.id_bf),
                     wait=[vec.mark, scl.mark], inc=tns)
                P.do("vector", lambda e, c=c, par=par: e.tensor_copy(
                        xafm[:, par * 1024 + c * 128: par * 1024 + (c + 1) * 128],
                        ps_tb[:, (c % 2) * 128:(c % 2 + 1) * 128]),
                     wait=[tns.mark], inc=vec, drain=True)
            hm = (vec.mark, scl.mark)
            if tstate["tail"] is not None:
                pdl_tail(tstate["tail"])
            tstate["tail"] = (ci, par, hm)
        if tstate["tail"] is not None:
            pdl_tail(tstate["tail"])
            tstate["tail"] = None
        ctx.phase_end()
        ctx.pop_scope()

        ctx.push_scope()
        woT = ctx.salloc(f"p4_woT{s}", [128, 8 * 1024], BF16)
        pbuf = ctx.salloc(f"p4_pbuf{s}", [128, 16 * 512], BF16)
        ofm = ctx.salloc(f"p4_ofm{s}", [128, 8 * 512], BF16)
        q2b = ctx.salloc(f"p4_q2b{s}", [64, 512], BF16)
        qsqrow = ctx.salloc(f"p4_qsqr{s}", [1, 512], BF16)
        denrow = ctx.salloc(f"p4_den{s}", [1, 512], F32)
        rcpden = denrow
        rbsb = ctx.salloc(f"p4_rb{s}", [64, 512], F32)
        xt2 = ctx.salloc(f"p4_xt2{s}", [128, 1024], F32)
        ps_s = [ctx.palloc(f"p4_pss0{s}", [128, 512]), ctx.palloc(f"p4_pss1{s}", [128, 512])]
        ps_dn = ctx.palloc(f"p4_psdn{s}", [1, 512])
        ps_rb = ctx.palloc(f"p4_psrb{s}", [64, 512])
        ps_pv = ctx.palloc(f"p4_pspv{s}", [65, 512])
        ps_y = ctx.palloc(f"p4_psy{s}", [128, 1024])
        for c in range(8):
            P.do("sync", lambda e, c=c: e.dma_start_transpose(
                    woT[:, c * 1024:(c + 1) * 1024],
                    ctx.wsc["L1_attn_Wo"][:, c * 128:(c + 1) * 128]),
                 wait=([ctx.castdone["L1_attn_Wo"], tns.mark] if c == 0 else []),
                 inc=[(dL, 16)])
        wol = dL.mark
        for qtg in range(4):
            for h in range(NH):
                hp = slice((h % 2) * 64, (h % 2) * 64 + 64)
                fc = h // 2
                qsl = slice(fc * SEQ + qtg * 512, fc * SEQ + (qtg + 1) * 512)
                P.do("scalar", lambda e, hp=hp, qsl=qsl: e.activation(
                        q2b, qfm4[hp, qsl], AF.Square),
                     wait=[vec.mark, tns.mark], inc=scl, drain=True)
                P.do("tensor", lambda e: e.matmul(
                        ps_dn, cmcol, q2b, start=True, stop=True),
                     wait=[scl.mark, vec.mark, cm_ready], inc=tns)
                P.do("scalar", lambda e: e.activation(qsqrow, ps_dn, AF.Copy),
                     wait=[tns.mark], inc=scl, drain=True)
                smark = [scl.mark, scl.mark]
                for si in range(16):
                    pp = ps_s[si % 2]
                    P.do("tensor", lambda e, hp=hp, qsl=qsl, si=si, pp=pp, fc=fc:
                         e.matmul(pp, kfm4[hp, fc * SEQ + si * 128:
                                           fc * SEQ + (si + 1) * 128],
                                  qfm4[hp, qsl], start=True, stop=False),
                         wait=[smark[si % 2]])
                    P.do("tensor", lambda e, si=si, pp=pp: e.matmul(
                            pp, ctx.ones_row_bf[0:1, 0:128], qsqrow,
                            start=False, stop=True), inc=tns)
                    P.do("scalar", lambda e, si=si, h=h, pp=pp: e.activation(
                            pbuf[:, si * 512:(si + 1) * 512], pp, AF.Exp,
                            scale=0.025, bias=ksqb[:, si * 16 + h: si * 16 + h + 1]),
                         wait=[tns.mark], inc=scl, drain=False)
                    smark[si % 2] = scl.mark
                for si in range(16):
                    P.do("tensor", lambda e, si=si, h=h: e.matmul(
                            ps_pv, vtm4[:, si * 1040 + h * 65: si * 1040 + h * 65 + 65],
                            pbuf[:, si * 512:(si + 1) * 512],
                            start=(si == 0), stop=(si == 15)),
                         wait=([vec.mark, scl.mark] if si == 0 else []),
                         inc=(tns if si == 15 else None))
                P.do("vector", lambda e: e.tensor_copy(denrow, ps_pv[64:65, :]),
                     wait=[tns], inc=vec, drain=True)
                P.do("scalar", lambda e: e.activation(denrow, denrow, AF.Ln),
                     wait=[vec], drain=True)
                P.do("scalar", lambda e: e.activation(denrow, denrow, AF.Exp,
                                                      scale=-1.0),
                     inc=scl, drain=True)
                P.do("tensor", lambda e: e.matmul(
                        ps_rb, ctx.ones_f32row[0:1, 0:64], rcpden,
                        start=True, stop=True),
                     wait=[scl.mark], inc=tns)
                P.do("vector", lambda e: e.tensor_copy(rbsb, ps_rb),
                     wait=[tns], inc=vec, drain=True)
                P.do("vector", lambda e, hp=hp, fc=fc: e.tensor_tensor(
                        ofm[hp, fc * 512:(fc + 1) * 512], ps_pv[0:64, :], rbsb,
                        ALU.mult),
                     wait=[vec.mark], inc=vec, drain=True)
            for u in range(CPG):
                gi = s * 16 + qtg * CPG + u
                P.dma("sync", xt2, xBv[gi], dL, wait=[vec.mark])
                xld = dL.mark
                for n in range(2):
                    for c in range(8):
                        P.do("tensor", lambda e, u=u, n=n, c=c: e.matmul(
                                ps_y[:, n * 512:(n + 1) * 512],
                                ofm[:, c * 512 + u * 128: c * 512 + (u + 1) * 128],
                                woT[:, c * 1024 + n * 512: c * 1024 + (n + 1) * 512],
                                start=(c == 0), stop=(c == 7)),
                             wait=([vec.mark, wol] if (n == 0 and c == 0) else []),
                             inc=(tns if (n == 1 and c == 7) else None))
                P.do("vector", lambda e: e.tensor_tensor(xt2, ps_y, xt2, ALU.add),
                     wait=[tns, xld, dS.mark], inc=vec, drain=False)
                P.dma("sync", xCv[gi], xt2, dS, wait=[vec.mark])
        ctx.phase_end()
        ctx.pop_scope()



QT = 1024
NQ = NT // QT
CPQ = QT // 128


def emit_p5(ctx):
    nc, P = ctx.nc, ctx.P
    dL, dG, dS, dT, tns, vec, scl, pol = (ctx.dL, ctx.dG, ctx.dS, ctx.dT,
                                          ctx.tns, ctx.vec, ctx.scl, ctx.pol)

    wts = ctx.salloc("p5_wts", [128, NCH * 8], F32)
    xn5sc = nc.dram_tensor("p5_xnsc", [NCH, 128, 8, 128], BF16, kind="Internal").ap()

    ctx.push_scope()
    wgsb = ctx.salloc("p5_wg", [8, 1024], F32)
    wgT = ctx.salloc("p5_wgT", [128, 8 * 8], F32)
    bgrow = ctx.salloc("p5_bg", [1, 8], F32)
    scb = ctx.salloc("p5_sc", [128, 8], F32)
    m1 = ctx.salloc("p5_m1", [128, 1], F32)
    m2 = ctx.salloc("p5_m2", [128, 1], F32)
    ltm = ctx.salloc("p5_ltm", [128, 8], F32)
    s2b = ctx.salloc("p5_s2", [128, 8], F32)
    xt = ctx.salloc("p5_xt", [128, 1024], F32)
    z5 = ctx.salloc("p5_z5", [128, 1024], F32)
    zfm = ctx.salloc("p5_zfm", [128, 1024], F32)
    xnb = ctx.salloc("p5_xnb", [128, 8 * 128], BF16)
    ps_t = [ctx.palloc("p5_pst0", [128, 128]), ctx.palloc("p5_pst1", [128, 128])]
    ps_g = ctx.palloc("p5_psg", [128, 8])

    P.dma("sync", wgsb, ctx.I["L1_ffn_Wg"], dL)
    gl = dL.mark
    for c in range(8):
        P.do("tensor", lambda e, c=c: e.transpose(
                ctx.ps_misc[:, 0:8], wgsb[:, c * 128:(c + 1) * 128],
                ctx.id_f32[0:8, 0:8]),
             wait=[gl, ctx.consts_ready, vec.mark], inc=tns)
        P.do("vector", lambda e, c=c: e.tensor_copy(wgT[:, c * 8:(c + 1) * 8],
                                                    ctx.ps_misc[:, 0:8]),
             wait=[tns], inc=vec, drain=True)
    P.dma("sync", ctx.rowbuf[0:1, 0:8],
          ctx.I["L1_ffn_bg"].rearrange("(o d) -> o d", o=1), dL, wait=[tns.mark])
    P.dma("sync", ctx.rowbuf[0:1, 8:16],
          ctx.I["L1_ffn_bias"].rearrange("(o d) -> o d", o=1), dL)
    P.do("vector", lambda e: e.tensor_tensor(bgrow, ctx.rowbuf[0:1, 0:8],
                                             ctx.rowbuf[0:1, 8:16], ALU.add),
         wait=[dL], inc=vec, drain=True)
    gate_ready = (vec.mark, tns.mark)

    xCv = ctx.streams["xC"].rearrange("(n p) d -> n p d", p=128)
    outv = ctx.streams["out"].rearrange("(n p) d -> n p d", p=128)

    for ci in range(NCH):
        P.dma("sync", xt, xCv[ci], dL, wait=[scl.mark, vec.mark])
        ld = dL.mark
        emit_ln(ctx, xt, ctx.rstd, src_wait=[ld], dump=z5)
        P.do("vector", lambda e: e.tensor_scalar(
                z5, xt, ctx.meancol, ctx.rstd, ALU.subtract, ALU.mult),
             wait=[scl], inc=vec, drain=True)
        for c in range(8):
            par = c % 2
            P.do("tensor", lambda e, c=c, par=par: e.transpose(
                    ps_t[par], z5[:, c * 128:(c + 1) * 128], ctx.id_f32),
                 wait=[vec.mark, dS.mark], inc=tns)
            P.do("vector", lambda e, c=c, par=par: e.tensor_copy(
                    zfm[:, c * 128:(c + 1) * 128], ps_t[par]),
                 wait=[tns.mark], inc=vec, drain=False)
            P.do("vector", lambda e, c=c, par=par: e.tensor_copy(
                    xnb[:, c * 128:(c + 1) * 128], ps_t[par]),
                 wait=[tns.mark, dS.mark], inc=vec, drain=True)
        P.dma("sync", xn5sc[ci], xnb.rearrange("p (c t) -> p c t", c=8), dS,
              wait=[vec.mark])
        for c in range(8):
            P.do("tensor", lambda e, c=c: e.matmul(
                    ps_g, zfm[:, c * 128:(c + 1) * 128],
                    wgT[:, c * 8:(c + 1) * 8], start=(c == 0), stop=False),
                 wait=([vec.mark, gate_ready[0], gate_ready[1]] if c == 0 else []))
        P.do("tensor", lambda e: e.matmul(
                ps_g, ctx.ones_f32row[0:1, 0:128], bgrow,
                start=False, stop=True), inc=tns)
        P.do("scalar", lambda e: e.activation(scb, ps_g, AF.Sigmoid),
             wait=[tns.mark, vec.mark], inc=scl, drain=True)
        P.do("vector", lambda e: e.tensor_reduce(m1, scb, AX.X, ALU.max),
             wait=[scl], drain=True)
        P.do("vector", lambda e: e.tensor_scalar(ltm, scb, m1, None, ALU.is_lt),
             drain=True)
        P.do("vector", lambda e: e.tensor_tensor(s2b, scb, ltm, ALU.mult),
             drain=True)
        P.do("vector", lambda e: e.tensor_reduce(m2, s2b, AX.X, ALU.max),
             drain=True)
        P.do("vector", lambda e: e.tensor_scalar(ltm, scb, m2, None, ALU.is_ge),
             drain=True)
        P.do("vector", lambda e, ci=ci: e.tensor_tensor(
                wts[:, ci * 8:(ci + 1) * 8], scb, ltm, ALU.mult),
             inc=vec, drain=True)
    ctx.phase_end()
    ctx.pop_scope()

    xnfm5 = ctx.salloc("p5_xnfm", [128, 8 * QT], BF16)
    w1T = ctx.salloc("p5_w1T", [128, 8 * 4096], BF16)
    w2T = ctx.salloc("p5_w2T", [128, 32 * 1024], BF16)
    h5 = ctx.salloc("p5_h", [128, 32 * 512], BF16)
    acc5 = ctx.salloc("p5_acc", [128, CPQ * 1024], BF16)
    b1all = ctx.salloc("p5_b1c", [128, 9 * 32], F32)
    b2row = ctx.salloc("p5_b2r", [1, 1024], BF16)
    t2b = ctx.salloc("p5_t2", [128, 1024], BF16)
    xt2 = ctx.salloc("p5_xt2", [128, 1024], F32)
    outb = xt2
    ps_h = [ctx.palloc("p5_psh0", [128, 512]), ctx.palloc("p5_psh1", [128, 512])]
    ps_o = [ctx.palloc("p5_pso0", [128, 1024]), ctx.palloc("p5_pso1", [128, 1024])]

    w1names = ["L1_ffn_W1"] * 8 + ["L1_ffn_Ws1"]
    w2names = ["L1_ffn_W2"] * 8 + ["L1_ffn_Ws2"]

    def w_src(names, e_i):
        shared = (e_i == 8)
        return ctx.wsc[names[e_i]] if shared else ctx.wsc[names[e_i]][e_i]

    def emit_w1_loads(e_i):
        s_ap = w_src(w1names, e_i)
        for c in range(8):
            P.do("sync", lambda e, c=c, s_ap=s_ap: e.dma_start_transpose(
                    w1T[:, c * 4096:(c + 1) * 4096],
                    s_ap[:, c * 128:(c + 1) * 128]),
                 wait=([ctx.castdone[w1names[e_i]], tns.mark] if c == 0 else []),
                 inc=[(dL, 16)])
        return dL.mark

    def emit_w2_loads(e_i):
        s_ap = w_src(w2names, e_i)
        for gc in range(32):
            P.do("sync", lambda e, gc=gc, s_ap=s_ap: e.dma_start_transpose(
                    w2T[:, gc * 1024:(gc + 1) * 1024],
                    s_ap[:, gc * 128:(gc + 1) * 128]),
                 wait=([ctx.castdone[w2names[e_i]], tns.mark] if gc == 0 else []),
                 inc=[(dL, 16)])
        return dL.mark

    def emit_bias_loads(e_i):
        shared = (e_i == 8)
        b2vec = (ctx.I["L1_ffn_bs2"] if shared else ctx.I["L1_ffn_b2"][e_i])
        P.dma("gpsimd", b2row, b2vec.rearrange("(o d) -> o d", o=1), dG,
              wait=[tns.mark])
        return (vec.mark, dG.mark)

    for e_i in range(min(P5_NE, 9)):
        b1vec = (ctx.I["L1_ffn_bs1"] if e_i == 8 else ctx.I["L1_ffn_b1"][e_i])
        load_cols(ctx, b1vec, b1all[:, e_i * 32:(e_i + 1) * 32], 32)

    insts = [(q, e_i) for q in range(NQ) for e_i in range(P5_NE)]
    nmarks = {}
    if insts:
        nmarks[insts[0]] = (emit_w1_loads(insts[0][1]), emit_w2_loads(insts[0][1]),
                            emit_bias_loads(insts[0][1]))
    omark = [vec.mark, vec.mark]
    lastq = [-1]
    for idx, (q, e_i) in enumerate(insts):
        if q != lastq[0]:
            lastq[0] = q
            P.dma("sync",
                  xnfm5.rearrange("p (c cq t) -> p c cq t", c=8, cq=CPQ),
                  xn5sc[q * CPQ:(q + 1) * CPQ].rearrange("cq p c t -> p c cq t"),
                  dL, wait=[tns.mark])
        xnld = dL.mark
        w1m, w2m, bm = nmarks[(q, e_i)]
        nxt = insts[idx + 1] if idx + 1 < len(insts) else None
        hmark = [scl.mark, scl.mark]
        n1 = n2 = None
        for tg2 in range(2):
            for j in range(32):
                pb = ps_h[j % 2]
                for c in range(8):
                    P.do("tensor", lambda e, j=j, c=c, tg2=tg2, pb=pb: e.matmul(
                            pb, w1T[:, c * 4096 + j * 128: c * 4096 + (j + 1) * 128],
                            xnfm5[:, c * QT + tg2 * 512: c * QT + (tg2 + 1) * 512],
                            start=(c == 0), stop=(c == 7)),
                         wait=([w1m, bm[0], bm[1], xnld, hmark[j % 2]]
                               if c == 0 else []),
                         inc=(tns if c == 7 else None))
                P.do("scalar", lambda e, j=j, pb=pb, e_i=e_i: e.activation(
                        h5[:, j * 512:(j + 1) * 512], pb, AF.Gelu,
                        bias=b1all[:, e_i * 32 + j: e_i * 32 + j + 1]),
                     wait=[tns.mark], inc=scl, drain=False)
                hmark[j % 2] = scl.mark
            if tg2 == 1 and nxt is not None:
                n1 = emit_w1_loads(nxt[1])
            for u in range(CPG):
                cq = tg2 * CPG + u
                ci = q * CPQ + cq
                po = ps_o[u % 2]
                for n in range(2):
                    for j in range(32):
                        P.do("tensor", lambda e, u=u, n=n, j=j, po=po: e.matmul(
                                po[:, n * 512:(n + 1) * 512],
                                h5[:, j * 512 + u * 128: j * 512 + (u + 1) * 128],
                                w2T[:, j * 1024 + n * 512: j * 1024 + n * 512 + 512],
                                start=(j == 0), stop=False),
                             wait=([scl.mark, w2m, omark[u % 2]]
                                   if (n == 0 and j == 0) else []))
                    P.do("tensor", lambda e, n=n, po=po: e.matmul(
                            po[:, n * 512:(n + 1) * 512],
                            ctx.ones_row_bf[0:1, 0:128],
                            b2row[0:1, n * 512:(n + 1) * 512],
                            start=False, stop=True),
                         inc=(tns if n == 1 else None))
                t2s = t2b
                if e_i == 8:
                    P.do("vector", lambda e, po=po, t2s=t2s: e.tensor_copy(t2s, po),
                         wait=[tns, pol.mark], inc=vec, drain=True)
                else:
                    P.do("vector", lambda e, po=po, t2s=t2s, ci=ci, e_i=e_i:
                         e.tensor_scalar(
                            t2s, po, wts[:, ci * 8 + e_i: ci * 8 + e_i + 1],
                            None, ALU.mult),
                         wait=[tns, pol.mark], inc=vec, drain=True)
                omark[u % 2] = vec.mark
                asl = acc5[:, cq * 1024:(cq + 1) * 1024]
                if e_i == 0:
                    P.do("gpsimd", lambda e, asl=asl, t2s=t2s: e.tensor_copy(asl, t2s),
                         wait=[vec.mark], inc=pol, drain=True)
                else:
                    P.do("gpsimd", lambda e, asl=asl, t2s=t2s: e.tensor_tensor(
                            asl, asl, t2s, ALU.add),
                         wait=[vec.mark], inc=pol, drain=True)
        if nxt is not None:
            n2 = emit_w2_loads(nxt[1])
            nb = emit_bias_loads(nxt[1])
            nmarks[nxt] = (n1, n2, nb)
        if e_i == P5_NE - 1:
            for cq in range(CPQ):
                ci = q * CPQ + cq
                P.dma("sync", xt2, xCv[ci], dL, wait=[vec.mark])
                P.do("vector", lambda e, cq=cq: e.tensor_tensor(
                        outb, xt2, acc5[:, cq * 1024:(cq + 1) * 1024], ALU.add),
                     wait=[dL, dS.mark, pol.mark], inc=vec, drain=False)
                P.dma("sync", outv[ci], outb, dS, wait=[vec.mark])



INPUT_SPECS = {
    "L0_attn_g1": (D,), "L0_attn_b1": (D,),
    "L0_attn_Wqkv_a": (D, 3 * D), "L0_attn_bqkv_a": (3 * D,),
    "L0_attn_Wp_a": (D, D), "L0_attn_bp_a": (D,),
    "L0_attn_Wqkv_s": (D, 3 * D), "L0_attn_bqkv_s": (3 * D,),
    "L0_attn_Wo_s": (D, D), "L0_attn_bo_s": (D,),
    "L0_attn_Wo": (D, D), "L0_attn_bo": (D,),
    "L0_attn_gm": (D,), "L0_attn_bm": (D,),
    "L0_attn_W1": (D, 4 * D), "L0_attn_b1m": (4 * D,),
    "L0_attn_W2": (4 * D, D), "L0_attn_b2m": (D,),
    "L0_ffn_router_w": (32, D), "L0_ffn_router_b": (32,),
    "L0_ffn_weights1": (32, D, 256), "L0_ffn_bias1": (32, 256),
    "L0_ffn_weights2": (32, 256, D), "L0_ffn_bias2": (32, D),
    "L1_attn_Wq": (D, D), "L1_attn_gq": (1, 1, D),
    "L1_attn_Wk": (D, D), "L1_attn_gk": (1, 1, D),
    "L1_attn_Wv": (D, D), "L1_attn_gv": (1, 1, D),
    "L1_attn_Wo": (D, D),
    "L1_ffn_Wg": (8, D), "L1_ffn_bg": (8,), "L1_ffn_bias": (8,),
    "L1_ffn_W1": (8, 4 * D, D), "L1_ffn_b1": (8, 4 * D),
    "L1_ffn_W2": (8, D, 4 * D), "L1_ffn_b2": (8, D),
    "L1_ffn_Ws1": (4 * D, D), "L1_ffn_bs1": (4 * D,),
    "L1_ffn_Ws2": (D, 4 * D), "L1_ffn_bs2": (D,),
}

STREAM_DEFS = {
    "xin": ((NT, D), F32),
    "swa": ((NTG, 128, 8, TG), BF16),
    "xnf": ((NTG, 128, 8, TG), BF16),
    "x1": ((NT, D), F32),
    "xA": ((NT, D), F32),
    "xB": ((NT, D), F32),
    "xC": ((NT, D), F32),
    "out": ((NT, D), F32),
}

PHASE_IO = {
    "p1a": (["xin"], ["swa", "xnf"]),
    "p1bc": (["xin", "swa", "xnf"], ["x1"]),
    "p2": (["x1", "xin"], ["xA"]),
    "p3": (["xA"], ["xB"]),
    "p4": (["xB"], ["xC"]),
    "p5": (["xC"], ["out"]),
}

PHASE_WEIGHTS = {
    "p1a": ["L0_attn_g1", "L0_attn_b1", "L0_attn_Wqkv_a", "L0_attn_bqkv_a"],
    "p1bc": ["L0_attn_g1", "L0_attn_b1", "L0_attn_Wqkv_s", "L0_attn_bqkv_s",
             "L0_attn_Wp_a", "L0_attn_bp_a", "L0_attn_Wo_s", "L0_attn_bo_s",
             "L0_attn_Wo", "L0_attn_bo"],
    "p2": ["L0_attn_gm", "L0_attn_bm", "L0_attn_W1", "L0_attn_b1m",
           "L0_attn_W2", "L0_attn_b2m"],
    "p3": ["L0_ffn_router_w", "L0_ffn_router_b", "L0_ffn_weights1",
           "L0_ffn_bias1", "L0_ffn_weights2", "L0_ffn_bias2"],
    "p4": ["L1_attn_Wq", "L1_attn_gq", "L1_attn_Wk", "L1_attn_gk",
           "L1_attn_Wv", "L1_attn_gv", "L1_attn_Wo"],
    "p5": ["L1_ffn_Wg", "L1_ffn_bg", "L1_ffn_bias", "L1_ffn_W1", "L1_ffn_b1",
           "L1_ffn_W2", "L1_ffn_b2", "L1_ffn_Ws1", "L1_ffn_bs1",
           "L1_ffn_Ws2", "L1_ffn_bs2"],
}

PHASE_CASTS = {
    "p1a": ["L0_attn_Wqkv_a"],
    "p1bc": ["L0_attn_Wqkv_s", "L0_attn_Wp_a", "L0_attn_Wo_s", "L0_attn_Wo"],
    "p2": ["L0_attn_W1", "L0_attn_W2"],
    "p3": ["L0_ffn_weights1", "L0_ffn_weights2"],
    "p4": ["L1_attn_Wo"],
    "p5": ["L1_ffn_W1", "L1_ffn_W2", "L1_ffn_Ws1", "L1_ffn_Ws2"],
}

EMITTERS = {}


def build(phases, debug_streams=False):
    nc = bass.Bass("TRN2", target_bir_lowering=False, debug=False)
    ctx = Ctx()
    ctx.nc = nc
    ctx._phase_allocs = []
    P = Prog(nc)
    ctx.P = P

    wanted = set()
    for ph in phases:
        wanted.update(PHASE_WEIGHTS[ph])
    ctx.I = {n: nc.dram_tensor(n, list(INPUT_SPECS[n]), F32, kind="ExternalInput").ap()
             for n in sorted(wanted)}

    reads, writes = set(), set()
    for ph in phases:
        r, w = PHASE_IO[ph]
        reads.update(r)
        writes.update(w)
    ctx.streams = {}
    ctx.external_streams_in = []
    ctx.external_streams_out = []
    for sname in sorted(reads | writes):
        shape, dt = STREAM_DEFS[sname]
        if sname in writes:
            if sname == "out" or debug_streams or sname not in reads:
                kind = "ExternalOutput"
                ctx.external_streams_out.append(sname)
            else:
                kind = "Internal"
        else:
            kind = "ExternalInput"
            ctx.external_streams_in.append(sname)
        ctx.streams[sname] = nc.dram_tensor("s_" + sname, list(shape), dt, kind=kind).ap()

    sem_names = ["dL", "dG", "dS", "tns", "vec", "scl", "pol", "bar", "dT"]
    ctx._sem_guards = []
    def fresh_sems(tag, names):
        for snm in names:
            g = nc.semaphore(f"{snm}_{tag}")
            h = g.__enter__()
            ctx._sem_guards.append(g)
            setattr(ctx, snm, S(h))
    fresh_sems("g", sem_names)

    perm = []
    def psalloc(name, shape, dt):
        g = nc.sbuf_tensor(name, shape, dt)
        h = g.__enter__()
        perm.append(g)
        return h.ap()
    ctx.id_f32 = psalloc("id_f32", [128, 128], F32)
    ctx.id_bf = psalloc("id_bf", [128, 128], BF16)
    ctx.ones_row_bf = psalloc("ones_row_bf", [1, 512], BF16)
    ctx.ones_col_bf = psalloc("ones_col_bf", [128, 1], BF16)
    ctx.ones_col_f32 = psalloc("ones_col_f32", [1, 1], F32)
    ctx.sqsum = psalloc("sqsum", [128, 1], F32)
    ctx.sumcol = psalloc("sumcol", [128, 1], F32)
    ctx.meancol = psalloc("meancol", [128, 1], F32)
    ctx.msq = psalloc("msq", [128, 1], F32)
    ctx.varcol = psalloc("varcol", [128, 1], F32)
    ctx.lnv = psalloc("lnv", [128, 1], F32)
    ctx.rstd = psalloc("rstd", [128, 1], F32)
    ctx.rowbuf = psalloc("rowbuf", [1, 512], F32)
    ctx.epscol = psalloc("epscol", [128, 1], F32)
    ctx.ones_f32row = psalloc("ones_f32row", [1, 128], F32)
    gp = nc.psum_tensor("ps_misc", [128, 512], F32)
    ctx.ps_misc = gp.__enter__().ap()
    perm.append(gp)
    ctx._perm = perm

    P.do("gpsimd", lambda e: e.memset(ctx.id_f32, 0.0), drain=True)
    P.do("gpsimd", lambda e: e.affine_select(out=ctx.id_f32, in_=ctx.id_f32,
            compare_op=ALU.not_equal, fill=1.0, base=0, pattern=[[-1, 128]],
            channel_multiplier=1), drain=True)
    P.do("gpsimd", lambda e: e.memset(ctx.id_bf, 0.0), drain=True)
    P.do("gpsimd", lambda e: e.affine_select(out=ctx.id_bf, in_=ctx.id_bf,
            compare_op=ALU.not_equal, fill=1.0, base=0, pattern=[[-1, 128]],
            channel_multiplier=1), drain=True)
    P.do("gpsimd", lambda e: e.memset(ctx.ones_row_bf, 1.0), drain=True)
    P.do("gpsimd", lambda e: e.memset(ctx.ones_col_bf, 1.0), drain=True)
    P.do("gpsimd", lambda e: e.memset(ctx.ones_col_f32, 1.0), drain=True)
    P.do("gpsimd", lambda e: e.memset(ctx.epscol, 1e-5), drain=True)
    P.do("gpsimd", lambda e: e.memset(ctx.ones_f32row, 1.0), inc=ctx.pol, drain=True)
    ctx.consts_ready = ctx.pol.mark

    ctx.wsc = {}
    ctx.castdone = {}
    fresh_sems("c", ["dC"])

    def emit_casts(ph):
        for wn in PHASE_CASTS.get(ph, []):
            if wn in ctx.wsc:
                continue
            shape = list(INPUT_SPECS[wn])
            ctx.wsc[wn] = nc.dram_tensor("wsc_" + wn, shape, BF16, kind="Internal").ap()
            s_ap = ctx.I[wn]
            d_ap = ctx.wsc[wn]
            if len(shape) == 3:
                s_ap = s_ap.rearrange("a b c -> (a b) c")
                d_ap = d_ap.rearrange("a b c -> (a b) c")
            P.dma("gpsimd", d_ap, s_ap, ctx.dC)
            ctx.castdone[wn] = ctx.dC.mark

    emit_casts(phases[0])
    for i, ph in enumerate(phases):
        fresh_sems(ph, ["dL", "dG", "dS", "tns", "vec", "scl", "pol", "dT"])
        EMITTERS[ph](ctx)
        if i == 0:
            for ph2 in phases[1:]:
                emit_casts(ph2)
        ctx.phase_end()
        ctx.free_phase_allocs()

    P.emit()
    return nc, ctx



def run_phases(phases, inputs, extra_streams=None, core_ids=None,
               debug_streams=False):
    core_ids = core_ids if core_ids is not None else list(range(NCORES))
    nc, ctx = build(phases, debug_streams=debug_streams)
    x = np.ascontiguousarray(inputs["L0_attn_x"], dtype=np.float32)
    in_maps = []
    for i, c in enumerate(core_ids):
        m = {}
        for n in ctx.I:
            m[n] = np.ascontiguousarray(inputs[n], dtype=np.float32)
        for sn in ctx.external_streams_in:
            if sn == "xin":
                m["s_xin"] = x[2 * c:2 * c + 2].reshape(NT, D)
            else:
                m["s_" + sn] = extra_streams[sn][i]
        if "xin" in ctx.streams and "s_xin" not in m:
            m["s_xin"] = x[2 * c:2 * c + 2].reshape(NT, D)
        in_maps.append(m)
    res = run_bass_kernel_spmd(nc, in_maps, core_ids=core_ids)
    return res, ctx


def kernel(**inputs):
    if FUSED:
        phases = ["p1a", "p1bc", "p2", "p3", "p4", "p5"]
        res, ctx = run_phases(phases, inputs)
        outs = [r["s_out"] for r in res.results]
    else:
        carry = {}
        ctx = None
        for g in PHASE_GROUPS:
            res, ctx = run_phases(g, inputs, extra_streams=carry,
                                  debug_streams=True)
            for sn in ctx.external_streams_out:
                carry[sn] = [r["s_" + sn] for r in res.results]
        outs = carry["out"]
    full = np.concatenate([o.reshape(2, SEQ, D) for o in outs], axis=0)
    return full.astype(np.float32)


EMITTERS["p2"] = emit_p2
EMITTERS["p3"] = emit_p3
EMITTERS["p5"] = emit_p5
EMITTERS["p1a"] = emit_p1a
EMITTERS["p1bc"] = emit_p1bc
EMITTERS["p4"] = emit_p4


# version 07 (final)
# speedup vs baseline: 1.2370x; 1.0070x over previous
import numpy as np
import ml_dtypes
import concourse.bass as bass
import concourse.mybir as mybir
from concourse.bass_utils import run_bass_kernel_spmd

F32 = mybir.dt.float32
BF16 = mybir.dt.bfloat16
AF = mybir.ActivationFunctionType
ALU = mybir.AluOpType
AX = mybir.AxisListType

NCORES = 8
D = 1024
NT = 4096
SEQ = 2048
NCH = NT // 128
TG = 512
NTG = NT // TG
CPG = TG // 128
NH = 16
EH = 64

ENGINES = ("sync", "tensor", "vector", "scalar", "gpsimd")

P3_NE = 32
P5_NE = 9
P1B_DBG = False

def dbg_ufm_hook(ctx, gidx):
    pass
P5_NOACC = False

FUSED = True
PHASE_GROUPS = [["p1a"], ["p1bc"], ["p2"], ["p3"], ["p4"], ["p5"]]


class S:
    def __init__(self, h):
        self.h = h
        self.n = 0

    @property
    def mark(self):
        return (self, self.n)


class Prog:
    def __init__(self, nc):
        self.nc = nc
        self.entries = {e: [] for e in ENGINES}

    def do(self, eng, fn, wait=(), inc=(), drain=False):
        w = []
        for x in wait:
            if isinstance(x, S):
                w.append((x.h, x.n))
            else:
                s, v = x
                assert v <= s.n, f"wait value {v} > recorded count {s.n}"
                if v > 0:
                    w.append((s.h, v))
        ii = []
        lst = inc if isinstance(inc, (list, tuple)) and not (
            len(inc) == 2 and isinstance(inc[0], S) and isinstance(inc[1], int)) else [inc]
        for x in lst:
            if x is None:
                continue
            if isinstance(x, S):
                x.n += 1
                ii.append((x.h, 1))
            else:
                s, n = x
                s.n += n
                ii.append((s.h, n))
        self.entries[eng].append((w, fn, ii, drain))

    def dma(self, eng, out, in_, sem, wait=(), **kw):
        self.do(eng, lambda e: e.dma_start(out, in_, **kw), wait=wait, inc=[(sem, 16)])

    def emit(self):
        with self.nc.Block() as block:
            for eng in ENGINES:
                entries = self.entries[eng]
                if not entries:
                    continue

                def body(e, entries=entries):
                    for w, fn, ii, drain in entries:
                        for h, v in w:
                            e.wait_ge(h, v)
                        inst = fn(e)
                        for h, n in ii:
                            inst.then_inc(h, n)
                        if drain:
                            e.drain()

                getattr(block, eng)(body)


class Ctx:
    def salloc(self, name, shape, dt):
        g = self.nc.sbuf_tensor(name, shape, dt)
        h = g.__enter__()
        self._phase_allocs.append(g)
        return h.ap()

    def palloc(self, name, shape, dt=F32):
        g = self.nc.psum_tensor(name, shape, dt)
        h = g.__enter__()
        self._phase_allocs.append(g)
        return h.ap()

    def free_phase_allocs(self):
        for g in reversed(self._phase_allocs):
            g.__exit__(None, None, None)
        self._phase_allocs = []

    def push_scope(self):
        self._scope_marks = getattr(self, "_scope_marks", [])
        self._scope_marks.append(len(self._phase_allocs))

    def pop_scope(self):
        k = self._scope_marks.pop()
        for g in reversed(self._phase_allocs[k:]):
            g.__exit__(None, None, None)
        del self._phase_allocs[k:]

    def phase_end(self):
        P = self.P
        P.do("sync", lambda e: e.nop(),
             wait=[self.dL, self.dG, self.dS])
        for e in ENGINES:
            P.do(e, lambda en: en.nop(), inc=self.bar)
        tgt = self.bar.n
        for e in ENGINES:
            P.do(e, lambda en: en.nop(), wait=[(self.bar, tgt)])



def load_cols(ctx, vec_ap, cols_dst, n128):
    P = ctx.P
    off = 0
    while off < n128:
        k = min(4, n128 - off)
        P.dma("sync", ctx.rowbuf[0:1, 0:k * 128],
              vec_ap[off * 128:(off + k) * 128].rearrange("(o d) -> o d", o=1), ctx.dL,
              wait=[ctx.tns.mark])
        for c in range(k):
            P.do("tensor",
                 lambda e, c=c, off=off: e.matmul(
                     ctx.ps_misc[:, off + c: off + c + 1],
                     ctx.rowbuf[0:1, c * 128:(c + 1) * 128],
                     ctx.ones_col_f32[0:1, 0:1], start=True, stop=True),
                 wait=[ctx.dL.mark, ctx.consts_ready, ctx.vec.mark], inc=ctx.tns)
        off += k
    P.do("vector", lambda e: e.tensor_copy(cols_dst, ctx.ps_misc[:, 0:n128]),
         wait=[ctx.tns], inc=ctx.vec, drain=True)


def emit_ln(ctx, src, rstd_out, src_wait=(), dump=None):
    P = ctx.P
    N = src.shape[-1]
    P.do("scalar", lambda e: e.activation(dump[:, 0:N], src, AF.Square,
                                          accum_out=ctx.sqsum),
         wait=list(src_wait) + [ctx.vec.mark], inc=ctx.scl, drain=True)
    P.do("vector", lambda e: e.tensor_reduce(ctx.sumcol, src, AX.X, ALU.add),
         wait=list(src_wait) + [ctx.scl.mark], drain=True)
    P.do("vector", lambda e: e.tensor_scalar(ctx.meancol, ctx.sumcol, 1.0 / N, None,
                                             ALU.mult), drain=True)
    P.do("vector", lambda e: e.tensor_tensor(ctx.msq, ctx.meancol, ctx.meancol, ALU.mult),
         drain=True)
    P.do("vector", lambda e: e.scalar_tensor_tensor(ctx.varcol, ctx.sqsum, 1.0 / N,
                                                    ctx.msq, ALU.mult, ALU.subtract),
         wait=[ctx.scl], inc=ctx.vec, drain=True)
    P.do("scalar", lambda e: e.activation(ctx.lnv, ctx.varcol, AF.Ln, bias=ctx.epscol),
         wait=[ctx.vec, ctx.consts_ready], drain=True)
    P.do("scalar", lambda e: e.activation(rstd_out, ctx.lnv, AF.Exp, scale=-0.5),
         inc=ctx.scl, drain=True)



def emit_p2(ctx):
    nc, P = ctx.nc, ctx.P
    dL, dG, dS, tns, vec, scl = ctx.dL, ctx.dG, ctx.dS, ctx.tns, ctx.vec, ctx.scl

    w1sb = ctx.salloc("p2_w1", [128, 8 * 4096], BF16)
    w2sb = ctx.salloc("p2_w2", [128, 32 * 1024], BF16)
    b2row = ctx.salloc("p2_b2", [1, 1024], BF16)
    x1t = ctx.salloc("p2_x1t", [128, CPG * 1024], F32)
    z1 = ctx.salloc("p2_z1", [128, 1024], F32)
    xnfm = ctx.salloc("p2_xnfm", [128, 8 * TG], BF16)
    hsb = ctx.salloc("p2_h", [128, 32 * TG], BF16)
    outb = ctx.salloc("p2_out", [128, 1024], F32)
    gmcols = ctx.salloc("p2_gmc", [128, 8], F32)
    bmcols = ctx.salloc("p2_bmc", [128, 8], F32)
    b1cols = ctx.salloc("p2_b1c", [128, 32], F32)

    ps_t0 = ctx.palloc("p2_pst0", [128, 128])
    ps_t = [ps_t0, ps_t0]
    ps_h = [ctx.palloc("p2_psh0", [128, 512]), ctx.palloc("p2_psh1", [128, 512])]
    ps_o = [ctx.palloc("p2_pso0", [128, 1024]), ctx.palloc("p2_pso1", [128, 1024])]

    P.dma("sync", w1sb.rearrange("p (c g) -> p c g", c=8),
          ctx.wsc["L0_attn_W1"].rearrange("(c p) g -> p c g", p=128), dL,
          wait=[ctx.castdone["L0_attn_W1"]])
    P.dma("sync", w2sb.rearrange("p (j d) -> p j d", j=32),
          ctx.wsc["L0_attn_W2"].rearrange("(j p) d -> p j d", p=128), dL,
          wait=[ctx.castdone["L0_attn_W2"]])
    P.dma("gpsimd", b2row, ctx.I["L0_attn_b2m"].rearrange("(o d) -> o d", o=1), dG)
    load_cols(ctx, ctx.I["L0_attn_gm"], gmcols, 8)
    load_cols(ctx, ctx.I["L0_attn_bm"], bmcols, 8)
    load_cols(ctx, ctx.I["L0_attn_b1m"], b1cols, 32)
    cols_ready = ctx.vec.mark
    wload = dL.mark

    x1v = ctx.streams["x1"].rearrange("(g u p) d -> g p u d", p=128, u=CPG)
    xAv = ctx.streams["xA"].rearrange("(g u p) d -> g p u d", p=128, u=CPG)
    xinv = ctx.streams["xin"].rearrange("(g u p) d -> g p u d", p=128, u=CPG)
    xst = ctx.salloc("p2_xst", [128, 1024], F32)

    for tg in range(NTG):
        P.dma("sync", x1t.rearrange("p (u d) -> p u d", u=CPG), x1v[tg], dL,
              wait=[vec.mark])
        ld = dL.mark
        for u in range(CPG):
            x1c = x1t[:, u * 1024:(u + 1) * 1024]
            emit_ln(ctx, x1c, ctx.rstd, src_wait=[ld], dump=z1)
            P.do("vector", lambda e, x1c=x1c: e.tensor_scalar(
                    z1, x1c, ctx.meancol, ctx.rstd, ALU.subtract, ALU.mult),
                 wait=[scl], inc=vec, drain=True)
            for c in range(8):
                par = c % 2
                P.do("tensor", lambda e, c=c, par=par: e.transpose(
                        ps_t[par], z1[:, c * 128:(c + 1) * 128], ctx.id_f32),
                     wait=[vec.mark, scl.mark], inc=tns)
                P.do("vector", lambda e, c=c, u=u, par=par: e.tensor_scalar(
                        xnfm[:, c * TG + u * 128: c * TG + (u + 1) * 128],
                        ps_t[par], gmcols[:, c:c + 1], bmcols[:, c:c + 1],
                        ALU.mult, ALU.add),
                     wait=[tns.mark], inc=vec, drain=False)
        hmark = [scl.mark, scl.mark]
        xn_ready = (vec.mark, scl.mark)
        for j in range(32):
            pb = ps_h[j % 2]
            for c in range(8):
                P.do("tensor", lambda e, j=j, c=c, pb=pb: e.matmul(
                        pb, w1sb[:, c * 4096 + j * 128: c * 4096 + (j + 1) * 128],
                        xnfm[:, c * TG:(c + 1) * TG],
                        start=(c == 0), stop=(c == 7)),
                     wait=([xn_ready[0], xn_ready[1], wload, hmark[j % 2]]
                           if c == 0 else []),
                     inc=(tns if c == 7 else None))
            P.do("scalar", lambda e, j=j, pb=pb: e.activation(
                    hsb[:, j * TG:(j + 1) * TG], pb, AF.Gelu,
                    bias=b1cols[:, j:j + 1]),
                 wait=[tns.mark, cols_ready], inc=scl, drain=False)
            hmark[j % 2] = scl.mark
        omark = [vec.mark, vec.mark]
        for u in range(CPG):
            po = ps_o[u % 2]
            xs = xst
            P.dma("sync", xs, xinv[tg][:, u], dL, wait=[ctx.pol.mark])
            xld = dL.mark
            for n in range(2):
                for j in range(32):
                    P.do("tensor", lambda e, u=u, n=n, j=j, po=po: e.matmul(
                            po[:, n * 512:(n + 1) * 512],
                            hsb[:, j * TG + u * 128: j * TG + (u + 1) * 128],
                            w2sb[:, j * 1024 + n * 512: j * 1024 + n * 512 + 512],
                            start=(j == 0), stop=False),
                         wait=([scl.mark, omark[u % 2], wload]
                               if (n == 0 and j == 0) else []))
                P.do("tensor", lambda e, n=n, po=po: e.matmul(
                        po[:, n * 512:(n + 1) * 512], ctx.ones_row_bf[0:1, 0:128],
                        b2row[0:1, n * 512:(n + 1) * 512], start=False, stop=True),
                     inc=(tns if n == 1 else None))
            P.do("vector", lambda e, u=u, po=po: e.scalar_tensor_tensor(
                    outb, po, 1.0, x1t[:, u * 1024:(u + 1) * 1024], ALU.mult, ALU.add),
                 wait=[tns, dS.mark], inc=vec, drain=False)
            omark[u % 2] = vec.mark
            P.do("gpsimd", lambda e: e.tensor_tensor(outb, outb, xs, ALU.add),
                 wait=[vec.mark, xld], inc=ctx.pol, drain=True)
            P.dma("sync", xAv[tg][:, u], outb, dS, wait=[ctx.pol.mark])



def emit_p1a(ctx):
    nc, P = ctx.nc, ctx.P
    dL, dG, dS, tns, vec, scl, pol = (ctx.dL, ctx.dG, ctx.dS, ctx.tns, ctx.vec,
                                      ctx.scl, ctx.pol)

    wq = ctx.salloc("p1a_wq", [128, 8 * 3072], BF16)
    bqrow = ctx.salloc("p1a_bq", [1, 3072], BF16)
    g1cols = ctx.salloc("p1a_g1c", [128, 8], F32)
    b1cols = ctx.salloc("p1a_b1c", [128, 8], F32)
    kfm = ctx.salloc("p1a_kfm", [128, 8 * SEQ], BF16)
    vtm = ctx.salloc("p1a_vtm", [128, 16 * 1024], BF16)
    qfm = ctx.salloc("p1a_qfm", [128, 8 * TG], BF16)
    xnfm = ctx.salloc("p1a_xnfm", [128, 8 * TG], BF16)
    sout = ctx.salloc("p1a_sout", [128, 8 * TG], BF16)
    xt = ctx.salloc("p1a_xt", [128, CPG * 1024], F32)
    zb = ctx.salloc("p1a_zb", [128, 1024], F32)
    z2 = ctx.salloc("p1a_z2", [128, 1024], F32)
    pf = ctx.salloc("p1a_pf", [128, 2 * 256], F32)
    pm = ctx.salloc("p1a_pm", [128, 2 * 256], F32)
    pbf = ctx.salloc("p1a_pbf", [128, 2 * 256], BF16)
    ptsb = ctx.salloc("p1a_ptsb", [128, 2 * 256], BF16)
    rmax = ctx.salloc("p1a_rmax", [128, 2], F32)
    rmneg = ctx.salloc("p1a_rmneg", [128, 2], F32)
    rs = ctx.salloc("p1a_rs", [128, 2], F32)
    rcp = ctx.salloc("p1a_rcp", [128, 2], F32)
    mgen = ctx.salloc("p1a_mgen", [128, 256], F32)
    mb0 = ctx.salloc("p1a_mb0", [128, 128], F32)

    ps_q = [ctx.palloc("p1a_psq0", [128, 512])]
    ps_q.append(ps_q[0])
    ps_v = ctx.palloc("p1a_psv", [128, 1024])
    ps_sm = [ctx.palloc("p1a_pssm0", [128, 256]), ctx.palloc("p1a_pssm1", [128, 256])]
    ps_pt = ctx.palloc("p1a_pspt", [128, 256], BF16)
    ps_pv = ctx.palloc("p1a_pspv", [64, 128])

    P.do("gpsimd", lambda e: e.memset(mgen, 1.0), drain=True)
    P.do("gpsimd", lambda e: e.affine_select(out=mgen, in_=mgen,
            compare_op=ALU.is_ge, fill=0.0, base=128, pattern=[[-1, 256]],
            channel_multiplier=1), drain=True)
    P.do("gpsimd", lambda e: e.affine_select(out=mgen, in_=mgen,
            compare_op=ALU.is_ge, fill=0.0, base=-1, pattern=[[1, 256]],
            channel_multiplier=-1), drain=True)
    P.do("gpsimd", lambda e: e.memset(mb0, 1.0), drain=True)
    P.do("gpsimd", lambda e: e.affine_select(out=mb0, in_=mb0,
            compare_op=ALU.is_ge, fill=0.0, base=0, pattern=[[-1, 128]],
            channel_multiplier=1), inc=pol, drain=True)
    masks_ready = pol.mark

    P.dma("sync", wq.rearrange("p (c f) -> p c f", c=8),
          ctx.wsc["L0_attn_Wqkv_a"].rearrange("(c p) f -> p c f", p=128), dL,
          wait=[ctx.castdone["L0_attn_Wqkv_a"]])
    P.dma("gpsimd", bqrow, ctx.I["L0_attn_bqkv_a"].rearrange("(o f) -> o f", o=1), dG)
    load_cols(ctx, ctx.I["L0_attn_g1"], g1cols, 8)
    load_cols(ctx, ctx.I["L0_attn_b1"], b1cols, 8)
    wld = (dL.mark, dG.mark, vec.mark)

    xinv = ctx.streams["xin"].rearrange("(g u p) d -> g p u d", p=128, u=CPG)
    swav = ctx.streams["swa"]
    xnfv = ctx.streams["xnf"]

    for s in range(2):
        for g4 in range(4):
            gidx = s * 4 + g4
            P.dma("sync", xt.rearrange("p (u d) -> p u d", u=CPG), xinv[gidx], dL,
                  wait=[scl.mark, vec.mark])
            ld = dL.mark
            for u in range(CPG):
                xc = xt[:, u * 1024:(u + 1) * 1024]
                emit_ln(ctx, xc, ctx.rstd, src_wait=[ld], dump=zb)
                P.do("vector", lambda e, xc=xc: e.tensor_scalar(
                        zb, xc, ctx.meancol, ctx.rstd, ALU.subtract, ALU.mult),
                     wait=[scl], inc=vec, drain=True)
                emit_ln(ctx, zb, ctx.rstd, src_wait=[], dump=z2)
                P.do("vector", lambda e: e.tensor_scalar(
                        z2, zb, ctx.meancol, ctx.rstd, ALU.subtract, ALU.mult),
                     wait=[scl], inc=vec, drain=True)
                for c in range(8):
                    P.do("tensor", lambda e, c=c: e.transpose(
                            ps_sm[c % 2][:, 0:128],
                            z2[:, c * 128:(c + 1) * 128], ctx.id_f32),
                         wait=[vec.mark, scl.mark, ctx.consts_ready], inc=tns)
                    P.do("vector", lambda e, c=c, u=u: e.tensor_scalar(
                            xnfm[:, c * TG + u * 128: c * TG + (u + 1) * 128],
                            ps_sm[c % 2][:, 0:128],
                            g1cols[:, c:c + 1], b1cols[:, c:c + 1],
                            ALU.mult, ALU.add),
                         wait=[tns.mark, wld[2], dS.mark], inc=vec, drain=True)
            P.dma("sync", xnfv[gidx], xnfm.rearrange("p (c t) -> p c t", c=8), dS,
                  wait=[vec.mark])
            for fc in range(8):
                pq = ps_q[fc % 2]
                for c in range(8):
                    P.do("tensor", lambda e, fc=fc, c=c, pq=pq: e.matmul(
                            pq, wq[:, c * 3072 + fc * 128: c * 3072 + (fc + 1) * 128],
                            xnfm[:, c * TG:(c + 1) * TG],
                            start=(c == 0), stop=False),
                         wait=([vec.mark, scl.mark, wld[0], wld[1]] if c == 0 else []))
                P.do("tensor", lambda e, fc=fc, pq=pq: e.matmul(
                        pq, bqrow[0:1, fc * 128:(fc + 1) * 128],
                        ctx.ones_row_bf[0:1, 0:TG], start=False, stop=True), inc=tns)
                P.do("scalar", lambda e, fc=fc, pq=pq: e.activation(
                        qfm[:, fc * TG:(fc + 1) * TG], pq, AF.Copy),
                     wait=[tns.mark], inc=scl, drain=False)
            for fc in range(8):
                pq = ps_q[fc % 2]
                for c in range(8):
                    P.do("tensor", lambda e, fc=fc, c=c, pq=pq: e.matmul(
                            pq, wq[:, c * 3072 + 1024 + fc * 128:
                                   c * 3072 + 1024 + (fc + 1) * 128],
                            xnfm[:, c * TG:(c + 1) * TG],
                            start=(c == 0), stop=False),
                         wait=([scl.mark] if c == 0 else []))
                P.do("tensor", lambda e, fc=fc, pq=pq: e.matmul(
                        pq, bqrow[0:1, 1024 + fc * 128: 1024 + (fc + 1) * 128],
                        ctx.ones_row_bf[0:1, 0:TG], start=False, stop=True), inc=tns)
                P.do("scalar", lambda e, fc=fc, g4=g4, pq=pq: e.activation(
                        kfm[:, fc * SEQ + g4 * TG: fc * SEQ + (g4 + 1) * TG],
                        pq, AF.Copy),
                     wait=[tns.mark], inc=scl, drain=False)
            for u in range(CPG):
                cs = g4 * CPG + u
                for n in range(2):
                    for c in range(8):
                        P.do("tensor", lambda e, u=u, n=n, c=c: e.matmul(
                                ps_v[:, n * 512:(n + 1) * 512],
                                xnfm[:, c * TG + u * 128: c * TG + (u + 1) * 128],
                                wq[:, c * 3072 + 2048 + n * 512:
                                   c * 3072 + 2048 + (n + 1) * 512],
                                start=(c == 0), stop=False),
                             wait=([vec.mark, scl.mark] if (n == 0 and c == 0) else []))
                    P.do("tensor", lambda e, n=n: e.matmul(
                            ps_v[:, n * 512:(n + 1) * 512],
                            ctx.ones_row_bf[0:1, 0:128],
                            bqrow[0:1, 2048 + n * 512: 2048 + (n + 1) * 512],
                            start=False, stop=True),
                         inc=(tns if n == 1 else None))
                P.do("vector", lambda e, cs=cs: e.tensor_copy(
                        vtm[:, cs * 1024:(cs + 1) * 1024], ps_v),
                     wait=[tns], inc=vec, drain=True)
            smark = [scl.mark, scl.mark]
            state = {"ptmark": scl.mark, "pvmark": scl.mark, "tail": None}

            def emit_tail(t):
                (u, h, par, W2, klo, pbf_mark) = t
                nw = W2 // 128
                hp = slice((h % 2) * 64, (h % 2) * 64 + 64)
                po = slice(par * 256, par * 256 + W2)
                for w in range(nw):
                    P.do("tensor", lambda e, w=w, par=par: e.transpose(
                            ps_pt[:, w * 128:(w + 1) * 128],
                            pbf[:, par * 256 + w * 128: par * 256 + (w + 1) * 128],
                            ctx.id_bf),
                         wait=([pbf_mark, state["ptmark"]] if w == 0 else []),
                         inc=(tns if w == nw - 1 else None))
                P.do("scalar", lambda e, W2=W2, po=po: e.copy(
                        ptsb[:, po], ps_pt[:, 0:W2]),
                     wait=[tns.mark, state["pvmark"]], inc=scl, drain=True)
                state["ptmark"] = scl.mark
                for w in range(nw):
                    kv = klo // 128 + w
                    P.do("tensor", lambda e, kv=kv, h=h, w=w, par=par: e.matmul(
                            ps_pv, vtm[:, kv * 1024 + h * 64: kv * 1024 + h * 64 + 64],
                            ptsb[:, par * 256 + w * 128: par * 256 + (w + 1) * 128],
                            start=(w == 0), stop=(w == nw - 1)),
                         wait=([scl.mark, vec.mark] if w == 0 else []),
                         inc=(tns if w == nw - 1 else None))
                P.do("scalar", lambda e, h=h, hp=hp, u=u: e.copy(
                        sout[hp, (h // 2) * TG + u * 128:
                             (h // 2) * TG + (u + 1) * 128], ps_pv),
                     wait=[tns.mark, dS.mark], inc=scl, drain=True)
                state["pvmark"] = scl.mark

            for u in range(CPG):
                cs = g4 * CPG + u
                b0 = (cs == 0)
                W2 = 128 if b0 else 256
                for h in range(NH):
                    par = h % 2
                    sm = ps_sm[par]
                    po = slice(par * 256, par * 256 + W2)
                    pc = slice(par, par + 1)
                    hp = slice((h % 2) * 64, (h % 2) * 64 + 64)
                    fc = h // 2
                    klo = (cs - (0 if b0 else 1)) * 128
                    P.do("tensor", lambda e, u=u, h=h, hp=hp, fc=fc, klo=klo, W2=W2,
                         sm=sm: e.matmul(sm[:, 0:W2],
                                  qfm[hp, fc * TG + u * 128: fc * TG + (u + 1) * 128],
                                  kfm[hp, fc * SEQ + klo: fc * SEQ + klo + W2],
                                  start=True, stop=True),
                         wait=[smark[par]], inc=tns)
                    P.do("vector", lambda e, W2=W2, sm=sm, pc=pc: e.tensor_reduce(
                            rmax[:, pc], sm[:, 0:W2], AX.X, ALU.max),
                         wait=[tns.mark], inc=vec, drain=True)
                    P.do("scalar", lambda e, pc=pc: e.mul(rmneg[:, pc], rmax[:, pc],
                                                          -0.125),
                         wait=[vec.mark], drain=True)
                    P.do("scalar", lambda e, W2=W2, sm=sm, po=po, pc=pc: e.activation(
                            pf[:, po], sm[:, 0:W2], AF.Exp,
                            bias=rmneg[:, pc], scale=0.125),
                         inc=scl, drain=True)
                    smark[par] = scl.mark
                    P.do("vector", lambda e, W2=W2, b0=b0, po=po: e.tensor_tensor(
                            pm[:, po], pf[:, po],
                            (mb0 if b0 else mgen)[:, 0:W2], ALU.mult),
                         wait=[scl.mark, masks_ready], drain=True)
                    P.do("vector", lambda e, W2=W2, po=po, pc=pc: e.tensor_reduce(
                            rs[:, pc], pm[:, po], AX.X, ALU.add), drain=True)
                    P.do("vector", lambda e, pc=pc: e.reciprocal(rcp[:, pc], rs[:, pc]),
                         drain=True)
                    P.do("vector", lambda e, W2=W2, po=po, pc=pc: e.tensor_scalar(
                            pbf[:, po], pm[:, po], rcp[:, pc], None, ALU.mult),
                         inc=vec, drain=True)
                    newtail = (u, h, par, W2, klo, vec.mark)
                    if state["tail"] is not None:
                        emit_tail(state["tail"])
                    state["tail"] = newtail
            if state["tail"] is not None:
                emit_tail(state["tail"])
                state["tail"] = None
            P.dma("sync", swav[gidx], sout.rearrange("p (c t) -> p c t", c=8), dS,
                  wait=[scl.mark, vec.mark])



def emit_p1bc(ctx):
    nc, P = ctx.nc, ctx.P
    dL, dG, dS, tns, vec, scl, pol = (ctx.dL, ctx.dG, ctx.dS, ctx.tns, ctx.vec,
                                      ctx.scl, ctx.pol)

    wqs = ctx.salloc("p1b_wqs", [128, 8 * 3072], BF16)
    wp = ctx.salloc("p1b_wp", [128, 8 * 1024], BF16)
    wos = ctx.salloc("p1b_wos", [128, 8 * 1024], BF16)
    wo = ctx.salloc("p1b_wo", [128, 8 * 1024], BF16)
    bqrow = ctx.salloc("p1b_bq", [1, 3072], BF16)
    bpbo = ctx.salloc("p1b_bpbo", [1, 1024], BF16)
    borow = ctx.salloc("p1b_bo", [1, 1024], BF16)
    brow2 = ctx.salloc("p1b_br2", [1, 1024], BF16)
    triu = ctx.salloc("p1b_triu", [128, 128], F32)
    xt = ctx.salloc("p1b_xt", [128, 2 * 1024], F32)
    xnin = ctx.salloc("p1b_xnin", [128, 8 * TG], BF16)
    swin = ctx.salloc("p1b_swin", [128, 8 * TG], BF16)
    qsf = ctx.salloc("p1b_qsf", [128, 8 * TG], BF16)
    ksf = ctx.salloc("p1b_ksf", [128, 8 * TG], BF16)
    vst = ctx.salloc("p1b_vst", [128, CPG * 1024], BF16)
    kst = ctx.salloc("p1b_kst", [128, CPG * 1024], BF16)
    lout = ctx.salloc("p1b_lout", [128, 8 * TG], BF16)
    ufm = ctx.salloc("p1b_ufm", [128, 8 * TG], BF16)
    Sf = ctx.salloc("p1b_Sf", [128, 8 * 64], F32)
    Sb = ctx.salloc("p1b_Sb", [128, 8 * 64], BF16)
    zsf = ctx.salloc("p1b_zsf", [128, 8], F32)
    zsb = ctx.salloc("p1b_zsb", [128, 8], BF16)
    t0b = ctx.salloc("p1b_t0", [128, 512], F32)
    t1b = t0b
    t2b = ctx.salloc("p1b_t2", [128, 512], F32)
    atb = ctx.salloc("p1b_atb", [128, 256], BF16)
    obf = ctx.salloc("p1b_obf", [128, 128], BF16)
    dcol = ctx.salloc("p1b_dcol", [128, 2], F32)
    rcpd = ctx.salloc("p1b_rcpd", [128, 2], F32)
    zb = ctx.salloc("p1b_zb", [128, 2 * 1024], F32)

    if P1B_DBG:
        ctx.dbg = nc.dram_tensor("s_dbg", [NTG, 128, 8, TG], BF16,
                                 kind="ExternalOutput").ap()
        ctx.dbg2 = nc.dram_tensor("s_dbg2", [NTG, 128, 8, TG], BF16,
                                  kind="ExternalOutput").ap()
    ps_q = ctx.palloc("p1b_psq", [128, 512])
    ps_v = ctx.palloc("p1b_psv", [128, 1024])
    ps_at = ctx.palloc("p1b_psat", [128, 128])
    ps_nd = ctx.palloc("p1b_psnd", [128, 128])
    ps_Sz = ctx.palloc("p1b_psSz", [64, 128])
    ps_ot = ctx.palloc("p1b_psot", [128, 128], BF16)

    P.do("gpsimd", lambda e: e.memset(triu, 1.0), drain=True)
    P.do("gpsimd", lambda e: e.affine_select(out=triu, in_=triu,
            compare_op=ALU.is_ge, fill=0.0, base=0, pattern=[[1, 128]],
            channel_multiplier=-1), inc=pol, drain=True)
    masks_ready = pol.mark

    P.dma("sync", wqs.rearrange("p (c f) -> p c f", c=8),
          ctx.wsc["L0_attn_Wqkv_s"].rearrange("(c p) f -> p c f", p=128), dL,
          wait=[ctx.castdone["L0_attn_Wqkv_s"]])
    P.dma("sync", wp.rearrange("p (c f) -> p c f", c=8),
          ctx.wsc["L0_attn_Wp_a"].rearrange("(c p) f -> p c f", p=128), dL,
          wait=[ctx.castdone["L0_attn_Wp_a"]])
    P.dma("sync", wos.rearrange("p (c f) -> p c f", c=8),
          ctx.wsc["L0_attn_Wo_s"].rearrange("(c p) f -> p c f", p=128), dL,
          wait=[ctx.castdone["L0_attn_Wo_s"]])
    P.dma("sync", wo.rearrange("p (c f) -> p c f", c=8),
          ctx.wsc["L0_attn_Wo"].rearrange("(c p) f -> p c f", p=128), dL,
          wait=[ctx.castdone["L0_attn_Wo"]])
    P.dma("gpsimd", bqrow, ctx.I["L0_attn_bqkv_s"].rearrange("(o f) -> o f", o=1), dG)
    P.dma("gpsimd", bpbo, ctx.I["L0_attn_bp_a"].rearrange("(o f) -> o f", o=1), dG)
    P.dma("gpsimd", brow2, ctx.I["L0_attn_bo_s"].rearrange("(o f) -> o f", o=1), dG)
    P.dma("gpsimd", borow, ctx.I["L0_attn_bo"].rearrange("(o f) -> o f", o=1), dG)
    P.do("vector", lambda e: e.tensor_tensor(bpbo, bpbo, brow2, ALU.add),
         wait=[dG], inc=vec, drain=True)
    wld = (dL.mark, dG.mark, vec.mark)

    xinv = ctx.streams["xin"].rearrange("(g u p) d -> g p u d", p=128, u=CPG)
    x1v = ctx.streams["x1"].rearrange("(g u p) d -> g p u d", p=128, u=CPG)
    swav = ctx.streams["swa"]
    xnfv = ctx.streams["xnf"]

    for s in range(2):
        for g4 in range(4):
            gidx = s * 4 + g4
            P.dma("sync", xnin.rearrange("p (c t) -> p c t", c=8), xnfv[gidx], dL,
                  wait=[scl.mark, vec.mark])
            P.dma("sync", swin.rearrange("p (c t) -> p c t", c=8), swav[gidx], dL)
            ld = dL.mark
            for fc in range(8):
                for c in range(8):
                    P.do("tensor", lambda e, fc=fc, c=c: e.matmul(
                            ps_q, wqs[:, c * 3072 + fc * 128: c * 3072 + (fc + 1) * 128],
                            xnin[:, c * TG:(c + 1) * TG],
                            start=(c == 0), stop=False),
                         wait=([vec.mark, scl.mark, ld, wld[0], wld[1]]
                               if c == 0 else []))
                P.do("tensor", lambda e, fc=fc: e.matmul(
                        ps_q, bqrow[0:1, fc * 128:(fc + 1) * 128],
                        ctx.ones_row_bf[0:1, 0:TG], start=False, stop=True), inc=tns)
                P.do("vector", lambda e: e.tensor_scalar(
                        t0b, ps_q, 0.125, 0.0, ALU.mult, ALU.min),
                     wait=[tns.mark], drain=True)
                P.do("vector", lambda e: e.tensor_scalar(
                        t2b, ps_q, 0.125, 0.0, ALU.mult, ALU.max),
                     inc=vec, drain=True)
                P.do("scalar", lambda e: e.activation(t1b, t0b, AF.Exp),
                     wait=[vec], inc=scl, drain=True)
                P.do("vector", lambda e, fc=fc: e.scalar_tensor_tensor(
                        qsf[:, fc * TG:(fc + 1) * TG], t1b, 1e-4, t2b,
                        ALU.add, ALU.add),
                     wait=[scl], inc=vec, drain=True)
            for fc in range(8):
                for c in range(8):
                    P.do("tensor", lambda e, fc=fc, c=c: e.matmul(
                            ps_q, wqs[:, c * 3072 + 1024 + fc * 128:
                                      c * 3072 + 1024 + (fc + 1) * 128],
                            xnin[:, c * TG:(c + 1) * TG],
                            start=(c == 0), stop=False),
                         wait=([vec.mark] if c == 0 else []))
                P.do("tensor", lambda e, fc=fc: e.matmul(
                        ps_q, bqrow[0:1, 1024 + fc * 128: 1024 + (fc + 1) * 128],
                        ctx.ones_row_bf[0:1, 0:TG], start=False, stop=True), inc=tns)
                P.do("vector", lambda e: e.tensor_single_scalar(
                        t0b, ps_q, 0.0, ALU.min),
                     wait=[tns.mark], drain=True)
                P.do("vector", lambda e: e.tensor_single_scalar(
                        t2b, ps_q, 0.0, ALU.max),
                     inc=vec, drain=True)
                P.do("scalar", lambda e: e.activation(t1b, t0b, AF.Exp),
                     wait=[vec], inc=scl, drain=True)
                P.do("vector", lambda e, fc=fc: e.scalar_tensor_tensor(
                        ksf[:, fc * TG:(fc + 1) * TG], t1b, 1e-4, t2b,
                        ALU.add, ALU.add),
                     wait=[scl], inc=vec, drain=True)
            for u in range(CPG):
                for n in range(2):
                    for c in range(8):
                        P.do("tensor", lambda e, u=u, n=n, c=c: e.matmul(
                                ps_v[:, n * 512:(n + 1) * 512],
                                xnin[:, c * TG + u * 128: c * TG + (u + 1) * 128],
                                wqs[:, c * 3072 + 2048 + n * 512:
                                    c * 3072 + 2048 + (n + 1) * 512],
                                start=(c == 0), stop=False),
                             wait=([vec.mark, scl.mark] if (n == 0 and c == 0) else []))
                    P.do("tensor", lambda e, n=n: e.matmul(
                            ps_v[:, n * 512:(n + 1) * 512],
                            ctx.ones_row_bf[0:1, 0:128],
                            bqrow[0:1, 2048 + n * 512: 2048 + (n + 1) * 512],
                            start=False, stop=True),
                         inc=(tns if n == 1 else None))
                P.do("vector", lambda e, u=u: e.tensor_copy(
                        vst[:, u * 1024:(u + 1) * 1024], ps_v),
                     wait=[tns], inc=vec, drain=True)
            for u in range(CPG):
                for fc in range(8):
                    P.do("tensor", lambda e, u=u, fc=fc: e.transpose(
                            ps_ot, ksf[:, fc * TG + u * 128: fc * TG + (u + 1) * 128],
                            ctx.id_bf),
                         wait=[vec.mark], inc=tns)
                    P.do("vector", lambda e, u=u, fc=fc: e.tensor_copy(
                            kst[:, u * 1024 + fc * 128: u * 1024 + (fc + 1) * 128],
                            ps_ot),
                         wait=[tns.mark], inc=vec, drain=True)
            lstate = {"atmark": [vec.mark, vec.mark], "ndmark": [vec.mark, vec.mark],
                      "otmark": scl.mark, "Szmark": vec.mark, "tail": None}
            ps_at2 = [ps_at, ps_v[:, 0:128]]
            ps_nd2 = [ps_nd, ps_q[:, 0:128]]

            def lin_tail(t):
                (u, h, par, obf_mark, atb_mark) = t
                hp = slice((h % 2) * 64, (h % 2) * 64 + 64)
                fc = h // 2
                tok = slice(fc * TG + u * 128, fc * TG + (u + 1) * 128)
                cs_t = g4 * CPG + u
                first = (cs_t == 0)
                P.do("tensor", lambda e, par=par: e.transpose(
                        ps_ot[0:64, :], obf[:, par * 64:(par + 1) * 64], ctx.id_bf),
                     wait=[obf_mark, lstate["otmark"]], inc=tns)
                P.do("scalar", lambda e, hp=hp, tok=tok: e.copy(
                        lout[hp, tok], ps_ot[0:64, :]),
                     wait=[tns.mark], inc=scl, drain=True)
                lstate["otmark"] = scl.mark
                P.do("tensor", lambda e, u=u, fc=fc, h=h: e.matmul(
                        ps_Sz[:, 0:64],
                        kst[:, u * 1024 + fc * 128 + (h % 2) * 64:
                            u * 1024 + fc * 128 + (h % 2) * 64 + 64],
                        vst[:, u * 1024 + h * 64: u * 1024 + h * 64 + 64],
                        start=True, stop=True),
                     wait=[lstate["Szmark"]])
                P.do("tensor", lambda e, u=u, fc=fc, h=h: e.matmul(
                        ps_Sz[:, 64:65],
                        kst[:, u * 1024 + fc * 128 + (h % 2) * 64:
                            u * 1024 + fc * 128 + (h % 2) * 64 + 64],
                        ctx.ones_col_bf, start=True, stop=True), inc=tns)
                Ssl = (hp, slice((h // 2) * 64, (h // 2 + 1) * 64))
                zsl = (hp, slice(h // 2, h // 2 + 1))
                if first:
                    P.do("vector", lambda e, Ssl=Ssl: e.tensor_copy(
                            Sf[Ssl], ps_Sz[:, 0:64]),
                         wait=[tns.mark], drain=True)
                    P.do("vector", lambda e, zsl=zsl: e.tensor_copy(
                            zsf[zsl], ps_Sz[:, 64:65]),
                         inc=vec, drain=True)
                else:
                    P.do("vector", lambda e, Ssl=Ssl: e.tensor_tensor(
                            Sf[Ssl], Sf[Ssl], ps_Sz[:, 0:64], ALU.add),
                         wait=[tns.mark], drain=True)
                    P.do("vector", lambda e, zsl=zsl: e.tensor_tensor(
                            zsf[zsl], zsf[zsl], ps_Sz[:, 64:65], ALU.add),
                         inc=vec, drain=True)
                lstate["Szmark"] = vec.mark

            for u in range(CPG):
                cs = g4 * CPG + u
                first = (cs == 0)
                for h in range(NH):
                    par = h % 2
                    pat = ps_at2[par]
                    pnd = ps_nd2[par]
                    hp = slice((h % 2) * 64, (h % 2) * 64 + 64)
                    fc = h // 2
                    tok = slice(fc * TG + u * 128, fc * TG + (u + 1) * 128)
                    atsl = atb[:, par * 128:(par + 1) * 128]
                    P.do("tensor", lambda e, hp=hp, tok=tok, pat=pat: e.matmul(
                            pat, ksf[hp, tok], qsf[hp, tok],
                            start=True, stop=True),
                         wait=[lstate["atmark"][par]], inc=tns)
                    P.do("vector", lambda e, pat=pat, atsl=atsl: e.scalar_tensor_tensor(
                            atsl, pat, 1.0, triu, ALU.mult, ALU.mult),
                         wait=[tns.mark, masks_ready], inc=vec, drain=True)
                    lstate["atmark"][par] = vec.mark
                    if first:
                        P.do("tensor", lambda e, u=u, h=h, pnd=pnd, atsl=atsl: e.matmul(
                                pnd[:, 0:64], atsl,
                                vst[:, u * 1024 + h * 64: u * 1024 + h * 64 + 64],
                                start=True, stop=True),
                             wait=[vec.mark, lstate["ndmark"][par]])
                        P.do("tensor", lambda e, pnd=pnd, atsl=atsl: e.matmul(
                                pnd[:, 64:65], atsl, ctx.ones_col_bf,
                                start=True, stop=True), inc=tns)
                    else:
                        P.do("tensor", lambda e, hp=hp, tok=tok, h=h, pnd=pnd: e.matmul(
                                pnd[:, 0:64], qsf[hp, tok],
                                Sb[hp, (h // 2) * 64:(h // 2 + 1) * 64],
                                start=True, stop=False),
                             wait=[vec.mark, lstate["ndmark"][par]])
                        P.do("tensor", lambda e, u=u, h=h, pnd=pnd, atsl=atsl: e.matmul(
                                pnd[:, 0:64], atsl,
                                vst[:, u * 1024 + h * 64: u * 1024 + h * 64 + 64],
                                start=False, stop=True))
                        P.do("tensor", lambda e, hp=hp, tok=tok, h=h, pnd=pnd: e.matmul(
                                pnd[:, 64:65], qsf[hp, tok],
                                zsb[hp, h // 2: h // 2 + 1],
                                start=True, stop=False))
                        P.do("tensor", lambda e, pnd=pnd, atsl=atsl: e.matmul(
                                pnd[:, 64:65], atsl, ctx.ones_col_bf,
                                start=False, stop=True), inc=tns)
                    P.do("vector", lambda e, pnd=pnd, par=par: e.tensor_single_scalar(
                            dcol[:, par:par + 1], pnd[:, 64:65], 1e-6, ALU.add),
                         wait=[tns.mark], drain=True)
                    P.do("vector", lambda e, par=par: e.reciprocal(
                            rcpd[:, par:par + 1], dcol[:, par:par + 1]), drain=True)
                    P.do("vector", lambda e, pnd=pnd, par=par: e.tensor_scalar(
                            obf[:, par * 64:(par + 1) * 64], pnd[:, 0:64],
                            rcpd[:, par:par + 1], None, ALU.mult),
                         inc=vec, drain=True)
                    lstate["ndmark"][par] = vec.mark
                    newt = (u, h, par, vec.mark, vec.mark)
                    if lstate["tail"] is not None:
                        lin_tail(lstate["tail"])
                    lstate["tail"] = newt
                if lstate["tail"] is not None:
                    lin_tail(lstate["tail"])
                    lstate["tail"] = None
                P.do("vector", lambda e: e.tensor_copy(Sb, Sf), drain=True)
                P.do("vector", lambda e: e.tensor_copy(zsb, zsf), inc=vec, drain=True)
            for fc in range(8):
                for c in range(8):
                    P.do("tensor", lambda e, fc=fc, c=c: e.matmul(
                            ps_q, wp[:, c * 1024 + fc * 128: c * 1024 + (fc + 1) * 128],
                            swin[:, c * TG:(c + 1) * TG],
                            start=(c == 0), stop=False),
                         wait=([vec.mark, scl.mark, ld] if c == 0 else []))
                for c in range(8):
                    P.do("tensor", lambda e, fc=fc, c=c: e.matmul(
                            ps_q, wos[:, c * 1024 + fc * 128: c * 1024 + (fc + 1) * 128],
                            lout[:, c * TG:(c + 1) * TG],
                            start=False, stop=False))
                P.do("tensor", lambda e, fc=fc: e.matmul(
                        ps_q, bpbo[0:1, fc * 128:(fc + 1) * 128],
                        ctx.ones_row_bf[0:1, 0:TG], start=False, stop=True),
                     wait=[wld[2]], inc=tns)
                P.do("scalar", lambda e, fc=fc: e.activation(
                        ufm[:, fc * TG:(fc + 1) * TG], ps_q, AF.Copy),
                     wait=[tns.mark], inc=scl, drain=False)
            if P1B_DBG:
                P.dma("sync", ctx.dbg2[gidx], ufm.rearrange("p (c t) -> p c t", c=8),
                      dS, wait=[scl.mark])
            ufm_ready = (scl.mark, vec.mark)

            def emit_mix_ln(u):
                par = u % 2
                zs = zb[:, par * 1024:(par + 1) * 1024]
                xc = xt[:, par * 1024:(par + 1) * 1024]
                P.dma("sync", xc, xinv[gidx][:, u], dL,
                      wait=[scl.mark, vec.mark])
                emit_ln(ctx, xc, ctx.rstd, src_wait=[dL.mark, dS.mark], dump=zs)
                P.do("vector", lambda e, xc=xc, zs=zs: e.tensor_scalar(
                        zs, xc, ctx.meancol, ctx.rstd, ALU.subtract, ALU.mult),
                     wait=[scl, dS.mark], inc=vec, drain=True)
                return (vec.mark, scl.mark)

            znext = emit_mix_ln(0)
            evmark = vec.mark
            for u in range(CPG):
                par = u % 2
                zs = zb[:, par * 1024:(par + 1) * 1024]
                zmark = znext
                for n in range(2):
                    for c in range(8):
                        P.do("tensor", lambda e, u=u, n=n, c=c: e.matmul(
                                ps_v[:, n * 512:(n + 1) * 512],
                                ufm[:, c * TG + u * 128: c * TG + (u + 1) * 128],
                                wo[:, c * 1024 + n * 512: c * 1024 + (n + 1) * 512],
                                start=(c == 0), stop=False),
                             wait=([ufm_ready[0], ufm_ready[1], evmark, wld[1]]
                                   if (n == 0 and c == 0) else []))
                    P.do("tensor", lambda e, n=n: e.matmul(
                            ps_v[:, n * 512:(n + 1) * 512],
                            ctx.ones_row_bf[0:1, 0:128],
                            borow[0:1, n * 512:(n + 1) * 512],
                            start=False, stop=True),
                         inc=(tns if n == 1 else None))
                mixmark = tns.mark
                if u + 1 < CPG:
                    znext = emit_mix_ln(u + 1)
                P.do("vector", lambda e, zs=zs: e.tensor_tensor(
                        zs, ps_v, zs, ALU.add),
                     wait=[(zmark[0][0], zmark[0][1]), (zmark[1][0], zmark[1][1]),
                           mixmark, dS.mark], inc=vec, drain=False)
                evmark = vec.mark
                P.dma("sync", x1v[gidx][:, u], zs, dS, wait=[vec.mark])



def emit_p3(ctx):
    nc, P = ctx.nc, ctx.P
    dL, dG, dS, tns, vec, scl, pol = (ctx.dL, ctx.dG, ctx.dS, ctx.tns, ctx.vec,
                                      ctx.scl, ctx.pol)

    xnfm3 = ctx.salloc("p3_xnfm", [128, 8 * NT], BF16)
    accy = ctx.salloc("p3_acc", [128, NCH * 1024], BF16)
    w1e = ctx.salloc("p3_w1e", [128, 2 * 8 * 256], BF16)
    w2e = ctx.salloc("p3_w2e", [128, 2 * 2 * 1024], BF16)
    b1e = ctx.salloc("p3_b1e", [1, 2 * 256], BF16)
    b2e = ctx.salloc("p3_b2e", [1, 2 * 1024], BF16)
    lgall = ctx.salloc("p3_lg", [128, NCH * 32], F32)
    rmaxall = ctx.salloc("p3_rmax", [128, NCH], F32)
    xt = ctx.salloc("p3_xt", [128, 2 * 1024], F32)
    z3 = ctx.salloc("p3_z3", [128, 1024], F32)
    zfm = ctx.salloc("p3_zfm", [128, 8 * 128], F32)
    hbf = ctx.salloc("p3_h", [128, 2 * 512], BF16)
    t2b = ctx.salloc("p3_t2", [128, 2 * 1024], BF16)
    mcol = ctx.salloc("p3_mcol", [128, 2], F32)
    outb = ctx.salloc("p3_out", [128, 1024], F32)

    xAv = ctx.streams["xA"].rearrange("(n p) d -> n p d", p=128)
    xBv = ctx.streams["xB"].rearrange("(n p) d -> n p d", p=128)

    ctx.push_scope()
    rwsb = ctx.salloc("p3_rw", [32, 1024], F32)
    rwT = ctx.salloc("p3_rwT", [128, 8 * 32], F32)
    rbrow = ctx.salloc("p3_rb", [1, 32], F32)
    ps_t = [ctx.palloc("p3_pst0", [128, 128]), ctx.palloc("p3_pst1", [128, 128])]
    ps_lg = ctx.palloc("p3_pslg", [128, 32])

    P.dma("sync", rwsb, ctx.I["L0_ffn_router_w"], dL)
    P.dma("sync", rbrow, ctx.I["L0_ffn_router_b"].rearrange("(o d) -> o d", o=1), dL)
    rwl = dL.mark
    for c in range(8):
        P.do("tensor", lambda e, c=c: e.transpose(
                ctx.ps_misc[:, 0:32], rwsb[:, c * 128:(c + 1) * 128],
                ctx.id_f32[0:32, 0:32]),
             wait=[rwl, ctx.consts_ready, vec.mark], inc=tns)
        P.do("vector", lambda e, c=c: e.tensor_copy(rwT[:, c * 32:(c + 1) * 32],
                                                    ctx.ps_misc[:, 0:32]),
             wait=[tns], inc=vec, drain=True)

    for ci in range(NCH):
        xs = xt[:, (ci % 2) * 1024:(ci % 2 + 1) * 1024]
        P.dma("sync", xs, xAv[ci], dL, wait=[scl.mark, vec.mark])
        ld = dL.mark
        emit_ln(ctx, xs, ctx.rstd, src_wait=[ld], dump=z3)
        P.do("vector", lambda e, xs=xs: e.tensor_scalar(
                z3, xs, ctx.meancol, ctx.rstd, ALU.subtract, ALU.mult),
             wait=[scl], inc=vec, drain=True)
        for c in range(8):
            par = c % 2
            P.do("tensor", lambda e, c=c, par=par: e.transpose(
                    ps_t[par], z3[:, c * 128:(c + 1) * 128], ctx.id_f32),
                 wait=[vec.mark, scl.mark], inc=tns)
            P.do("vector", lambda e, c=c, par=par: e.tensor_copy(
                    zfm[:, c * 128:(c + 1) * 128], ps_t[par]),
                 wait=[tns.mark], inc=vec, drain=False)
            P.do("vector", lambda e, c=c, ci=ci, par=par: e.tensor_copy(
                    xnfm3[:, c * NT + ci * 128: c * NT + (ci + 1) * 128],
                    ps_t[par]),
                 wait=[tns.mark], inc=vec, drain=False)
        for c in range(8):
            P.do("tensor", lambda e, c=c: e.matmul(
                    ps_lg, zfm[:, c * 128:(c + 1) * 128],
                    rwT[:, c * 32:(c + 1) * 32], start=(c == 0), stop=False),
                 wait=([vec.mark] if c == 0 else []))
        P.do("tensor", lambda e: e.matmul(
                ps_lg, ctx.ones_f32row[0:1, 0:128], rbrow,
                start=False, stop=True), wait=[rwl], inc=tns)
        P.do("vector", lambda e, ci=ci: e.tensor_copy(lgall[:, ci * 32:(ci + 1) * 32],
                                                      ps_lg),
             wait=[tns], inc=vec, drain=True)
        P.do("vector", lambda e, ci=ci: e.tensor_reduce(
                rmaxall[:, ci:ci + 1], lgall[:, ci * 32:(ci + 1) * 32],
                AX.X, ALU.max), inc=vec, drain=True)
    ctx.phase_end()
    ctx.pop_scope()

    ps_h = [ctx.palloc("p3_psh0", [128, 512]), ctx.palloc("p3_psh1", [128, 512])]
    ps_o = [ctx.palloc("p3_pso0", [128, 1024]), ctx.palloc("p3_pso1", [128, 1024])]
    w1v = ctx.wsc["L0_ffn_weights1"].rearrange("e (c p) h -> e p c h", p=128)
    w2v = ctx.wsc["L0_ffn_weights2"].rearrange("e (g p) d -> e p g d", p=128)

    def emit_eloads(e_i):
        sl = e_i % 2
        wt = [ctx.castdone["L0_ffn_weights1"], ctx.castdone["L0_ffn_weights2"],
              tns.mark]
        P.dma("sync", w1e.rearrange("p (s c h) -> p s c h", s=2, c=8)[:, sl],
              w1v[e_i], dL, wait=wt)
        P.dma("sync", w2e.rearrange("p (s g d) -> p s g d", s=2, g=2)[:, sl],
              w2v[e_i], dL, wait=wt)
        P.dma("gpsimd", b1e[0:1, sl * 256:(sl + 1) * 256],
              ctx.I["L0_ffn_bias1"][e_i].rearrange("(o h) -> o h", o=1), dG,
              wait=[tns.mark])
        P.dma("gpsimd", b2e[0:1, sl * 1024:(sl + 1) * 1024],
              ctx.I["L0_ffn_bias2"][e_i].rearrange("(o h) -> o h", o=1), dG,
              wait=[tns.mark])
        return (dL.mark, dG.mark)

    emarks = {0: emit_eloads(0)}
    omark = [vec.mark, vec.mark]
    t2mark = [pol.mark, pol.mark]
    for e_i in range(P3_NE):
        sl = e_i % 2
        wld = emarks[e_i]
        hmark = [vec.mark, vec.mark]
        for tgg in range(NTG):
            if tgg == NTG // 2 and e_i + 1 < P3_NE:
                emarks[e_i + 1] = emit_eloads(e_i + 1)
            for g in range(2):
                pb = ps_h[g]
                for c in range(8):
                    P.do("tensor", lambda e, c=c, g=g, sl=sl, tgg=tgg, pb=pb: e.matmul(
                            pb, w1e[:, sl * 2048 + c * 256 + g * 128:
                                    sl * 2048 + c * 256 + (g + 1) * 128],
                            xnfm3[:, c * NT + tgg * TG:(c * NT + (tgg + 1) * TG)],
                            start=(c == 0), stop=False),
                         wait=([wld[0], wld[1], hmark[g]] if c == 0 else []))
                P.do("tensor", lambda e, g=g, sl=sl, pb=pb: e.matmul(
                        pb, b1e[0:1, sl * 256 + g * 128: sl * 256 + (g + 1) * 128],
                        ctx.ones_row_bf[0:1, 0:TG], start=False, stop=True),
                     inc=tns)
                P.do("vector", lambda e, g=g, pb=pb: e.tensor_scalar(
                        hbf[:, g * 512:(g + 1) * 512], pb, 0.0, None, ALU.max),
                     wait=[tns.mark], inc=vec, drain=False)
                hmark[g] = vec.mark
            hready = vec.mark
            for u in range(CPG):
                ci = tgg * CPG + u
                po = ps_o[u % 2]
                for n in range(2):
                    for g in range(2):
                        P.do("tensor", lambda e, u=u, n=n, g=g, sl=sl, po=po: e.matmul(
                                po[:, n * 512:(n + 1) * 512],
                                hbf[:, g * 512 + u * 128: g * 512 + (u + 1) * 128],
                                w2e[:, sl * 2048 + g * 1024 + n * 512:
                                    sl * 2048 + g * 1024 + (n + 1) * 512],
                                start=(g == 0), stop=False),
                             wait=([hready, omark[u % 2]] if (n == 0 and g == 0)
                                   else []))
                    P.do("tensor", lambda e, n=n, sl=sl, po=po: e.matmul(
                            po[:, n * 512:(n + 1) * 512],
                            ctx.ones_row_bf[0:1, 0:128],
                            b2e[0:1, sl * 1024 + n * 512: sl * 1024 + (n + 1) * 512],
                            start=False, stop=True),
                         inc=(tns if n == 1 else None))
                mc = mcol[:, (u % 2):(u % 2) + 1]
                P.do("vector", lambda e, ci=ci, e_i=e_i, mc=mc: e.tensor_scalar(
                        mc, lgall[:, ci * 32 + e_i: ci * 32 + e_i + 1],
                        rmaxall[:, ci:ci + 1], None, ALU.is_ge),
                     wait=[scl.mark], inc=vec, drain=True)
                t2s = t2b[:, (u % 2) * 1024:(u % 2 + 1) * 1024]
                P.do("scalar", lambda e, po=po, t2s=t2s, mc=mc: e.activation(
                        t2s, po, AF.Relu, scale=mc),
                     wait=[tns.mark, vec.mark, t2mark[u % 2]], inc=scl, drain=False)
                omark[u % 2] = scl.mark
                asl = accy[:, ci * 1024:(ci + 1) * 1024]
                if e_i == 0:
                    P.do("gpsimd", lambda e, asl=asl, t2s=t2s: e.tensor_copy(asl, t2s),
                         wait=[scl.mark], inc=pol, drain=True)
                else:
                    P.do("gpsimd", lambda e, asl=asl, t2s=t2s: e.tensor_tensor(
                            asl, asl, t2s, ALU.add),
                         wait=[scl.mark], inc=pol, drain=True)
                t2mark[u % 2] = pol.mark

    for ci in range(NCH):
        xs = xt[:, (ci % 2) * 1024:(ci % 2 + 1) * 1024]
        P.dma("sync", xs, xAv[ci], dL, wait=[vec.mark])
        P.do("vector", lambda e, ci=ci, xs=xs: e.tensor_tensor(
                outb, xs, accy[:, ci * 1024:(ci + 1) * 1024], ALU.add),
             wait=[dL.mark, pol.mark, dS.mark], inc=vec, drain=False)
        P.dma("sync", xBv[ci], outb, dS, wait=[vec.mark])



def emit_p4(ctx):
    nc, P = ctx.nc, ctx.P
    dL, dG, dS, tns, vec, scl, pol = (ctx.dL, ctx.dG, ctx.dS, ctx.tns, ctx.vec,
                                      ctx.scl, ctx.pol)

    waT = {W: ctx.salloc(f"p4_waT_{W}", [128, 8 * 1024], BF16) for W in "qkv"}
    wsqm = {W: ctx.salloc(f"p4_wsqm_{W}", [1, 1024], BF16) for W in "qkv"}
    qfm4 = ctx.salloc("p4_qfm", [128, 8 * SEQ], BF16)
    kfm4 = ctx.salloc("p4_kfm", [128, 8 * SEQ], BF16)
    vtm4 = ctx.salloc("p4_vtm", [128, 16 * 1040], BF16)
    ksqb = ctx.salloc("p4_ksqb", [128, 16 * 16], F32)
    cmcol = ctx.salloc("p4_cm", [64, 1], BF16)
    P.do("gpsimd", lambda e: e.memset(cmcol, -0.5), drain=True)
    P.do("gpsimd", lambda e: e.memset(vtm4, 1.0), inc=pol, drain=True)
    cm_ready = pol.mark

    wa_dram = {W: nc.dram_tensor(f"p4_wa_{W}", [D, D], BF16, kind="Internal").ap()
               for W in "qkv"}

    ctx.push_scope()
    stage = ctx.salloc("p4_stage", [128, 4 * 1024], F32)
    wabf = ctx.salloc("p4_wabf", [128, 8 * 1024], BF16)
    ps_row = ctx.palloc("p4_psrow0", [1, 512])
    Wmap = {"q": "L1_attn_Wq", "k": "L1_attn_Wk", "v": "L1_attn_Wv"}
    Gmap = {"q": "L1_attn_gq", "k": "L1_attn_gk", "v": "L1_attn_gv"}
    for W in "qkv":
        for half in range(2):
            P.dma("sync", stage.rearrange("p (c f) -> p c f", c=4),
                  ctx.I[Wmap[W]].rearrange("(c p) f -> p c f", p=128)[
                      :, half * 4:(half + 1) * 4], dL,
                  wait=[scl.mark])
            P.do("scalar", lambda e, half=half: e.activation(
                    wabf[:, half * 4096:(half + 1) * 4096], stage, AF.Tanh),
                 wait=[dL], inc=scl, drain=True)
        P.dma("sync", wa_dram[W].rearrange("(c p) f -> p c f", p=128),
              wabf.rearrange("p (c f) -> p c f", c=8), dS, wait=[scl.mark])
        for c in range(8):
            P.do("sync", lambda e, W=W, c=c: e.dma_start_transpose(
                    waT[W][:, c * 1024:(c + 1) * 1024],
                    wa_dram[W][:, c * 128:(c + 1) * 128]),
                 wait=([dS.mark] if c == 0 else []), inc=[(dL, 16)])
        P.do("scalar", lambda e, W=W: e.activation(wabf, waT[W], AF.Square),
             wait=[dL], inc=scl, drain=True)
        for n in range(2):
            for c in range(8):
                P.do("tensor", lambda e, n=n, c=c: e.matmul(
                        ps_row, ctx.ones_col_bf,
                        wabf[:, c * 1024 + n * 512: c * 1024 + (n + 1) * 512],
                        start=(c == 0), stop=(c == 7)),
                     wait=([scl.mark, vec.mark, ctx.consts_ready] if c == 0 else []),
                     inc=(tns if c == 7 else None))
            P.do("vector", lambda e, W=W, n=n: e.tensor_scalar(
                    wsqm[W][0:1, n * 512:(n + 1) * 512], ps_row, -0.5, None,
                    ALU.mult),
                 wait=[tns], inc=vec, drain=True)
    prol_done = (dL.mark, vec.mark, scl.mark)
    ctx.phase_end()
    ctx.pop_scope()

    xBv = ctx.streams["xB"].rearrange("(n p) d -> n p d", p=128)
    xCv = ctx.streams["xC"].rearrange("(n p) d -> n p d", p=128)

    for s in range(2):
        ctx.push_scope()
        xt = ctx.salloc(f"p4_xt{s}", [128, 1024], F32)
        zb = ctx.salloc(f"p4_zb{s}", [128, 1024], F32)
        xaf = ctx.salloc(f"p4_xaf{s}", [128, 1024], F32)
        xabf = ctx.salloc(f"p4_xabf{s}", [128, 1024], BF16)
        xafm = ctx.salloc(f"p4_xafm{s}", [128, 2 * 1024], BF16)
        Ub = ctx.salloc(f"p4_U{s}", [128, 1024], F32)
        xsqc = ctx.salloc(f"p4_xsq{s}", [128, 2], F32)
        qtmb = ctx.salloc(f"p4_qtm{s}", [128, 1024], BF16)
        ktmb = ctx.salloc(f"p4_ktm{s}", [128, 1024], BF16)
        ksqraw = ctx.salloc(f"p4_ksqr{s}", [128, 16], F32)
        G1 = {W: ctx.salloc(f"p4s_G1_{W}{s}", [128, 1024], BF16) for W in "qkv"}
        G2 = {W: ctx.salloc(f"p4s_G2_{W}{s}", [128, 1024], BF16) for W in "qkv"}
        ps_tb = ctx.palloc(f"p4_pstb{s}", [128, 256], BF16)
        ps_d = ctx.palloc(f"p4_psd{s}", [128, 1024])
        for W in "qkv":
            gflat = ctx.I[Gmap[W]].rearrange("a b d -> (a b d)")
            for half in range(2):
                P.dma("sync", ctx.rowbuf,
                      gflat[half * 512:(half + 1) * 512].rearrange("(o d) -> o d", o=1),
                      dL, wait=[tns.mark])
                P.do("tensor", lambda e: e.matmul(
                        ctx.ps_misc, ctx.ones_f32row[0:1, 0:128], ctx.rowbuf,
                        start=True, stop=True),
                     wait=[dL.mark, ctx.consts_ready, vec.mark], inc=tns)
                P.do("vector", lambda e, W=W, half=half: e.tensor_scalar(
                        G1[W][:, half * 512:(half + 1) * 512], ctx.ps_misc,
                        1.0 / 1024, None, ALU.mult),
                     wait=[tns], drain=True)
                P.do("vector", lambda e, W=W, half=half: e.tensor_scalar(
                        G2[W][:, half * 512:(half + 1) * 512], ctx.ps_misc,
                        0.5, None, ALU.mult),
                     inc=vec, drain=True)
        k2s = ctx.salloc(f"p4_k2s{s}", [128, 1024], F32)
        tstate = {"dmark": scl.mark, "vmark": vec.mark, "tail": None}

        def pdl_tail(t):
            ci, par, hm = t
            xafm_p = xafm[:, par * 1024:(par + 1) * 1024]
            xsq_p = xsqc[:, par:par + 1]
            for W in "qkv":
                for n in range(2):
                    for c in range(8):
                        P.do("tensor", lambda e, W=W, n=n, c=c, xafm_p=xafm_p: e.matmul(
                                ps_d[:, n * 512:(n + 1) * 512],
                                xafm_p[:, c * 128:(c + 1) * 128],
                                waT[W][:, c * 1024 + n * 512: c * 1024 + (n + 1) * 512],
                                start=(c == 0), stop=False),
                             wait=([hm[0], hm[1], tstate["dmark"]] + list(prol_done)
                                   if (n == 0 and c == 0) else []))
                    P.do("tensor", lambda e, W=W, n=n: e.matmul(
                            ps_d[:, n * 512:(n + 1) * 512],
                            ctx.ones_row_bf[0:1, 0:128],
                            wsqm[W][0:1, n * 512:(n + 1) * 512],
                            start=False, stop=True),
                         inc=(tns if n == 1 else None))
                P.do("scalar", lambda e, xsq_p=xsq_p: e.activation(
                        Ub, ps_d, AF.Relu, bias=xsq_p, scale=-2.0),
                     wait=[tns, tstate["vmark"]], inc=scl, drain=True)
                tstate["dmark"] = scl.mark
                P.do("vector", lambda e, W=W: e.tensor_tensor(
                        Ub, Ub, G1[W], ALU.mult),
                     wait=[scl], drain=True)
                P.do("vector", lambda e, W=W: e.tensor_tensor(
                        Ub, G2[W], Ub, ALU.subtract),
                     inc=vec, drain=True)
                tstate["vmark"] = vec.mark
                if W == "v":
                    P.do("vector", lambda e, ci=ci: e.tensor_copy(
                            vtm4[:, ci * 1040:ci * 1040 + 1040].rearrange(
                                "p (h y) -> p h y", h=16)[:, :, 0:64],
                            Ub.rearrange("p (h y) -> p h y", h=16)),
                         wait=[cm_ready], inc=vec, drain=True)
                    tstate["vmark"] = vec.mark
                else:
                    dst = qtmb if W == "q" else ktmb
                    P.do("scalar", lambda e, dst=dst: e.activation(dst, Ub, AF.Tanh),
                         wait=[vec], inc=scl, drain=True)
            P.do("scalar", lambda e: e.activation(k2s, ktmb, AF.Square),
                 wait=[scl.mark], inc=scl, drain=True)
            P.do("vector", lambda e: e.tensor_reduce(
                    ksqraw, k2s.rearrange("p (h x) -> p h x", h=16), AX.X, ALU.add),
                 wait=[scl], drain=True)
            P.do("vector", lambda e, ci=ci: e.tensor_scalar(
                    ksqb[:, ci * 16:(ci + 1) * 16], ksqraw, -1.0 / 80.0, None,
                    ALU.mult),
                 inc=vec, drain=True)
            for src_b, dstb in ((qtmb, qfm4), (ktmb, kfm4)):
                for c in range(8):
                    P.do("tensor", lambda e, src_b=src_b, c=c: e.transpose(
                            ps_tb[:, (c % 2) * 128:(c % 2 + 1) * 128],
                            src_b[:, c * 128:(c + 1) * 128], ctx.id_bf),
                         wait=[vec.mark, scl.mark], inc=tns)
                    P.do("vector", lambda e, dstb=dstb, c=c, ci=ci: e.tensor_copy(
                            dstb[:, c * SEQ + ci * 128: c * SEQ + (ci + 1) * 128],
                            ps_tb[:, (c % 2) * 128:(c % 2 + 1) * 128]),
                         wait=[tns.mark], inc=vec, drain=True)

        for ci in range(16):
            gi = s * 16 + ci
            par = ci % 2
            P.dma("sync", xt, xBv[gi], dL, wait=[scl.mark, vec.mark])
            ld = dL.mark
            emit_ln(ctx, xt, ctx.rstd, src_wait=[ld], dump=zb)
            P.do("vector", lambda e: e.tensor_scalar(
                    zb, xt, ctx.meancol, ctx.rstd, ALU.subtract, ALU.mult),
                 wait=[scl], inc=vec, drain=True)
            P.do("scalar", lambda e: e.activation(xaf, zb, AF.Tanh),
                 wait=[vec], inc=scl, drain=True)
            P.do("scalar", lambda e, par=par: e.activation(zb, xaf, AF.Square,
                                                  accum_out=xsqc[:, par:par + 1]),
                 inc=scl, drain=True)
            P.do("vector", lambda e: e.tensor_copy(xabf, xaf),
                 wait=[scl.mark], inc=vec, drain=True)
            for c in range(8):
                P.do("tensor", lambda e, c=c: e.transpose(
                        ps_tb[:, (c % 2) * 128:(c % 2 + 1) * 128],
                        xabf[:, c * 128:(c + 1) * 128], ctx.id_bf),
                     wait=[vec.mark, scl.mark], inc=tns)
                P.do("vector", lambda e, c=c, par=par: e.tensor_copy(
                        xafm[:, par * 1024 + c * 128: par * 1024 + (c + 1) * 128],
                        ps_tb[:, (c % 2) * 128:(c % 2 + 1) * 128]),
                     wait=[tns.mark], inc=vec, drain=True)
            hm = (vec.mark, scl.mark)
            if tstate["tail"] is not None:
                pdl_tail(tstate["tail"])
            tstate["tail"] = (ci, par, hm)
        if tstate["tail"] is not None:
            pdl_tail(tstate["tail"])
            tstate["tail"] = None
        ctx.phase_end()
        ctx.pop_scope()

        ctx.push_scope()
        woT = ctx.salloc(f"p4_woT{s}", [128, 8 * 1024], BF16)
        pbuf = ctx.salloc(f"p4_pbuf{s}", [128, 16 * 512], BF16)
        ofm = ctx.salloc(f"p4_ofm{s}", [128, 8 * 512], BF16)
        q2b = ctx.salloc(f"p4_q2b{s}", [64, 512], BF16)
        qsqrow = ctx.salloc(f"p4_qsqr{s}", [1, 512], BF16)
        denrow = ctx.salloc(f"p4_den{s}", [1, 512], F32)
        rcpden = denrow
        rbsb = ctx.salloc(f"p4_rb{s}", [64, 512], F32)
        xt2 = ctx.salloc(f"p4_xt2{s}", [128, 1024], F32)
        ps_s = [ctx.palloc(f"p4_pss0{s}", [128, 512]), ctx.palloc(f"p4_pss1{s}", [128, 512])]
        ps_dn = ctx.palloc(f"p4_psdn{s}", [1, 512])
        ps_rb = ctx.palloc(f"p4_psrb{s}", [64, 512])
        ps_pv = ctx.palloc(f"p4_pspv{s}", [65, 512])
        ps_y = ctx.palloc(f"p4_psy{s}", [128, 1024])
        for c in range(8):
            P.do("sync", lambda e, c=c: e.dma_start_transpose(
                    woT[:, c * 1024:(c + 1) * 1024],
                    ctx.wsc["L1_attn_Wo"][:, c * 128:(c + 1) * 128]),
                 wait=([ctx.castdone["L1_attn_Wo"], tns.mark] if c == 0 else []),
                 inc=[(dL, 16)])
        wol = dL.mark
        for qtg in range(4):
            for h in range(NH):
                hp = slice((h % 2) * 64, (h % 2) * 64 + 64)
                fc = h // 2
                qsl = slice(fc * SEQ + qtg * 512, fc * SEQ + (qtg + 1) * 512)
                P.do("scalar", lambda e, hp=hp, qsl=qsl: e.activation(
                        q2b, qfm4[hp, qsl], AF.Square),
                     wait=[vec.mark, tns.mark], inc=scl, drain=True)
                P.do("tensor", lambda e: e.matmul(
                        ps_dn, cmcol, q2b, start=True, stop=True),
                     wait=[scl.mark, vec.mark, cm_ready], inc=tns)
                P.do("scalar", lambda e: e.activation(qsqrow, ps_dn, AF.Copy),
                     wait=[tns.mark], inc=scl, drain=True)
                smark = [scl.mark, scl.mark]
                for si in range(16):
                    pp = ps_s[si % 2]
                    P.do("tensor", lambda e, hp=hp, qsl=qsl, si=si, pp=pp, fc=fc:
                         e.matmul(pp, kfm4[hp, fc * SEQ + si * 128:
                                           fc * SEQ + (si + 1) * 128],
                                  qfm4[hp, qsl], start=True, stop=False),
                         wait=[smark[si % 2]])
                    P.do("tensor", lambda e, si=si, pp=pp: e.matmul(
                            pp, ctx.ones_row_bf[0:1, 0:128], qsqrow,
                            start=False, stop=True), inc=tns)
                    P.do("scalar", lambda e, si=si, h=h, pp=pp: e.activation(
                            pbuf[:, si * 512:(si + 1) * 512], pp, AF.Exp,
                            scale=0.025, bias=ksqb[:, si * 16 + h: si * 16 + h + 1]),
                         wait=[tns.mark], inc=scl, drain=False)
                    smark[si % 2] = scl.mark
                for si in range(16):
                    P.do("tensor", lambda e, si=si, h=h: e.matmul(
                            ps_pv, vtm4[:, si * 1040 + h * 65: si * 1040 + h * 65 + 65],
                            pbuf[:, si * 512:(si + 1) * 512],
                            start=(si == 0), stop=(si == 15)),
                         wait=([vec.mark, scl.mark] if si == 0 else []),
                         inc=(tns if si == 15 else None))
                P.do("vector", lambda e: e.tensor_copy(denrow, ps_pv[64:65, :]),
                     wait=[tns], inc=vec, drain=True)
                P.do("scalar", lambda e: e.activation(denrow, denrow, AF.Ln),
                     wait=[vec], drain=True)
                P.do("scalar", lambda e: e.activation(denrow, denrow, AF.Exp,
                                                      scale=-1.0),
                     inc=scl, drain=True)
                P.do("tensor", lambda e: e.matmul(
                        ps_rb, ctx.ones_f32row[0:1, 0:64], rcpden,
                        start=True, stop=True),
                     wait=[scl.mark], inc=tns)
                P.do("vector", lambda e: e.tensor_copy(rbsb, ps_rb),
                     wait=[tns], inc=vec, drain=True)
                P.do("vector", lambda e, hp=hp, fc=fc: e.tensor_tensor(
                        ofm[hp, fc * 512:(fc + 1) * 512], ps_pv[0:64, :], rbsb,
                        ALU.mult),
                     wait=[vec.mark], inc=vec, drain=True)
            for u in range(CPG):
                gi = s * 16 + qtg * CPG + u
                P.dma("sync", xt2, xBv[gi], dL, wait=[vec.mark])
                xld = dL.mark
                for n in range(2):
                    for c in range(8):
                        P.do("tensor", lambda e, u=u, n=n, c=c: e.matmul(
                                ps_y[:, n * 512:(n + 1) * 512],
                                ofm[:, c * 512 + u * 128: c * 512 + (u + 1) * 128],
                                woT[:, c * 1024 + n * 512: c * 1024 + (n + 1) * 512],
                                start=(c == 0), stop=(c == 7)),
                             wait=([vec.mark, wol] if (n == 0 and c == 0) else []),
                             inc=(tns if (n == 1 and c == 7) else None))
                P.do("vector", lambda e: e.tensor_tensor(xt2, ps_y, xt2, ALU.add),
                     wait=[tns, xld, dS.mark], inc=vec, drain=False)
                P.dma("sync", xCv[gi], xt2, dS, wait=[vec.mark])
        ctx.phase_end()
        ctx.pop_scope()



QT = 1024
NQ = NT // QT
CPQ = QT // 128


def emit_p5(ctx):
    nc, P = ctx.nc, ctx.P
    dL, dG, dS, dT, tns, vec, scl, pol = (ctx.dL, ctx.dG, ctx.dS, ctx.dT,
                                          ctx.tns, ctx.vec, ctx.scl, ctx.pol)

    wts = ctx.salloc("p5_wts", [128, NCH * 8], F32)
    xn5sc = nc.dram_tensor("p5_xnsc", [NCH, 128, 8, 128], BF16, kind="Internal").ap()

    ctx.push_scope()
    wgsb = ctx.salloc("p5_wg", [8, 1024], F32)
    wgT = ctx.salloc("p5_wgT", [128, 8 * 8], F32)
    bgrow = ctx.salloc("p5_bg", [1, 8], F32)
    scb = ctx.salloc("p5_sc", [128, 8], F32)
    m1 = ctx.salloc("p5_m1", [128, 1], F32)
    m2 = ctx.salloc("p5_m2", [128, 1], F32)
    ltm = ctx.salloc("p5_ltm", [128, 8], F32)
    s2b = ctx.salloc("p5_s2", [128, 8], F32)
    xt = ctx.salloc("p5_xt", [128, 1024], F32)
    z5 = ctx.salloc("p5_z5", [128, 1024], F32)
    zfm = ctx.salloc("p5_zfm", [128, 1024], F32)
    xnb = ctx.salloc("p5_xnb", [128, 8 * 128], BF16)
    ps_t = [ctx.palloc("p5_pst0", [128, 128]), ctx.palloc("p5_pst1", [128, 128])]
    ps_g = ctx.palloc("p5_psg", [128, 8])

    P.dma("sync", wgsb, ctx.I["L1_ffn_Wg"], dL)
    gl = dL.mark
    for c in range(8):
        P.do("tensor", lambda e, c=c: e.transpose(
                ctx.ps_misc[:, 0:8], wgsb[:, c * 128:(c + 1) * 128],
                ctx.id_f32[0:8, 0:8]),
             wait=[gl, ctx.consts_ready, vec.mark], inc=tns)
        P.do("vector", lambda e, c=c: e.tensor_copy(wgT[:, c * 8:(c + 1) * 8],
                                                    ctx.ps_misc[:, 0:8]),
             wait=[tns], inc=vec, drain=True)
    P.dma("sync", ctx.rowbuf[0:1, 0:8],
          ctx.I["L1_ffn_bg"].rearrange("(o d) -> o d", o=1), dL, wait=[tns.mark])
    P.dma("sync", ctx.rowbuf[0:1, 8:16],
          ctx.I["L1_ffn_bias"].rearrange("(o d) -> o d", o=1), dL)
    P.do("vector", lambda e: e.tensor_tensor(bgrow, ctx.rowbuf[0:1, 0:8],
                                             ctx.rowbuf[0:1, 8:16], ALU.add),
         wait=[dL], inc=vec, drain=True)
    gate_ready = (vec.mark, tns.mark)

    xCv = ctx.streams["xC"].rearrange("(n p) d -> n p d", p=128)
    outv = ctx.streams["out"].rearrange("(n p) d -> n p d", p=128)

    for ci in range(NCH):
        P.dma("sync", xt, xCv[ci], dL, wait=[scl.mark, vec.mark])
        ld = dL.mark
        emit_ln(ctx, xt, ctx.rstd, src_wait=[ld], dump=z5)
        P.do("vector", lambda e: e.tensor_scalar(
                z5, xt, ctx.meancol, ctx.rstd, ALU.subtract, ALU.mult),
             wait=[scl], inc=vec, drain=True)
        for c in range(8):
            par = c % 2
            P.do("tensor", lambda e, c=c, par=par: e.transpose(
                    ps_t[par], z5[:, c * 128:(c + 1) * 128], ctx.id_f32),
                 wait=[vec.mark, dS.mark], inc=tns)
            P.do("vector", lambda e, c=c, par=par: e.tensor_copy(
                    zfm[:, c * 128:(c + 1) * 128], ps_t[par]),
                 wait=[tns.mark], inc=vec, drain=False)
            P.do("vector", lambda e, c=c, par=par: e.tensor_copy(
                    xnb[:, c * 128:(c + 1) * 128], ps_t[par]),
                 wait=[tns.mark, dS.mark], inc=vec, drain=True)
        P.dma("sync", xn5sc[ci], xnb.rearrange("p (c t) -> p c t", c=8), dS,
              wait=[vec.mark])
        for c in range(8):
            P.do("tensor", lambda e, c=c: e.matmul(
                    ps_g, zfm[:, c * 128:(c + 1) * 128],
                    wgT[:, c * 8:(c + 1) * 8], start=(c == 0), stop=False),
                 wait=([vec.mark, gate_ready[0], gate_ready[1]] if c == 0 else []))
        P.do("tensor", lambda e: e.matmul(
                ps_g, ctx.ones_f32row[0:1, 0:128], bgrow,
                start=False, stop=True), inc=tns)
        P.do("scalar", lambda e: e.activation(scb, ps_g, AF.Sigmoid),
             wait=[tns.mark, vec.mark], inc=scl, drain=True)
        P.do("vector", lambda e: e.tensor_reduce(m1, scb, AX.X, ALU.max),
             wait=[scl], drain=True)
        P.do("vector", lambda e: e.tensor_scalar(ltm, scb, m1, None, ALU.is_lt),
             drain=True)
        P.do("vector", lambda e: e.tensor_tensor(s2b, scb, ltm, ALU.mult),
             drain=True)
        P.do("vector", lambda e: e.tensor_reduce(m2, s2b, AX.X, ALU.max),
             drain=True)
        P.do("vector", lambda e: e.tensor_scalar(ltm, scb, m2, None, ALU.is_ge),
             drain=True)
        P.do("vector", lambda e, ci=ci: e.tensor_tensor(
                wts[:, ci * 8:(ci + 1) * 8], scb, ltm, ALU.mult),
             inc=vec, drain=True)
    ctx.phase_end()
    ctx.pop_scope()

    xnfm5 = ctx.salloc("p5_xnfm", [128, 8 * QT], BF16)
    w1T = ctx.salloc("p5_w1T", [128, 8 * 4096], BF16)
    w2T = ctx.salloc("p5_w2T", [128, 32 * 1024], BF16)
    h5 = ctx.salloc("p5_h", [128, 32 * 512], BF16)
    acc5 = ctx.salloc("p5_acc", [128, CPQ * 1024], BF16)
    b1all = ctx.salloc("p5_b1c", [128, 9 * 32], F32)
    b2row = ctx.salloc("p5_b2r", [1, 1024], BF16)
    t2b = ctx.salloc("p5_t2", [128, 1024], BF16)
    xt2 = ctx.salloc("p5_xt2", [128, 1024], F32)
    outb = xt2
    ps_h = [ctx.palloc("p5_psh0", [128, 512]), ctx.palloc("p5_psh1", [128, 512])]
    ps_o = [ctx.palloc("p5_pso0", [128, 1024]), ctx.palloc("p5_pso1", [128, 1024])]

    w1names = ["L1_ffn_W1"] * 8 + ["L1_ffn_Ws1"]
    w2names = ["L1_ffn_W2"] * 8 + ["L1_ffn_Ws2"]

    def w_src(names, e_i):
        shared = (e_i == 8)
        return ctx.wsc[names[e_i]] if shared else ctx.wsc[names[e_i]][e_i]

    def emit_w1_loads(e_i):
        s_ap = w_src(w1names, e_i)
        for c in range(8):
            P.do("sync", lambda e, c=c, s_ap=s_ap: e.dma_start_transpose(
                    w1T[:, c * 4096:(c + 1) * 4096],
                    s_ap[:, c * 128:(c + 1) * 128]),
                 wait=([ctx.castdone[w1names[e_i]], tns.mark] if c == 0 else []),
                 inc=[(dL, 16)])
        return dL.mark

    def emit_w2_loads(e_i):
        s_ap = w_src(w2names, e_i)
        for gc in range(32):
            P.do("sync", lambda e, gc=gc, s_ap=s_ap: e.dma_start_transpose(
                    w2T[:, gc * 1024:(gc + 1) * 1024],
                    s_ap[:, gc * 128:(gc + 1) * 128]),
                 wait=([ctx.castdone[w2names[e_i]], tns.mark] if gc == 0 else []),
                 inc=[(dL, 16)])
        return dL.mark

    def emit_bias_loads(e_i):
        shared = (e_i == 8)
        b2vec = (ctx.I["L1_ffn_bs2"] if shared else ctx.I["L1_ffn_b2"][e_i])
        P.dma("gpsimd", b2row, b2vec.rearrange("(o d) -> o d", o=1), dG,
              wait=[tns.mark])
        return (vec.mark, dG.mark)

    for e_i in range(min(P5_NE, 9)):
        b1vec = (ctx.I["L1_ffn_bs1"] if e_i == 8 else ctx.I["L1_ffn_b1"][e_i])
        load_cols(ctx, b1vec, b1all[:, e_i * 32:(e_i + 1) * 32], 32)

    insts = [(q, e_i) for q in range(NQ) for e_i in range(P5_NE)]
    nmarks = {}
    if insts:
        nmarks[insts[0]] = (emit_w1_loads(insts[0][1]), emit_w2_loads(insts[0][1]),
                            emit_bias_loads(insts[0][1]))
    omark = [vec.mark, vec.mark]
    lastq = [-1]
    xnldm = {}

    def emit_xnld(q):
        P.dma("sync",
              xnfm5.rearrange("p (c cq t) -> p c cq t", c=8, cq=CPQ),
              xn5sc[q * CPQ:(q + 1) * CPQ].rearrange("cq p c t -> p c cq t"),
              dL, wait=[tns.mark])
        return dL.mark

    for idx, (q, e_i) in enumerate(insts):
        if q != lastq[0]:
            lastq[0] = q
            if q not in xnldm:
                xnldm[q] = emit_xnld(q)
        xnld = xnldm[q]
        w1m, w2m, bm = nmarks[(q, e_i)]
        nxt = insts[idx + 1] if idx + 1 < len(insts) else None
        hmark = [scl.mark, scl.mark]
        n1 = n2 = None
        for tg2 in range(2):
            for j in range(32):
                pb = ps_h[j % 2]
                for c in range(8):
                    P.do("tensor", lambda e, j=j, c=c, tg2=tg2, pb=pb: e.matmul(
                            pb, w1T[:, c * 4096 + j * 128: c * 4096 + (j + 1) * 128],
                            xnfm5[:, c * QT + tg2 * 512: c * QT + (tg2 + 1) * 512],
                            start=(c == 0), stop=(c == 7)),
                         wait=([w1m, bm[0], bm[1], xnld, hmark[j % 2]]
                               if c == 0 else []),
                         inc=(tns if c == 7 else None))
                P.do("scalar", lambda e, j=j, pb=pb, e_i=e_i: e.activation(
                        h5[:, j * 512:(j + 1) * 512], pb, AF.Gelu,
                        bias=b1all[:, e_i * 32 + j: e_i * 32 + j + 1]),
                     wait=[tns.mark], inc=scl, drain=False)
                hmark[j % 2] = scl.mark
            if tg2 == 1 and nxt is not None:
                if e_i == P5_NE - 1 and nxt[0] == q + 1:
                    xnldm[q + 1] = emit_xnld(q + 1)
                n1 = emit_w1_loads(nxt[1])
            for u in range(CPG):
                cq = tg2 * CPG + u
                ci = q * CPQ + cq
                po = ps_o[u % 2]
                for n in range(2):
                    for j in range(32):
                        P.do("tensor", lambda e, u=u, n=n, j=j, po=po: e.matmul(
                                po[:, n * 512:(n + 1) * 512],
                                h5[:, j * 512 + u * 128: j * 512 + (u + 1) * 128],
                                w2T[:, j * 1024 + n * 512: j * 1024 + n * 512 + 512],
                                start=(j == 0), stop=False),
                             wait=([scl.mark, w2m, omark[u % 2]]
                                   if (n == 0 and j == 0) else []))
                    P.do("tensor", lambda e, n=n, po=po: e.matmul(
                            po[:, n * 512:(n + 1) * 512],
                            ctx.ones_row_bf[0:1, 0:128],
                            b2row[0:1, n * 512:(n + 1) * 512],
                            start=False, stop=True),
                         inc=(tns if n == 1 else None))
                t2s = t2b
                if e_i == 8:
                    P.do("vector", lambda e, po=po, t2s=t2s: e.tensor_copy(t2s, po),
                         wait=[tns, pol.mark], inc=vec, drain=True)
                else:
                    P.do("vector", lambda e, po=po, t2s=t2s, ci=ci, e_i=e_i:
                         e.tensor_scalar(
                            t2s, po, wts[:, ci * 8 + e_i: ci * 8 + e_i + 1],
                            None, ALU.mult),
                         wait=[tns, pol.mark], inc=vec, drain=True)
                omark[u % 2] = vec.mark
                asl = acc5[:, cq * 1024:(cq + 1) * 1024]
                if e_i == 0:
                    P.do("gpsimd", lambda e, asl=asl, t2s=t2s: e.tensor_copy(asl, t2s),
                         wait=[vec.mark], inc=pol, drain=True)
                else:
                    P.do("gpsimd", lambda e, asl=asl, t2s=t2s: e.tensor_tensor(
                            asl, asl, t2s, ALU.add),
                         wait=[vec.mark], inc=pol, drain=True)
        if nxt is not None:
            n2 = emit_w2_loads(nxt[1])
            nb = emit_bias_loads(nxt[1])
            nmarks[nxt] = (n1, n2, nb)
        if e_i == P5_NE - 1:
            for cq in range(CPQ):
                ci = q * CPQ + cq
                P.dma("sync", xt2, xCv[ci], dL, wait=[vec.mark])
                P.do("vector", lambda e, cq=cq: e.tensor_tensor(
                        outb, xt2, acc5[:, cq * 1024:(cq + 1) * 1024], ALU.add),
                     wait=[dL, dS.mark, pol.mark], inc=vec, drain=False)
                P.dma("sync", outv[ci], outb, dS, wait=[vec.mark])



INPUT_SPECS = {
    "L0_attn_g1": (D,), "L0_attn_b1": (D,),
    "L0_attn_Wqkv_a": (D, 3 * D), "L0_attn_bqkv_a": (3 * D,),
    "L0_attn_Wp_a": (D, D), "L0_attn_bp_a": (D,),
    "L0_attn_Wqkv_s": (D, 3 * D), "L0_attn_bqkv_s": (3 * D,),
    "L0_attn_Wo_s": (D, D), "L0_attn_bo_s": (D,),
    "L0_attn_Wo": (D, D), "L0_attn_bo": (D,),
    "L0_attn_gm": (D,), "L0_attn_bm": (D,),
    "L0_attn_W1": (D, 4 * D), "L0_attn_b1m": (4 * D,),
    "L0_attn_W2": (4 * D, D), "L0_attn_b2m": (D,),
    "L0_ffn_router_w": (32, D), "L0_ffn_router_b": (32,),
    "L0_ffn_weights1": (32, D, 256), "L0_ffn_bias1": (32, 256),
    "L0_ffn_weights2": (32, 256, D), "L0_ffn_bias2": (32, D),
    "L1_attn_Wq": (D, D), "L1_attn_gq": (1, 1, D),
    "L1_attn_Wk": (D, D), "L1_attn_gk": (1, 1, D),
    "L1_attn_Wv": (D, D), "L1_attn_gv": (1, 1, D),
    "L1_attn_Wo": (D, D),
    "L1_ffn_Wg": (8, D), "L1_ffn_bg": (8,), "L1_ffn_bias": (8,),
    "L1_ffn_W1": (8, 4 * D, D), "L1_ffn_b1": (8, 4 * D),
    "L1_ffn_W2": (8, D, 4 * D), "L1_ffn_b2": (8, D),
    "L1_ffn_Ws1": (4 * D, D), "L1_ffn_bs1": (4 * D,),
    "L1_ffn_Ws2": (D, 4 * D), "L1_ffn_bs2": (D,),
}

STREAM_DEFS = {
    "xin": ((NT, D), F32),
    "swa": ((NTG, 128, 8, TG), BF16),
    "xnf": ((NTG, 128, 8, TG), BF16),
    "x1": ((NT, D), F32),
    "xA": ((NT, D), F32),
    "xB": ((NT, D), F32),
    "xC": ((NT, D), F32),
    "out": ((NT, D), F32),
}

PHASE_IO = {
    "p1a": (["xin"], ["swa", "xnf"]),
    "p1bc": (["xin", "swa", "xnf"], ["x1"]),
    "p2": (["x1", "xin"], ["xA"]),
    "p3": (["xA"], ["xB"]),
    "p4": (["xB"], ["xC"]),
    "p5": (["xC"], ["out"]),
}

PHASE_WEIGHTS = {
    "p1a": ["L0_attn_g1", "L0_attn_b1", "L0_attn_Wqkv_a", "L0_attn_bqkv_a"],
    "p1bc": ["L0_attn_g1", "L0_attn_b1", "L0_attn_Wqkv_s", "L0_attn_bqkv_s",
             "L0_attn_Wp_a", "L0_attn_bp_a", "L0_attn_Wo_s", "L0_attn_bo_s",
             "L0_attn_Wo", "L0_attn_bo"],
    "p2": ["L0_attn_gm", "L0_attn_bm", "L0_attn_W1", "L0_attn_b1m",
           "L0_attn_W2", "L0_attn_b2m"],
    "p3": ["L0_ffn_router_w", "L0_ffn_router_b", "L0_ffn_weights1",
           "L0_ffn_bias1", "L0_ffn_weights2", "L0_ffn_bias2"],
    "p4": ["L1_attn_Wq", "L1_attn_gq", "L1_attn_Wk", "L1_attn_gk",
           "L1_attn_Wv", "L1_attn_gv", "L1_attn_Wo"],
    "p5": ["L1_ffn_Wg", "L1_ffn_bg", "L1_ffn_bias", "L1_ffn_W1", "L1_ffn_b1",
           "L1_ffn_W2", "L1_ffn_b2", "L1_ffn_Ws1", "L1_ffn_bs1",
           "L1_ffn_Ws2", "L1_ffn_bs2"],
}

PHASE_CASTS = {
    "p1a": ["L0_attn_Wqkv_a"],
    "p1bc": ["L0_attn_Wqkv_s", "L0_attn_Wp_a", "L0_attn_Wo_s", "L0_attn_Wo"],
    "p2": ["L0_attn_W1", "L0_attn_W2"],
    "p3": ["L0_ffn_weights1", "L0_ffn_weights2"],
    "p4": ["L1_attn_Wo"],
    "p5": ["L1_ffn_W1", "L1_ffn_W2", "L1_ffn_Ws1", "L1_ffn_Ws2"],
}

EMITTERS = {}


def build(phases, debug_streams=False):
    nc = bass.Bass("TRN2", target_bir_lowering=False, debug=False)
    ctx = Ctx()
    ctx.nc = nc
    ctx._phase_allocs = []
    P = Prog(nc)
    ctx.P = P

    wanted = set()
    for ph in phases:
        wanted.update(PHASE_WEIGHTS[ph])
    ctx.I = {n: nc.dram_tensor(n, list(INPUT_SPECS[n]), F32, kind="ExternalInput").ap()
             for n in sorted(wanted)}

    reads, writes = set(), set()
    for ph in phases:
        r, w = PHASE_IO[ph]
        reads.update(r)
        writes.update(w)
    ctx.streams = {}
    ctx.external_streams_in = []
    ctx.external_streams_out = []
    for sname in sorted(reads | writes):
        shape, dt = STREAM_DEFS[sname]
        if sname in writes:
            if sname == "out" or debug_streams or sname not in reads:
                kind = "ExternalOutput"
                ctx.external_streams_out.append(sname)
            else:
                kind = "Internal"
        else:
            kind = "ExternalInput"
            ctx.external_streams_in.append(sname)
        ctx.streams[sname] = nc.dram_tensor("s_" + sname, list(shape), dt, kind=kind).ap()

    sem_names = ["dL", "dG", "dS", "tns", "vec", "scl", "pol", "bar", "dT"]
    ctx._sem_guards = []
    def fresh_sems(tag, names):
        for snm in names:
            g = nc.semaphore(f"{snm}_{tag}")
            h = g.__enter__()
            ctx._sem_guards.append(g)
            setattr(ctx, snm, S(h))
    fresh_sems("g", sem_names)

    perm = []
    def psalloc(name, shape, dt):
        g = nc.sbuf_tensor(name, shape, dt)
        h = g.__enter__()
        perm.append(g)
        return h.ap()
    ctx.id_f32 = psalloc("id_f32", [128, 128], F32)
    ctx.id_bf = psalloc("id_bf", [128, 128], BF16)
    ctx.ones_row_bf = psalloc("ones_row_bf", [1, 512], BF16)
    ctx.ones_col_bf = psalloc("ones_col_bf", [128, 1], BF16)
    ctx.ones_col_f32 = psalloc("ones_col_f32", [1, 1], F32)
    ctx.sqsum = psalloc("sqsum", [128, 1], F32)
    ctx.sumcol = psalloc("sumcol", [128, 1], F32)
    ctx.meancol = psalloc("meancol", [128, 1], F32)
    ctx.msq = psalloc("msq", [128, 1], F32)
    ctx.varcol = psalloc("varcol", [128, 1], F32)
    ctx.lnv = psalloc("lnv", [128, 1], F32)
    ctx.rstd = psalloc("rstd", [128, 1], F32)
    ctx.rowbuf = psalloc("rowbuf", [1, 512], F32)
    ctx.epscol = psalloc("epscol", [128, 1], F32)
    ctx.ones_f32row = psalloc("ones_f32row", [1, 128], F32)
    gp = nc.psum_tensor("ps_misc", [128, 512], F32)
    ctx.ps_misc = gp.__enter__().ap()
    perm.append(gp)
    ctx._perm = perm

    P.do("gpsimd", lambda e: e.memset(ctx.id_f32, 0.0), drain=True)
    P.do("gpsimd", lambda e: e.affine_select(out=ctx.id_f32, in_=ctx.id_f32,
            compare_op=ALU.not_equal, fill=1.0, base=0, pattern=[[-1, 128]],
            channel_multiplier=1), drain=True)
    P.do("gpsimd", lambda e: e.memset(ctx.id_bf, 0.0), drain=True)
    P.do("gpsimd", lambda e: e.affine_select(out=ctx.id_bf, in_=ctx.id_bf,
            compare_op=ALU.not_equal, fill=1.0, base=0, pattern=[[-1, 128]],
            channel_multiplier=1), drain=True)
    P.do("gpsimd", lambda e: e.memset(ctx.ones_row_bf, 1.0), drain=True)
    P.do("gpsimd", lambda e: e.memset(ctx.ones_col_bf, 1.0), drain=True)
    P.do("gpsimd", lambda e: e.memset(ctx.ones_col_f32, 1.0), drain=True)
    P.do("gpsimd", lambda e: e.memset(ctx.epscol, 1e-5), drain=True)
    P.do("gpsimd", lambda e: e.memset(ctx.ones_f32row, 1.0), inc=ctx.pol, drain=True)
    ctx.consts_ready = ctx.pol.mark

    ctx.wsc = {}
    ctx.castdone = {}
    fresh_sems("c", ["dC"])

    def emit_casts(ph):
        for wn in PHASE_CASTS.get(ph, []):
            if wn in ctx.wsc:
                continue
            shape = list(INPUT_SPECS[wn])
            ctx.wsc[wn] = nc.dram_tensor("wsc_" + wn, shape, BF16, kind="Internal").ap()
            s_ap = ctx.I[wn]
            d_ap = ctx.wsc[wn]
            if len(shape) == 3:
                s_ap = s_ap.rearrange("a b c -> (a b) c")
                d_ap = d_ap.rearrange("a b c -> (a b) c")
            P.dma("gpsimd", d_ap, s_ap, ctx.dC)
            ctx.castdone[wn] = ctx.dC.mark

    emit_casts(phases[0])
    for i, ph in enumerate(phases):
        fresh_sems(ph, ["dL", "dG", "dS", "tns", "vec", "scl", "pol", "dT"])
        EMITTERS[ph](ctx)
        if i == 0:
            for ph2 in phases[1:]:
                emit_casts(ph2)
        ctx.phase_end()
        ctx.free_phase_allocs()

    P.emit()
    return nc, ctx



def run_phases(phases, inputs, extra_streams=None, core_ids=None,
               debug_streams=False):
    core_ids = core_ids if core_ids is not None else list(range(NCORES))
    nc, ctx = build(phases, debug_streams=debug_streams)
    x = np.ascontiguousarray(inputs["L0_attn_x"], dtype=np.float32)
    in_maps = []
    for i, c in enumerate(core_ids):
        m = {}
        for n in ctx.I:
            m[n] = np.ascontiguousarray(inputs[n], dtype=np.float32)
        for sn in ctx.external_streams_in:
            if sn == "xin":
                m["s_xin"] = x[2 * c:2 * c + 2].reshape(NT, D)
            else:
                m["s_" + sn] = extra_streams[sn][i]
        if "xin" in ctx.streams and "s_xin" not in m:
            m["s_xin"] = x[2 * c:2 * c + 2].reshape(NT, D)
        in_maps.append(m)
    res = run_bass_kernel_spmd(nc, in_maps, core_ids=core_ids)
    return res, ctx


def kernel(**inputs):
    if FUSED:
        phases = ["p1a", "p1bc", "p2", "p3", "p4", "p5"]
        res, ctx = run_phases(phases, inputs)
        outs = [r["s_out"] for r in res.results]
    else:
        carry = {}
        ctx = None
        for g in PHASE_GROUPS:
            res, ctx = run_phases(g, inputs, extra_streams=carry,
                                  debug_streams=True)
            for sn in ctx.external_streams_out:
                carry[sn] = [r["s_" + sn] for r in res.results]
        outs = carry["out"]
    full = np.concatenate([o.reshape(2, SEQ, D) for o in outs], axis=0)
    return full.astype(np.float32)


EMITTERS["p2"] = emit_p2
EMITTERS["p3"] = emit_p3
EMITTERS["p5"] = emit_p5
EMITTERS["p1a"] = emit_p1a
EMITTERS["p1bc"] = emit_p1bc
EMITTERS["p4"] = emit_p4
